# version 01
# baseline (speedup 1.0000x reference)
import numpy as np
import ml_dtypes
from contextlib import ExitStack

import concourse.bass as bass
import concourse.mybir as mybir
from concourse.bass_utils import run_bass_kernel_spmd

F32 = mybir.dt.float32
BF16 = mybir.dt.bfloat16
I32 = mybir.dt.int32
AF = mybir.ActivationFunctionType
ALU = mybir.AluOpType
AX = mybir.AxisListType

D = 2048
S_LEN = 8192
BATCH = 2
TOK = BATCH * S_LEN
NCORES = 8
TPC = TOK // NCORES
DC = D // 128
F_FF = 8192
FC = F_FF // 128
TT = 512
NTT = TPC // TT
HPC = 4
CH = 128
NCH = S_LEN // CH

BF = ml_dtypes.bfloat16


class Sched:
    def __init__(self, nc):
        self.nc = nc
        self.steps = []

    def dma(self, eng, fn):
        self.steps.append((eng, fn, "dma"))

    def op(self, eng, fn):
        self.steps.append((eng, fn, "compute"))

    def emit(self, blk, ch):
        steps = self.steps
        n = len(steps)
        solo = {"vector", "scalar", "gpsimd"}
        group_id = [0] * n
        g = 0
        for i in range(1, n):
            if steps[i][0] != steps[i - 1][0] or (
                    steps[i][0] in solo and steps[i][2] == "compute"):
                g += 1
            group_id[i] = g
        ngroups = g + 1
        inc_of = [0] * n
        last_in_group = {}
        for i in range(n):
            last_in_group[group_id[i]] = i
        for i in range(n):
            if steps[i][2] == "dma":
                inc_of[i] = 16
        for gid, i in last_in_group.items():
            if steps[i][2] == "compute":
                inc_of[i] = 1
        cum = 0
        group_wait = [0] * ngroups
        seen = -1
        for i in range(n):
            if group_id[i] != seen:
                group_wait[group_id[i]] = cum
                seen = group_id[i]
            cum += inc_of[i]
        total = cum

        last_set = set(last_in_group.values())

        def section(name):
            drain_engines = ("vector", "scalar", "gpsimd")

            def body(e):
                prev_gid = -1
                for i in range(n):
                    eng, fn, kind = steps[i]
                    if eng != name:
                        continue
                    gid = group_id[i]
                    if gid != prev_gid:
                        w = group_wait[gid]
                        if w > 0:
                            e.wait_ge(ch, w)
                        prev_gid = gid
                    inst = fn(e)
                    if inc_of[i]:
                        if (kind == "compute" and name in drain_engines
                                and i in last_set):
                            e.drain().then_inc(ch, inc_of[i])
                        else:
                            inst.then_inc(ch, inc_of[i])
                if name == "sync":
                    e.wait_ge(ch, total)
            return body

        blk.sync(section("sync"))
        blk.tensor(section("tensor"))
        blk.vector(section("vector"))
        blk.scalar(section("scalar"))
        blk.gpsimd(section("gpsimd"))


class Ctx:
    def __init__(self, nc, sched, stack):
        self.nc = nc
        self.s = sched
        self.stack = stack
        self.t = {}
        self.cmap = {}

    def cst(self, value):
        if "cpool" not in self.t:
            self.sb("cpool", [128, 32], F32)
        if value not in self.cmap:
            col = len(self.cmap)
            assert col < 32
            self.cmap[value] = col
            self.s.op("vector", (lambda col=col, value=value: (
                lambda e: e.memset(self.t["cpool"][:, col:col + 1], value)))())
        col = self.cmap[value]
        return self.t["cpool"][:, col:col + 1]

    def sb(self, name, shape, dt):
        h = self.stack.enter_context(self.nc.sbuf_tensor("s_" + name, list(shape), dt))
        self.t[name] = h
        return h

    def ps(self, name, shape, dt=F32):
        h = self.stack.enter_context(self.nc.psum_tensor("p_" + name, list(shape), dt))
        self.t[name] = h
        return h



def emit_ln(c, x_dram, xln_sbuf, tok0, ntok, x_dram_off=0):
    s, t = c.s, c.t
    pa, pb = t["pa"], t["pb"]
    ones = t["ones_bf"]
    col = x_dram_off + tok0
    for ci in range(DC):
        s.dma("sync", (lambda ci: lambda e: e.dma_start(
            out=t["xt"][:, :ntok], in_=x_dram[ci * 128:(ci + 1) * 128, col:col + ntok]))(ci))
        s.op("scalar", lambda e: e.activation(t["xbt"][:, :ntok], t["xt"][:, :ntok], AF.Copy))
        s.op("scalar", lambda e: e.activation(t["sqt"][:, :ntok], t["xt"][:, :ntok], AF.Square))
        s.op("tensor", (lambda ci: lambda e: e.matmul(
            pa[:, :ntok], ones[:, :], t["xbt"][:, :ntok],
            start=(ci == 0), stop=(ci == DC - 1), skip_group_check=True))(ci))
        s.op("tensor", (lambda ci: lambda e: e.matmul(
            pb[:, :ntok], ones[:, :], t["sqt"][:, :ntok],
            start=(ci == 0), stop=(ci == DC - 1), skip_group_check=True))(ci))
    s.op("scalar", lambda e: e.activation(t["mb"][:, :ntok], pa[:, :ntok], AF.Copy, scale=1.0 / D))
    s.op("scalar", lambda e: e.activation(t["vb"][:, :ntok], pb[:, :ntok], AF.Copy, scale=1.0 / D))
    s.op("scalar", lambda e: e.activation(t["m2"][:, :ntok], t["mb"][:, :ntok], AF.Square))
    s.op("vector", lambda e: e.tensor_sub(t["vb"][:, :ntok], t["vb"][:, :ntok], t["m2"][:, :ntok]))
    eps_ap = c.cst(1e-5)
    s.op("scalar", lambda e: e.activation(t["rb"][:, :ntok], t["vb"][:, :ntok], AF.Sqrt, bias=eps_ap))
    s.op("vector", lambda e: e.reciprocal(t["rb"][:, :ntok], t["rb"][:, :ntok]))
    for ci in range(DC):
        s.dma("sync", (lambda ci: lambda e: e.dma_start(
            out=t["xt"][:, :ntok], in_=x_dram[ci * 128:(ci + 1) * 128, col:col + ntok]))(ci))
        s.op("scalar", lambda e: e.activation(t["xbt"][:, :ntok], t["xt"][:, :ntok], AF.Copy))
        s.op("vector", lambda e: e.tensor_sub(t["tmpf"][:, :ntok], t["xbt"][:, :ntok], t["mb"][:, :ntok]))
        s.op("vector", (lambda ci: lambda e: e.tensor_mul(
            xln_sbuf[:, ci * ntok:(ci + 1) * ntok], t["tmpf"][:, :ntok], t["rb"][:, :ntok]))(ci))


QCOL = 2048


def emit_quant(c, w_dram, rows, cols, kind, wq_internal, wq_out):
    s, t = c.s, c.t
    nrc = rows // 128
    ncb = cols // QCOL
    nblk = nrc * ncb
    assert nblk <= 64
    for r in range(nrc):
        for cb in range(ncb):
            j = r * ncb + cb
            s.dma("sync", (lambda r, cb: lambda e: e.dma_start(
                out=t["wrow"][:, :], in_=w_dram[r * 128:(r + 1) * 128, cb * QCOL:(cb + 1) * QCOL]))(r, cb))
            s.op("vector", (lambda j: lambda e: e.reduce_sum(
                t["parts"][:, 2 + j:3 + j], t["wrow"][:, :], AX.X, apply_absolute_value=True))(j))
    s.op("vector", (lambda nblk: lambda e: e.reduce_sum(
        t["parts"][:, 0:1], t["parts"][:, 2:2 + nblk], AX.X))(nblk))
    s.op("tensor", lambda e: e.matmul(t["pq"][:, 0:1], t["ones_f32"][:, :], t["parts"][:, 0:1],
                                      start=True, stop=True))
    if kind == "absmean":
        s.op("scalar", lambda e: e.activation(t["thr"][:, 0:1], t["pq"][:, 0:1], AF.Copy,
                                              scale=0.7 / (rows * cols)))
    else:
        s.op("scalar", lambda e: e.activation(t["thr"][:, 0:1], t["pq"][:, 0:1], AF.Copy,
                                              scale=1.0 / (rows * cols)))
        s.op("vector", lambda e: e.tensor_scalar(t["thr"][:, 0:1], t["thr"][:, 0:1], 1e-8, 0.5,
                                                 ALU.max, ALU.mult))
    for r in range(nrc):
        for cb in range(ncb):
            cs_ = slice(cb * QCOL, (cb + 1) * QCOL)
            s.dma("sync", (lambda r, cs_: lambda e: e.dma_start(
                out=t["wrow"][:, :], in_=w_dram[r * 128:(r + 1) * 128, cs_]))(r, cs_))
            s.op("scalar", lambda e: e.activation(t["sgn"][:, :], t["wrow"][:, :], AF.Sign))
            s.op("scalar", lambda e: e.activation(t["absw"][:, :], t["wrow"][:, :], AF.Abs))
            s.op("vector", lambda e: e.tensor_scalar(
                t["absw"][:, :], t["absw"][:, :], t["thr"][:, 0:1], None, ALU.is_ge))
            s.op("vector", lambda e: e.tensor_mul(t["wqt"][:, :], t["sgn"][:, :], t["absw"][:, :]))
            s.dma("sync", (lambda r, cs_: lambda e: e.dma_start(
                out=wq_internal[r * 128:(r + 1) * 128, cs_], in_=t["wqt"][:, :]))(r, cs_))
            if wq_out is not None:
                s.dma("sync", (lambda r, cs_: lambda e: e.dma_start(
                    out=wq_out[r * 128:(r + 1) * 128, cs_], in_=t["wqt"][:, :]))(r, cs_))


def emit_ffn(c, x_in_dram, w1q, w2q, b1t, halfb2, x_outs, x_in_off=0):
    s, t = c.s, c.t
    XLN, H = t["XLN"], t["H"]
    pc, pd = t["pc"], t["pd"]
    w1v = w1q.rearrange("(c p) f -> p c f", p=128)
    w2v = w2q.rearrange("(c p) f -> p c f", p=128)
    for ti in range(NTT):
        tok0 = ti * TT
        emit_ln(c, x_in_dram, XLN, tok0, TT, x_dram_off=x_in_off)
        for fi in range(FC):
            s.dma("sync", (lambda fi: lambda e: e.dma_start(
                out=t["w1t"][:, :].rearrange("p (c f) -> p c f", c=DC),
                in_=w1v[:, :, fi * 128:(fi + 1) * 128]))(fi))
            for ci in range(DC):
                s.op("tensor", (lambda fi, ci: lambda e: e.matmul(
                    pc[:, :TT], t["w1t"][:, ci * 128:(ci + 1) * 128],
                    XLN[:, ci * TT:(ci + 1) * TT],
                    start=(ci == 0), stop=(ci == DC - 1)))(fi, ci))
            s.op("scalar", (lambda fi: lambda e: e.activation(
                H[:, fi * TT:(fi + 1) * TT], pc[:, :TT], AF.Gelu, bias=b1t[:, fi:fi + 1]))(fi))
        for di in range(DC):
            s.dma("sync", (lambda di: lambda e: e.dma_start(
                out=t["w2t"][:, :].rearrange("p (c f) -> p c f", c=FC),
                in_=w2v[:, :, di * 128:(di + 1) * 128]))(di))
            for fi in range(FC):
                s.op("tensor", (lambda di, fi: lambda e: e.matmul(
                    pd[:, :TT], t["w2t"][:, fi * 128:(fi + 1) * 128],
                    H[:, fi * TT:(fi + 1) * TT],
                    start=(fi == 0), stop=(fi == FC - 1)))(di, fi))
            s.dma("sync", (lambda di, tok0: lambda e: e.dma_start(
                out=t["xt"][:, :TT],
                in_=x_in_dram[di * 128:(di + 1) * 128, x_in_off + tok0:x_in_off + tok0 + TT]))(di, tok0))
            s.op("vector", (lambda di: lambda e: e.tensor_scalar(
                t["tmpf"][:, :TT], pd[:, :TT], 0.5, halfb2[:, di:di + 1], ALU.mult, ALU.add))(di))
            s.op("vector", lambda e: e.tensor_add(t["tmpf"][:, :TT], t["tmpf"][:, :TT], t["xt"][:, :TT]))
            for xo in x_outs:
                s.dma("sync", (lambda di, tok0, xo: lambda e: e.dma_start(
                    out=xo[di * 128:(di + 1) * 128, tok0:tok0 + TT],
                    in_=t["tmpf"][:, :TT]))(di, tok0, xo))


def emit_ln_to_dram(c, x_dram, xln_dram, x_off=0):
    s, t = c.s, c.t
    for ti in range(NTT):
        tok0 = ti * TT
        emit_ln(c, x_dram, t["XLN"], tok0, TT, x_dram_off=x_off)
        for ci in range(DC):
            s.dma("sync", (lambda ci, tok0: lambda e: e.dma_start(
                out=xln_dram[ci * 128:(ci + 1) * 128, tok0:tok0 + TT],
                in_=t["XLN"][:, ci * TT:(ci + 1) * TT]))(ci, tok0))


def alloc_common(c):
    c.sb("xt", [128, TT], F32)
    c.sb("xbt", [128, TT], BF16)
    c.sb("sqt", [128, TT], BF16)
    c.sb("tmpf", [128, TT], F32)
    c.sb("mb", [128, TT], F32)
    c.sb("vb", [128, TT], F32)
    c.sb("m2", [128, TT], F32)
    c.sb("rb", [128, TT], F32)
    c.sb("ones_bf", [128, 128], BF16)
    c.sb("ones_f32", [128, 128], F32)
    c.sb("XLN", [128, DC * TT], BF16)
    c.ps("pa", [128, TT])
    c.ps("pb", [128, TT])
    c.ps("pc", [128, TT])
    c.ps("pd", [128, TT])


def alloc_ffn(c):
    c.sb("H", [128, FC * TT], BF16)
    c.sb("w1t", [128, DC * 128], BF16)
    c.sb("w2t", [128, FC * 128], BF16)


def alloc_quant(c):
    c.sb("wrow", [128, QCOL], F32)
    c.sb("parts", [128, 66], F32)
    c.sb("thr", [128, 1], F32)
    c.sb("sgn", [128, QCOL], F32)
    c.sb("absw", [128, QCOL], F32)
    c.sb("wqt", [128, QCOL], BF16)
    c.ps("pq", [128, 8])


def load_consts(c, d_ones_bf, d_ones_f32):
    s, t = c.s, c.t
    s.dma("sync", lambda e: e.dma_start(out=t["ones_bf"][:, :], in_=d_ones_bf[:, :]))
    s.dma("sync", lambda e: e.dma_start(out=t["ones_f32"][:, :], in_=d_ones_f32[:, :]))


def build_ffn_launch(quant_kind):
    nc = bass.Bass("TRN2", target_bir_lowering=False, debug=False)
    d_x = nc.dram_tensor("x0", [D, TPC], F32, kind="ExternalInput").ap()
    d_w1 = nc.dram_tensor("w1t", [D, F_FF], F32, kind="ExternalInput").ap()
    d_w2 = nc.dram_tensor("w2t", [F_FF, D], F32, kind="ExternalInput").ap()
    d_b1 = nc.dram_tensor("b1", [128, FC], F32, kind="ExternalInput").ap()
    d_b2 = nc.dram_tensor("b2", [128, DC], F32, kind="ExternalInput").ap()
    d_ones_bf = nc.dram_tensor("ones_bf", [128, 128], BF16, kind="ExternalInput").ap()
    d_ones_f32 = nc.dram_tensor("ones_f32", [128, 128], F32, kind="ExternalInput").ap()

    o_x = nc.dram_tensor("x1", [D, TPC], F32, kind="ExternalOutput").ap()
    o_xln = nc.dram_tensor("xln1", [D, TPC], BF16, kind="ExternalOutput").ap()
    o_w1q = nc.dram_tensor("w1q", [D, F_FF], BF16, kind="ExternalOutput").ap()
    o_w2q = nc.dram_tensor("w2q", [F_FF, D], BF16, kind="ExternalOutput").ap()

    i_w1q = nc.dram_tensor("i_w1q", [D, F_FF], BF16, kind="Internal").ap()
    i_w2q = nc.dram_tensor("i_w2q", [F_FF, D], BF16, kind="Internal").ap()
    i_x1 = nc.dram_tensor("i_x1", [D, TPC], F32, kind="Internal").ap()

    with ExitStack() as stack:
        sched = Sched(nc)
        c = Ctx(nc, sched, stack)
        alloc_common(c)
        alloc_ffn(c)
        alloc_quant(c)
        c.sb("b1t", [128, FC], F32)
        c.sb("halfb2", [128, DC], F32)
        ch = stack.enter_context(nc.semaphore("ch"))
        blk = stack.enter_context(nc.Block())

        s, t = c.s, c.t
        load_consts(c, d_ones_bf, d_ones_f32)
        s.dma("sync", lambda e: e.dma_start(out=t["b1t"][:, :], in_=d_b1[:, :]))
        s.dma("sync", lambda e: e.dma_start(out=t["halfb2"][:, :], in_=d_b2[:, :]))
        s.op("scalar", lambda e: e.activation(t["halfb2"][:, :], t["halfb2"][:, :], AF.Copy, scale=0.5))

        emit_quant(c, d_w1, D, F_FF, quant_kind, i_w1q, o_w1q)
        emit_quant(c, d_w2, F_FF, D, quant_kind, i_w2q, o_w2q)
        emit_ffn(c, d_x, i_w1q, i_w2q, t["b1t"], t["halfb2"], [o_x, i_x1])
        emit_ln_to_dram(c, i_x1, o_xln)

        sched.emit(blk, ch)
    return nc


TTA = 1024
NTA = S_LEN // TTA
BPT = TTA // CH


def build_launch2(dbg=False):
    nc = bass.Bass("TRN2", target_bir_lowering=False, debug=False)
    d_xln = nc.dram_tensor("xln", [D, S_LEN], BF16, kind="ExternalInput").ap()
    d_wmix = nc.dram_tensor("wmix", [D, 1536], F32, kind="ExternalInput").ap()
    d_wz = nc.dram_tensor("wz", [D, 512], F32, kind="ExternalInput").ap()
    d_wba = nc.dram_tensor("wba", [D, 8], F32, kind="ExternalInput").ap()
    d_convw = nc.dram_tensor("convw", [128, 48], F32, kind="ExternalInput").ap()
    d_dtb = nc.dram_tensor("dtb", [4, 1], F32, kind="ExternalInput").ap()
    d_alog = nc.dram_tensor("alog", [4, 1], F32, kind="ExternalInput").ap()
    d_nw = nc.dram_tensor("nw", [128, 1], F32, kind="ExternalInput").ap()
    d_ones_bf = nc.dram_tensor("ones_bf", [128, 128], BF16, kind="ExternalInput").ap()
    d_ones_f32 = nc.dram_tensor("ones_f32", [128, 128], F32, kind="ExternalInput").ap()
    d_ident_bf = nc.dram_tensor("ident_bf", [128, 128], BF16, kind="ExternalInput").ap()
    d_ident_f32 = nc.dram_tensor("ident_f32", [128, 128], F32, kind="ExternalInput").ap()
    d_triu = nc.dram_tensor("triu", [128, 128], F32, kind="ExternalInput").ap()
    d_mski = nc.dram_tensor("mski", [128, 128], BF16, kind="ExternalInput").ap()
    d_mskn = nc.dram_tensor("mskn", [128, 128], BF16, kind="ExternalInput").ap()
    d_mskn2 = nc.dram_tensor("mskn2", [128, 128], BF16, kind="ExternalInput").ap()
    o_hout = nc.dram_tensor("hout", [512, S_LEN], BF16, kind="ExternalOutput").ap()

    with ExitStack() as stack:
        sched = Sched(nc)
        c = Ctx(nc, sched, stack)
        s = sched
        c.sb("Wmix", [128, DC * 1536], BF16)
        c.sb("Wz", [128, DC * 512], BF16)
        c.sb("Wba", [128, DC * 8], BF16)
        c.sb("convw", [128, 48], F32)
        c.sb("nw", [128, 1], F32)
        c.sb("negEA", [4, 1], F32)
        c.sb("dtb", [4, 1], F32)
        for nm, dt in [("ones_bf", BF16), ("ones_f32", F32), ("ident_bf", BF16),
                       ("ident_f32", F32), ("triu", F32), ("mski", BF16),
                       ("mskn", BF16), ("mskn2", BF16)]:
            c.sb(nm, [128, 128], dt)
        c.sb("xs", [128, DC * 512], BF16)
        c.sb("mixed", [128, 12 * (TTA + 3)], BF16)
        c.sb("tail", [128, 12 * 3], BF16)
        c.sb("Y", [128, 12 * TTA], BF16)
        c.sb("Z", [128, 4 * TTA], BF16)
        c.sb("acc_a", [128, TTA], F32)
        c.sb("acc_b", [128, TTA], F32)
        c.sb("bgate", [4, TTA], F32)
        c.sb("arows", [4, TTA], F32)
        c.sb("grow", [4, TTA], F32)
        c.sb("betaf", [128, HPC * TTA], BF16)
        c.sb("beta_tm", [128, BPT * 4], F32)
        c.sb("g_tm", [128, BPT * 4], F32)
        c.sb("gc_all", [128, BPT * 4], F32)
        c.sb("egc_all", [128, BPT * 4], F32)
        c.sb("gl_all", [128, BPT * 4], F32)
        c.sb("ekd_all", [128, BPT * 4], F32)
        c.sb("egl_all", [128, BPT * 4], F32)
        c.sb("glast", [1, BPT * 4], F32)
        c.sb("ktm", [128, HPC * BPT * 128], BF16)
        c.sb("vtm", [128, HPC * BPT * 128], BF16)
        c.sb("sqn", [128, TTA], BF16)
        c.sb("rstdn", [128, TTA], F32)
        c.sb("gcb", [128, 128], F32)
        c.sb("dec", [128, 128], F32)
        c.sb("dec2", [128, 128], F32)
        c.sb("mn", [128, 128], F32)
        c.sb("egcf", [128, 128], F32)
        c.sb("kbfm", [128, 128], BF16)
        c.sb("ttmp", [128, 128], F32)
        c.sb("ttmp2", [128, 128], F32)
        c.sb("Atile", [128, 128], BF16)
        c.sb("Q0", [128, 128], BF16)
        c.sb("PQ_a", [128, 256], BF16)
        c.sb("PQ_b", [128, 256], BF16)
        c.sb("X_a", [128, 128], BF16)
        c.sb("X_b", [128, 128], BF16)
        c.sb("uw", [128, 256], BF16)
        c.sb("vnew", [128, 128], BF16)
        c.sb("qe", [128, 128], BF16)
        c.sb("attnT", [128, 128], BF16)
        c.sb("vb_tm", [128, 128], BF16)
        c.sb("kbe_tm", [128, 128], BF16)
        c.sb("kdec_tm", [128, 128], BF16)
        c.sb("Sf", [128, HPC * 2 * 128], F32)
        c.sb("Sb", [128, HPC * 128], BF16)
        c.sb("sqo", [128, 128], BF16)
        c.sb("rstdo", [128, 128], F32)
        c.sb("szt", [128, 128], BF16)
        c.sb("houtt", [128, 128], BF16)
        c.ps("pa", [128, 512])
        c.ps("pz", [128, 512])
        c.ps("pt", [128, 512], BF16)
        c.ps("pfa", [128, 512])
        c.ps("pfb", [128, 512])
        c.ps("po", [128, 128])
        c.ps("pg", [128, 512])

        if dbg:
            c.sb("_dbg_po", [128, 128], F32)
            _dsel = dbg if isinstance(dbg, tuple) else (0, dbg if isinstance(dbg, int) else 0)
            c.t["_dbg_sel"] = (_dsel[0], _dsel[1], 0)
            c.t["_dbg_Sdram"] = nc.dram_tensor("dbg_Straj", [128, 64 * 128], F32, kind="ExternalOutput").ap()
            c.t["_dbg_Bdram"] = nc.dram_tensor("dbg_Btraj", [128, 64 * 128], F32, kind="ExternalOutput").ap()
        ch = stack.enter_context(nc.semaphore("ch"))
        blk = stack.enter_context(nc.Block())
        t = c.t
        t["sp1"] = t["acc_a"]
        t["sp2"] = t["acc_b"]

        for nm, src in [("ones_bf", d_ones_bf), ("ones_f32", d_ones_f32),
                        ("ident_bf", d_ident_bf), ("ident_f32", d_ident_f32),
                        ("triu", d_triu), ("mski", d_mski), ("mskn", d_mskn),
                        ("mskn2", d_mskn2), ("convw", d_convw), ("nw", d_nw),
                        ("dtb", d_dtb)]:
            s.dma("sync", (lambda nm, src: lambda e: e.dma_start(out=t[nm][:, :], in_=src[:, :]))(nm, src))
        s.dma("gpsimd", lambda e: e.dma_start(
            out=t["Wmix"][:, :].rearrange("p (c f) -> p c f", c=DC),
            in_=d_wmix.rearrange("(c p) f -> p c f", p=128)))
        s.dma("gpsimd", lambda e: e.dma_start(
            out=t["Wz"][:, :].rearrange("p (c f) -> p c f", c=DC),
            in_=d_wz.rearrange("(c p) f -> p c f", p=128)))
        s.dma("gpsimd", lambda e: e.dma_start(
            out=t["Wba"][:, :].rearrange("p (c f) -> p c f", c=DC),
            in_=d_wba.rearrange("(c p) f -> p c f", p=128)))
        s.dma("sync", lambda e: e.dma_start(out=t["negEA"][:, :], in_=d_alog[:, :]))
        s.op("scalar", lambda e: e.activation(t["negEA"][:, :], t["negEA"][:, :], AF.Exp))
        s.op("vector", lambda e: e.tensor_scalar_mul(t["negEA"][:, :], t["negEA"][:, :], -1.0))
        s.op("vector", lambda e: e.memset(t["tail"][:, :], 0.0))
        s.op("vector", lambda e: e.memset(t["Sf"][:, :], 0.0))
        s.op("vector", lambda e: e.memset(t["Sb"][:, :], 0.0))

        ln128 = float(np.log(128.0))
        eps_l2 = c.cst(1e-6)
        one_c = c.cst(1.0)
        qbias = c.cst(-0.5 * ln128)
        zero_c = c.cst(0.0)
        eps_rms = c.cst(1e-6)

        for ti in range(NTA):
            tok0 = ti * TTA
            for sl in range(2):
                t0 = tok0 + sl * 512
                s.dma("sync", (lambda t0: lambda e: e.dma_start(
                    out=t["xs"][:, :].rearrange("p (c f) -> p c f", c=DC),
                    in_=d_xln.rearrange("(c p) t -> p c t", p=128)[:, :, t0:t0 + 512]))(t0))
                for m in range(12):
                    for ci in range(DC):
                        s.op("tensor", (lambda m, ci: lambda e: e.matmul(
                            t["pa"][:, :], t["Wmix"][:, ci * 1536 + m * 128:ci * 1536 + (m + 1) * 128],
                            t["xs"][:, ci * 512:(ci + 1) * 512],
                            start=(ci == 0), stop=(ci == DC - 1)))(m, ci))
                    s.op("scalar", (lambda m, sl: lambda e: e.activation(
                        t["mixed"][:, m * (TTA + 3) + 3 + sl * 512: m * (TTA + 3) + 3 + (sl + 1) * 512],
                        t["pa"][:, :], AF.Copy))(m, sl))
                for zc in range(4):
                    for ci in range(DC):
                        s.op("tensor", (lambda zc, ci: lambda e: e.matmul(
                            t["pz"][:, :], t["Wz"][:, ci * 512 + zc * 128:ci * 512 + (zc + 1) * 128],
                            t["xs"][:, ci * 512:(ci + 1) * 512],
                            start=(ci == 0), stop=(ci == DC - 1)))(zc, ci))
                    s.op("scalar", (lambda zc, sl: lambda e: e.activation(
                        t["Z"][:, zc * TTA + sl * 512: zc * TTA + (sl + 1) * 512],
                        t["pz"][:, :], AF.Copy))(zc, sl))
                for ci in range(DC):
                    s.op("tensor", (lambda ci: lambda e: e.matmul(
                        t["pfa"][0:4, 0:512], t["Wba"][:, ci * 8:ci * 8 + 4],
                        t["xs"][:, ci * 512:(ci + 1) * 512],
                        start=(ci == 0), stop=(ci == DC - 1)))(ci))
                    s.op("tensor", (lambda ci: lambda e: e.matmul(
                        t["pfb"][0:4, 0:512], t["Wba"][:, ci * 8 + 4:ci * 8 + 8],
                        t["xs"][:, ci * 512:(ci + 1) * 512],
                        start=(ci == 0), stop=(ci == DC - 1)))(ci))
                s.op("scalar", (lambda sl: lambda e: e.activation(
                    t["bgate"][:, sl * 512:(sl + 1) * 512], t["pfa"][0:4, 0:512], AF.Sigmoid))(sl))
                s.op("vector", (lambda sl: lambda e: e.tensor_scalar(
                    t["arows"][:, sl * 512:(sl + 1) * 512], t["pfb"][0:4, 0:512],
                    t["dtb"][0:4, 0:1], None, ALU.add))(sl))
            s.op("scalar", lambda e: e.activation(t["sp1"][0:4, :], t["arows"][0:4, :], AF.Abs))
            s.op("scalar", lambda e: e.activation(t["grow"][0:4, :], t["sp1"][0:4, :], AF.Exp, scale=-1.0))
            s.op("scalar", lambda e: e.activation(t["sp1"][0:4, :], t["grow"][0:4, :], AF.Ln, bias=one_c[0:4, :]))
            s.op("vector", lambda e: e.tensor_scalar_max(t["sp2"][0:4, :], t["arows"][0:4, :], 0.0))
            s.op("vector", lambda e: e.tensor_add(t["sp2"][0:4, :], t["sp2"][0:4, :], t["sp1"][0:4, :]))
            s.op("vector", lambda e: e.tensor_scalar(
                t["grow"][:, :], t["sp2"][0:4, :], t["negEA"][0:4, 0:1], None, ALU.mult))
            for b_ in range(BPT):
                s.op("tensor", (lambda b_: lambda e: e.transpose(
                    t["pg"][:, 0:4],
                    t["bgate"][0:4, b_ * 128:(b_ + 1) * 128], t["ident_f32"][0:4, 0:4]))(b_))
                s.op("scalar", (lambda b_: lambda e: e.activation(
                    t["beta_tm"][:, b_ * 4:(b_ + 1) * 4], t["pg"][:, 0:4], AF.Copy))(b_))
                s.op("tensor", (lambda b_: lambda e: e.transpose(
                    t["pg"][:, 128:132], t["grow"][0:4, b_ * 128:(b_ + 1) * 128], t["ident_f32"][0:4, 0:4]))(b_))
                s.op("scalar", (lambda b_: lambda e: e.activation(
                    t["g_tm"][:, b_ * 4:(b_ + 1) * 4], t["pg"][:, 128:132], AF.Copy))(b_))
            s.op("tensor", lambda e: e.matmul(t["pg"][:, 0:32], t["triu"][:, :], t["g_tm"][:, :],
                                              start=True, stop=True))
            s.op("scalar", lambda e: e.activation(t["gc_all"][:, :], t["pg"][:, 0:32], AF.Copy))
            s.op("scalar", lambda e: e.activation(t["egc_all"][:, :], t["pg"][:, 0:32], AF.Exp))
            s.dma("sync", lambda e: e.dma_start(out=t["glast"][0:1, :], in_=t["gc_all"][127:128, :]))
            s.op("tensor", lambda e: e.matmul(t["pg"][:, 64:96], t["ones_f32"][0:1, :], t["glast"][0:1, :],
                                              start=True, stop=True))
            s.op("scalar", lambda e: e.activation(t["gl_all"][:, :], t["pg"][:, 64:96], AF.Copy))
            s.op("scalar", lambda e: e.activation(t["egl_all"][:, :], t["pg"][:, 64:96], AF.Exp))
            s.op("vector", lambda e: e.tensor_sub(t["ekd_all"][:, :], t["gl_all"][:, :], t["gc_all"][:, :]))
            s.op("scalar", lambda e: e.activation(t["ekd_all"][:, :], t["ekd_all"][:, :], AF.Exp))
            for h in range(HPC):
                s.dma("sync", (lambda h: lambda e: e.dma_start(
                    out=t["acc_a"][0:1, :], in_=t["bgate"][h:h + 1, :]))(h))
                for sl in range(2):
                    s.op("tensor", (lambda h, sl: lambda e: e.matmul(
                        t["pg"][:, 0:512],
                        t["ones_f32"][0:1, :], t["acc_a"][0:1, sl * 512:(sl + 1) * 512],
                        start=True, stop=True))(h, sl))
                    s.op("scalar", (lambda h, sl: lambda e: e.activation(
                        t["betaf"][:, h * TTA + sl * 512: h * TTA + (sl + 1) * 512],
                        t["pg"][:, 0:512], AF.Copy))(h, sl))
            for m in range(12):
                mo = m * (TTA + 3)
                s.op("vector", (lambda m, mo: lambda e: e.tensor_copy(
                    t["mixed"][:, mo:mo + 3], t["tail"][:, m * 3:(m + 1) * 3]))(m, mo))
                s.op("vector", (lambda m, mo: lambda e: e.tensor_scalar(
                    t["acc_a"][:, :], t["mixed"][:, mo:mo + TTA], t["convw"][:, m * 4:m * 4 + 1],
                    None, ALU.mult))(m, mo))
                s.op("vector", (lambda m, mo: lambda e: e.scalar_tensor_tensor(
                    t["acc_b"][:, :], t["mixed"][:, mo + 1:mo + 1 + TTA], t["convw"][:, m * 4 + 1:m * 4 + 2],
                    t["acc_a"][:, :], ALU.mult, ALU.add))(m, mo))
                s.op("vector", (lambda m, mo: lambda e: e.scalar_tensor_tensor(
                    t["acc_a"][:, :], t["mixed"][:, mo + 2:mo + 2 + TTA], t["convw"][:, m * 4 + 2:m * 4 + 3],
                    t["acc_b"][:, :], ALU.mult, ALU.add))(m, mo))
                s.op("vector", (lambda m, mo: lambda e: e.scalar_tensor_tensor(
                    t["acc_b"][:, :], t["mixed"][:, mo + 3:mo + 3 + TTA], t["convw"][:, m * 4 + 3:m * 4 + 4],
                    t["acc_a"][:, :], ALU.mult, ALU.add))(m, mo))
                s.op("scalar", (lambda m: lambda e: e.activation(
                    t["Y"][:, m * TTA:(m + 1) * TTA], t["acc_b"][:, :], AF.Silu))(m))
                s.op("vector", (lambda m, mo: lambda e: e.tensor_copy(
                    t["tail"][:, m * 3:(m + 1) * 3], t["mixed"][:, mo + TTA:mo + TTA + 3]))(m, mo))
            for h in range(HPC):
                for which, bias in ((0, qbias), (4, zero_c)):
                    m = which + h
                    s.op("scalar", (lambda m: lambda e: e.activation(
                        t["sqn"][:, :], t["Y"][:, m * TTA:(m + 1) * TTA], AF.Square))(m))
                    for sl in range(2):
                        s.op("tensor", (lambda sl: lambda e: e.matmul(
                            t["pa"][:, :], t["ones_bf"][:, :], t["sqn"][:, sl * 512:(sl + 1) * 512],
                            start=True, stop=True))(sl))
                        s.op("scalar", (lambda sl: lambda e: e.activation(
                            t["rstdn"][:, sl * 512:(sl + 1) * 512], t["pa"][:, :], AF.Ln,
                            bias=eps_l2))(sl))
                    s.op("scalar", (lambda bias: lambda e: e.activation(
                        t["rstdn"][:, :], t["rstdn"][:, :], AF.Exp, scale=-0.5, bias=bias))(bias))
                    s.op("vector", (lambda m: lambda e: e.tensor_mul(
                        t["Y"][:, m * TTA:(m + 1) * TTA], t["Y"][:, m * TTA:(m + 1) * TTA],
                        t["rstdn"][:, :]))(m))
            for h in range(HPC):
                for b_ in range(BPT):
                    for which, dst in ((4, "ktm"), (8, "vtm")):
                        m = which + h
                        s.op("tensor", (lambda m, b_: lambda e: e.transpose(
                            t["pt"][:, 0:128], t["Y"][:, m * TTA + b_ * 128:m * TTA + (b_ + 1) * 128],
                            t["ident_bf"][:, :]))(m, b_))
                        s.op("scalar", (lambda h, b_, dst: lambda e: e.activation(
                            t[dst][:, (h * BPT + b_) * 128:(h * BPT + b_ + 1) * 128],
                            t["pt"][:, 0:128], AF.Copy))(h, b_, dst))
            if dbg and ti == (dbg[0] if isinstance(dbg, tuple) else 0):
                for nm in ["Y", "Z", "mixed", "gc_all", "beta_tm", "g_tm", "egc_all",
                           "ekd_all", "egl_all", "betaf", "ktm", "vtm", "bgate", "grow"]:
                    hnd = t[nm]
                    shp = [hnd.shape[0], hnd.shape[1]]
                    ap = nc.dram_tensor("dbg_" + nm, shp, hnd.dtype, kind="ExternalOutput").ap()
                    s.dma("sync", (lambda ap, hnd: lambda e: e.dma_start(
                        out=ap[:, :], in_=hnd[:, :]))(ap, hnd))
            for b_ in range(BPT):
                for h in range(HPC):
                    emit_delta_chunk(c, ti, b_, h, o_hout)
                    _dsel = dbg if isinstance(dbg, tuple) else (0, dbg if isinstance(dbg, int) else 0)
                    if dbg and (ti, b_, h) == (_dsel[0], _dsel[1], 0):
                        for nm in ["dec", "dec2", "egcf", "kbfm", "Atile", "Q0", "X_a",
                                   "X_b", "uw", "vnew", "qe", "attnT", "vb_tm", "kbe_tm",
                                   "kdec_tm", "houtt", "rstdo", "Sf", "_dbg_po"]:
                            hnd = t[nm]
                            shp = [hnd.shape[0], hnd.shape[1]]
                            ap = nc.dram_tensor("dbg_" + nm, shp, hnd.dtype, kind="ExternalOutput").ap()
                            s.dma("sync", (lambda ap, hnd: lambda e: e.dma_start(
                                out=ap[:, :], in_=hnd[:, :]))(ap, hnd))

        sched.emit(blk, ch)
    return nc


def emit_delta_chunk(c, ti, b_, h, o_hout):
    s, t = c.s, c.t
    col = b_ * 4 + h
    gc_col = t["gc_all"][:, col:col + 1]
    beta_col = t["beta_tm"][:, col:col + 1]
    egc_col = t["egc_all"][:, col:col + 1]
    ekd_col = t["ekd_all"][:, col:col + 1]
    egl_col = t["egl_all"][:, col:col + 1]
    qY = t["Y"][:, (0 + h) * TTA + b_ * 128:(0 + h) * TTA + (b_ + 1) * 128]
    kY = t["Y"][:, (4 + h) * TTA + b_ * 128:(4 + h) * TTA + (b_ + 1) * 128]
    ktm = t["ktm"][:, (h * BPT + b_) * 128:(h * BPT + b_ + 1) * 128]
    vtm = t["vtm"][:, (h * BPT + b_) * 128:(h * BPT + b_ + 1) * 128]
    bf_sl = t["betaf"][:, h * TTA + b_ * 128:h * TTA + (b_ + 1) * 128]
    z_sl = t["Z"][:, h * TTA + b_ * 128:h * TTA + (b_ + 1) * 128]
    par = (ti * BPT + b_) % 2
    S_old = t["Sf"][:, (h * 2 + par) * 128:(h * 2 + par + 1) * 128]
    S_new = t["Sf"][:, (h * 2 + (1 - par)) * 128:(h * 2 + (1 - par) + 1) * 128]
    S_bf = t["Sb"][:, h * 128:(h + 1) * 128]
    chunk_glob = (ti * BPT + b_)

    s.op("vector", lambda e: e.tensor_scalar(
        t["gcb"][:, :], t["ones_f32"][:, :], gc_col, None, ALU.mult))
    s.op("tensor", lambda e: e.transpose(t["pg"][:, 0:128], t["gcb"][:, :], t["ident_f32"][:, :]))
    s.op("vector", lambda e: e.tensor_scalar(
        t["ttmp"][:, :], t["pg"][:, 0:128], gc_col, None, ALU.subtract))
    s.op("vector", lambda e: e.tensor_scalar_min(t["mn"][:, :], t["ttmp"][:, :], 0.0))
    s.op("scalar", lambda e: e.activation(t["dec"][:, :], t["mn"][:, :], AF.Exp))
    s.op("vector", lambda e: e.tensor_scalar_max(t["mn"][:, :], t["ttmp"][:, :], 0.0))
    s.op("scalar", lambda e: e.activation(t["dec2"][:, :], t["mn"][:, :], AF.Exp, scale=-1.0))
    s.op("scalar", lambda e: e.activation(t["egcf"][:, :], t["pg"][:, 0:128], AF.Exp))
    s.op("vector", lambda e: e.tensor_mul(t["kbfm"][:, :], kY, bf_sl))
    s.op("tensor", lambda e: e.matmul(t["pfa"][:, 0:128], kY, t["kbfm"][:, :], start=True, stop=True))
    s.op("vector", lambda e: e.tensor_mul(t["ttmp"][:, :], t["pfa"][:, 0:128], t["dec"][:, :]))
    s.op("vector", lambda e: e.tensor_mul(t["Atile"][:, :], t["ttmp"][:, :], t["mskn"][:, :]))
    s.op("tensor", lambda e: e.matmul(t["pfa"][:, 128:256], t["kbfm"][:, :], kY, start=True, stop=True))
    s.op("vector", lambda e: e.tensor_mul(t["ttmp"][:, :], t["pfa"][:, 128:256], t["dec2"][:, :]))
    s.op("vector", lambda e: e.tensor_mul(t["Q0"][:, :], t["ttmp"][:, :], t["mskn2"][:, :]))
    s.op("vector", lambda e: e.tensor_add(t["X_a"][:, :], t["Atile"][:, :], t["ident_bf"][:, :]))
    P_prev, Q_prev, X_prev = t["Atile"], t["Q0"], t["X_a"]
    for k in range(1, 7):
        PQ = t["PQ_a"] if k % 2 == 1 else t["PQ_b"]
        Xn = t["X_b"] if k % 2 == 1 else t["X_a"]
        s.op("tensor", (lambda Q_prev, P_prev: lambda e: e.matmul(
            t["pfb"][:, 0:128], Q_prev[:, 0:128], P_prev[:, 0:128], start=True, stop=True))(Q_prev, P_prev))
        s.op("tensor", (lambda Q_prev, P_prev: lambda e: e.matmul(
            t["pfb"][:, 128:256], P_prev[:, 0:128], Q_prev[:, 0:128], start=True, stop=True))(Q_prev, P_prev))
        s.op("scalar", (lambda PQ: lambda e: e.activation(PQ[:, :], t["pfb"][:, 0:256], AF.Copy))(PQ))
        s.op("tensor", (lambda PQ, X_prev: lambda e: e.matmul(
            t["pfa"][:, 0:128], PQ[:, 128:256], X_prev[:, 0:128], start=True, stop=True))(PQ, X_prev))
        s.op("vector", (lambda Xn, X_prev: lambda e: e.tensor_add(
            Xn[:, :], t["pfa"][:, 0:128], X_prev[:, 0:128]))(Xn, X_prev))
        P_prev, Q_prev, X_prev = (lambda PQ: (PQ[:, 0:128], PQ[:, 128:256]))(PQ) + (Xn,)
    X_fin = X_prev[:, 0:128]
    s.op("vector", lambda e: e.tensor_scalar(t["vb_tm"][:, :], vtm, beta_col, None, ALU.mult))
    s.op("vector", lambda e: e.tensor_scalar(t["kbe_tm"][:, :], ktm, beta_col, egc_col, ALU.mult, ALU.mult))
    s.op("vector", lambda e: e.tensor_scalar(t["kdec_tm"][:, :], ktm, ekd_col, None, ALU.mult))
    s.op("tensor", (lambda X_fin: lambda e: e.matmul(
        t["pfb"][:, 0:128], X_fin, t["vb_tm"][:, :], start=True, stop=True))(X_fin))
    s.op("tensor", (lambda X_fin: lambda e: e.matmul(
        t["pfb"][:, 128:256], t["kbe_tm"][:, :], X_fin, start=True, stop=True))(X_fin))
    s.op("scalar", lambda e: e.activation(t["uw"][:, :], t["pfb"][:, 0:256], AF.Copy))
    s.op("tensor", lambda e: e.matmul(t["pfa"][:, 0:128], t["uw"][:, 128:256], S_bf, start=True, stop=True))
    s.op("vector", lambda e: e.tensor_sub(t["vnew"][:, :], t["uw"][:, 0:128], t["pfa"][:, 0:128]))
    s.op("vector", lambda e: e.tensor_mul(t["qe"][:, :], qY, t["egcf"][:, :]))
    s.op("tensor", lambda e: e.matmul(t["po"][:, :], S_bf, t["qe"][:, :], start=True, stop=False,
                                      skip_group_check=True))
    s.op("tensor", lambda e: e.matmul(t["pfa"][:, 128:256], kY, qY, start=True, stop=True))
    s.op("vector", lambda e: e.tensor_mul(t["ttmp2"][:, :], t["pfa"][:, 128:256], t["dec"][:, :]))
    s.op("vector", lambda e: e.tensor_mul(t["attnT"][:, :], t["ttmp2"][:, :], t["mski"][:, :]))
    s.op("tensor", lambda e: e.matmul(t["po"][:, :], t["vnew"][:, :], t["attnT"][:, :],
                                      start=False, stop=True, skip_group_check=True))
    if c.t.get("_dbg_po") is not None and (ti, b_, h) == c.t.get("_dbg_sel", (0, 1, 0)):
        s.op("vector", lambda e: e.tensor_copy(c.t["_dbg_po"][:, :], t["po"][:, :]))
    s.op("tensor", lambda e: e.matmul(t["pfb"][:, 0:128], t["kdec_tm"][:, :], t["vnew"][:, :],
                                      start=True, stop=True))
    s.op("vector", (lambda S_new, S_old: lambda e: e.scalar_tensor_tensor(
        S_new, S_old, egl_col, t["pfb"][:, 0:128], ALU.mult, ALU.add))(S_new, S_old))
    s.op("vector", (lambda S_bf, S_new: lambda e: e.tensor_copy(S_bf, S_new))(S_bf, S_new))
    if c.t.get("_dbg_Sdram") is not None and h == 0:
        cg = ti * BPT + b_
        s.dma("sync", (lambda S_new, cg: lambda e: e.dma_start(
            out=c.t["_dbg_Sdram"][:, cg * 128:(cg + 1) * 128], in_=S_new))(S_new, cg))
        s.op("vector", lambda e: e.tensor_copy(t["ttmp"][:, :], t["pfb"][:, 0:128]))
        s.dma("sync", (lambda cg: lambda e: e.dma_start(
            out=c.t["_dbg_Bdram"][:, cg * 128:(cg + 1) * 128], in_=t["ttmp"][:, :]))(cg))
    s.op("scalar", lambda e: e.activation(t["sqo"][:, :], t["po"][:, :], AF.Square))
    s.op("tensor", lambda e: e.matmul(t["pg"][:, 128:256], t["ones_bf"][:, :], t["sqo"][:, :],
                                      start=True, stop=True))
    s.op("scalar", lambda e: e.activation(t["rstdo"][:, :], t["pg"][:, 128:256], AF.Ln,
                                          scale=1.0 / 128, bias=c.cst(1e-6)))
    s.op("scalar", lambda e: e.activation(t["rstdo"][:, :], t["rstdo"][:, :], AF.Exp, scale=-0.5))
    s.op("vector", lambda e: e.tensor_mul(t["ttmp"][:, :], t["po"][:, :], t["rstdo"][:, :]))
    s.op("vector", lambda e: e.tensor_scalar(t["ttmp"][:, :], t["ttmp"][:, :], t["nw"][:, 0:1],
                                             None, ALU.mult))
    s.op("scalar", (lambda z_sl: lambda e: e.activation(t["szt"][:, :], z_sl, AF.Silu))(z_sl))
    s.op("vector", lambda e: e.tensor_mul(t["houtt"][:, :], t["ttmp"][:, :], t["szt"][:, :]))
    s.dma("sync", (lambda h, chunk_glob: lambda e: e.dma_start(
        out=o_hout[h * 128:(h + 1) * 128, chunk_glob * 128:(chunk_glob + 1) * 128],
        in_=t["houtt"][:, :]))(h, chunk_glob))


def emit_headproj_residual(c, hout_dram, w_dram_f32, x_in_dram, x_outs):
    s, t = c.s, c.t
    WB = t["H"]
    HT = t["XLN"]
    pd = t["pd"]
    s.dma("gpsimd", lambda e: e.dma_start(
        out=WB[:, 0:DC * D].rearrange("p (c f) -> p c f", c=DC),
        in_=w_dram_f32.rearrange("(c p) f -> p c f", p=128)))
    for sl in range(TPC // 512):
        t0 = sl * 512
        s.dma("sync", (lambda t0: lambda e: e.dma_start(
            out=HT[:, :].rearrange("p (c f) -> p c f", c=DC),
            in_=hout_dram.rearrange("(c p) t -> p c t", p=128)[:, :, t0:t0 + 512]))(t0))
        for di in range(DC):
            for cc in range(DC):
                s.op("tensor", (lambda di, cc: lambda e: e.matmul(
                    pd[:, :], WB[:, cc * D + di * 128:cc * D + (di + 1) * 128],
                    HT[:, cc * 512:(cc + 1) * 512],
                    start=(cc == 0), stop=(cc == DC - 1)))(di, cc))
            s.dma("sync", (lambda di, t0: lambda e: e.dma_start(
                out=t["xt"][:, :],
                in_=x_in_dram[di * 128:(di + 1) * 128, t0:t0 + 512]))(di, t0))
            s.op("vector", lambda e: e.tensor_add(t["tmpf"][:, :], pd[:, :], t["xt"][:, :]))
            for xo in x_outs:
                s.dma("sync", (lambda di, t0, xo: lambda e: e.dma_start(
                    out=xo[di * 128:(di + 1) * 128, t0:t0 + 512], in_=t["tmpf"][:, :]))(di, t0, xo))


def build_launch3():
    nc = bass.Bass("TRN2", target_bir_lowering=False, debug=False)
    d_x = nc.dram_tensor("x1", [D, TPC], F32, kind="ExternalInput").ap()
    d_hout = nc.dram_tensor("hout", [D, TPC], BF16, kind="ExternalInput").ap()
    d_wout = nc.dram_tensor("wout", [D, D], F32, kind="ExternalInput").ap()
    d_w1q0 = nc.dram_tensor("w1q0", [D, F_FF], BF16, kind="ExternalInput").ap()
    d_w2q0 = nc.dram_tensor("w2q0", [F_FF, D], BF16, kind="ExternalInput").ap()
    d_b1_0 = nc.dram_tensor("b1_0", [128, FC], F32, kind="ExternalInput").ap()
    d_b2_0 = nc.dram_tensor("b2_0", [128, DC], F32, kind="ExternalInput").ap()
    d_w1_1 = nc.dram_tensor("w1t_1", [D, F_FF], F32, kind="ExternalInput").ap()
    d_w2_1 = nc.dram_tensor("w2t_1", [F_FF, D], F32, kind="ExternalInput").ap()
    d_b1_1 = nc.dram_tensor("b1_1", [128, FC], F32, kind="ExternalInput").ap()
    d_b2_1 = nc.dram_tensor("b2_1", [128, DC], F32, kind="ExternalInput").ap()
    d_ones_bf = nc.dram_tensor("ones_bf", [128, 128], BF16, kind="ExternalInput").ap()
    d_ones_f32 = nc.dram_tensor("ones_f32", [128, 128], F32, kind="ExternalInput").ap()

    o_x = nc.dram_tensor("x3", [D, TPC], F32, kind="ExternalOutput").ap()
    o_xln = nc.dram_tensor("xln3", [D, TPC], BF16, kind="ExternalOutput").ap()
    o_w1q = nc.dram_tensor("w1q1", [D, F_FF], BF16, kind="ExternalOutput").ap()
    o_w2q = nc.dram_tensor("w2q1", [F_FF, D], BF16, kind="ExternalOutput").ap()

    i_x2 = nc.dram_tensor("i_x2", [D, TPC], F32, kind="Internal").ap()
    i_x3a = nc.dram_tensor("i_x3a", [D, TPC], F32, kind="Internal").ap()
    i_x3 = nc.dram_tensor("i_x3", [D, TPC], F32, kind="Internal").ap()
    i_w1q1 = nc.dram_tensor("i_w1q1", [D, F_FF], BF16, kind="Internal").ap()
    i_w2q1 = nc.dram_tensor("i_w2q1", [F_FF, D], BF16, kind="Internal").ap()

    with ExitStack() as stack:
        sched = Sched(nc)
        c = Ctx(nc, sched, stack)
        alloc_common(c)
        alloc_ffn(c)
        alloc_quant(c)
        c.sb("b1t", [128, FC], F32)
        c.sb("halfb2", [128, DC], F32)
        c.sb("b1t1", [128, FC], F32)
        c.sb("halfb21", [128, DC], F32)
        ch = stack.enter_context(nc.semaphore("ch"))
        blk = stack.enter_context(nc.Block())
        s, t = c.s, c.t
        load_consts(c, d_ones_bf, d_ones_f32)
        for dst, src in [("b1t", d_b1_0), ("halfb2", d_b2_0), ("b1t1", d_b1_1), ("halfb21", d_b2_1)]:
            s.dma("sync", (lambda dst, src: lambda e: e.dma_start(out=t[dst][:, :], in_=src[:, :]))(dst, src))
        s.op("scalar", lambda e: e.activation(t["halfb2"][:, :], t["halfb2"][:, :], AF.Copy, scale=0.5))
        s.op("scalar", lambda e: e.activation(t["halfb21"][:, :], t["halfb21"][:, :], AF.Copy, scale=0.5))

        emit_headproj_residual(c, d_hout, d_wout, d_x, [i_x2])
        emit_ffn(c, i_x2, d_w1q0, d_w2q0, t["b1t"], t["halfb2"], [i_x3a])
        emit_quant(c, d_w1_1, D, F_FF, "bitquant", i_w1q1, o_w1q)
        emit_quant(c, d_w2_1, F_FF, D, "bitquant", i_w2q1, o_w2q)
        emit_ffn(c, i_x3a, i_w1q1, i_w2q1, t["b1t1"], t["halfb21"], [o_x, i_x3])
        emit_ln_to_dram(c, i_x3, o_xln)
        sched.emit(blk, ch)
    return nc


def build_launch5():
    nc = bass.Bass("TRN2", target_bir_lowering=False, debug=False)
    d_x = nc.dram_tensor("x3", [D, TPC], F32, kind="ExternalInput").ap()
    d_hout = nc.dram_tensor("hout2", [D, TPC], BF16, kind="ExternalInput").ap()
    d_wo = nc.dram_tensor("woT", [D, D], F32, kind="ExternalInput").ap()
    d_w1q1 = nc.dram_tensor("w1q1", [D, F_FF], BF16, kind="ExternalInput").ap()
    d_w2q1 = nc.dram_tensor("w2q1", [F_FF, D], BF16, kind="ExternalInput").ap()
    d_b1_1 = nc.dram_tensor("b1_1", [128, FC], F32, kind="ExternalInput").ap()
    d_b2_1 = nc.dram_tensor("b2_1", [128, DC], F32, kind="ExternalInput").ap()
    d_ones_bf = nc.dram_tensor("ones_bf", [128, 128], BF16, kind="ExternalInput").ap()
    d_ones_f32 = nc.dram_tensor("ones_f32", [128, 128], F32, kind="ExternalInput").ap()
    o_x = nc.dram_tensor("xout", [D, TPC], F32, kind="ExternalOutput").ap()
    i_x4 = nc.dram_tensor("i_x4", [D, TPC], F32, kind="Internal").ap()

    with ExitStack() as stack:
        sched = Sched(nc)
        c = Ctx(nc, sched, stack)
        alloc_common(c)
        alloc_ffn(c)
        c.sb("b1t1", [128, FC], F32)
        c.sb("halfb21", [128, DC], F32)
        ch = stack.enter_context(nc.semaphore("ch"))
        blk = stack.enter_context(nc.Block())
        s, t = c.s, c.t
        load_consts(c, d_ones_bf, d_ones_f32)
        s.dma("sync", lambda e: e.dma_start(out=t["b1t1"][:, :], in_=d_b1_1[:, :]))
        s.dma("sync", lambda e: e.dma_start(out=t["halfb21"][:, :], in_=d_b2_1[:, :]))
        s.op("scalar", lambda e: e.activation(t["halfb21"][:, :], t["halfb21"][:, :], AF.Copy, scale=0.5))
        emit_headproj_residual(c, d_hout, d_wo, d_x, [i_x4])
        emit_ffn(c, i_x4, d_w1q1, d_w2q1, t["b1t1"], t["halfb21"], [o_x])
        sched.emit(blk, ch)
    return nc


TWO_PI = 6.283185307179586
_hi64 = np.float64(TWO_PI)
_2PI_HI = np.float32(np.ldexp(np.round(np.ldexp(_hi64, 10)), -10))
_2PI_MID = np.float32(np.ldexp(np.round(np.ldexp(_hi64 - _2PI_HI, 22)), -22))
_2PI_LO = np.float32(_hi64 - np.float64(_2PI_HI) - np.float64(_2PI_MID))
RMS_EPS = 1.1920929e-07


def build_launch4():
    nc = bass.Bass("TRN2", target_bir_lowering=False, debug=False)
    d_xln = nc.dram_tensor("xln", [D, S_LEN], BF16, kind="ExternalInput").ap()
    d_wqkvT = nc.dram_tensor("wqkvT", [D, 1536], F32, kind="ExternalInput").ap()
    d_iota = nc.dram_tensor("iota", [1, S_LEN], F32, kind="ExternalInput").ap()
    d_dhalf = nc.dram_tensor("dhalf", [128, 1], F32, kind="ExternalInput").ap()
    d_perm = nc.dram_tensor("permT", [128, 128], BF16, kind="ExternalInput").ap()
    d_ones_bf = nc.dram_tensor("ones_bf", [128, 128], BF16, kind="ExternalInput").ap()
    d_ones_f32 = nc.dram_tensor("ones_f32", [128, 128], F32, kind="ExternalInput").ap()
    d_masks = nc.dram_tensor("cmask", [128, 4 * 512], BF16, kind="ExternalInput").ap()
    o_hout = nc.dram_tensor("hout2", [512, S_LEN], BF16, kind="ExternalOutput").ap()

    with ExitStack() as stack:
        sched = Sched(nc)
        c = Ctx(nc, sched, stack)
        s = sched
        c.sb("Wq", [128, DC * 1536], BF16)
        c.sb("sin", [128, S_LEN], BF16)
        c.sb("cos", [128, S_LEN], BF16)
        c.sb("iota", [1, S_LEN], F32)
        c.sb("dhalf", [128, 1], F32)
        c.sb("invf", [128, 1], F32)
        c.sb("perm", [128, 128], BF16)
        c.sb("ones_bf", [128, 128], BF16)
        c.sb("ones_f32", [128, 128], F32)
        c.sb("cmask", [128, 4 * 512], BF16)
        c.sb("xs", [128, DC * 512], BF16)
        c.sb("qf", [128, S_LEN], BF16)
        c.sb("kf", [128, S_LEN], BF16)
        c.sb("vtm", [128, S_LEN], BF16)
        c.sb("ang", [128, 512], F32)
        c.sb("tha", [128, 512], F32)
        c.sb("thb", [128, 512], F32)
        c.sb("ki", [128, 512], I32)
        c.sb("kfl", [128, 512], F32)
        c.sb("t1", [128, 512], F32)
        c.sb("t2", [128, 512], F32)
        c.sb("sqr", [128, 512], BF16)
        c.sb("rstd", [128, 512], F32)
        c.sb("pt", [128, 512], BF16)
        c.sb("linv", [128, 512], F32)
        c.sb("hot", [128, 512], BF16)
        c.ps("pa", [128, 512])
        c.ps("pe", [128, 512])
        c.ps("po", [128, 512])
        c.ps("pl", [128, 512])
        c.ps("pr", [128, 512])
        ch = stack.enter_context(nc.semaphore("ch"))
        blk = stack.enter_context(nc.Block())
        t = c.t

        for nm, src in [("iota", d_iota), ("dhalf", d_dhalf), ("perm", d_perm),
                        ("ones_bf", d_ones_bf), ("ones_f32", d_ones_f32), ("cmask", d_masks)]:
            s.dma("sync", (lambda nm, src: lambda e: e.dma_start(out=t[nm][:, :], in_=src[:, :]))(nm, src))
        s.dma("gpsimd", lambda e: e.dma_start(
            out=t["Wq"][:, :].rearrange("p (c f) -> p c f", c=DC),
            in_=d_wqkvT.rearrange("(c p) f -> p c f", p=128)))
        s.op("scalar", lambda e: e.activation(t["invf"][:, :], t["dhalf"][:, :], AF.Exp,
                                              scale=-float(np.log(10000.0)) / 64.0))
        for sl in range(16):
            t0 = sl * 512
            s.op("tensor", (lambda t0: lambda e: e.matmul(
                t["pa"][:, :], t["ones_f32"][0:1, :], t["iota"][0:1, t0:t0 + 512],
                start=True, stop=True))(t0))
            s.op("vector", lambda e: e.tensor_scalar(t["ang"][:, :], t["pa"][:, :], t["invf"][:, 0:1],
                                                     None, ALU.mult))
            for which, dst in ((0.0, "sin"), (np.pi / 2.0, "cos")):
                if which == 0.0:
                    s.op("vector", lambda e: e.tensor_copy(t["tha"][:, :], t["ang"][:, :]))
                else:
                    s.op("vector", (lambda which: lambda e: e.tensor_scalar(
                        t["tha"][:, :], t["ang"][:, :], which, None, ALU.add))(which))
                s.op("vector", lambda e: e.tensor_scalar(t["thb"][:, :], t["tha"][:, :],
                                                         1.0 / TWO_PI, None, ALU.mult))
                s.op("vector", lambda e: e.tensor_copy(t["ki"][:, :], t["thb"][:, :]))
                s.op("vector", lambda e: e.tensor_copy(t["kfl"][:, :], t["ki"][:, :]))
                s.op("vector", lambda e: e.scalar_tensor_tensor(
                    t["thb"][:, :], t["kfl"][:, :], -float(_2PI_HI), t["tha"][:, :], ALU.mult, ALU.add))
                s.op("vector", lambda e: e.scalar_tensor_tensor(
                    t["tha"][:, :], t["kfl"][:, :], -float(_2PI_MID), t["thb"][:, :], ALU.mult, ALU.add))
                s.op("vector", lambda e: e.scalar_tensor_tensor(
                    t["thb"][:, :], t["kfl"][:, :], -float(_2PI_LO), t["tha"][:, :], ALU.mult, ALU.add))
                s.op("scalar", (lambda dst, t0: lambda e: e.activation(
                    t[dst][:, t0:t0 + 512], t["thb"][:, :], AF.Sin))(dst, t0))

        eps_rms = c.cst(RMS_EPS)
        qb = c.cst(-0.5 * float(np.log(128.0)))
        zb = c.cst(0.0)
        exb = c.cst(-12.0)
        for h in range(HPC):
            for sl in range(16):
                t0 = sl * 512
                s.dma("sync", (lambda t0: lambda e: e.dma_start(
                    out=t["xs"][:, :].rearrange("p (c f) -> p c f", c=DC),
                    in_=d_xln.rearrange("(c p) t -> p c t", p=128)[:, :, t0:t0 + 512]))(t0))
                for which, m in ((0, 0 * 4 + h), (1, 1 * 4 + h)):
                    dst = "qf" if which == 0 else "kf"
                    for ci in range(DC):
                        s.op("tensor", (lambda m, ci: lambda e: e.matmul(
                            t["pa"][:, :], t["Wq"][:, ci * 1536 + m * 128:ci * 1536 + (m + 1) * 128],
                            t["xs"][:, ci * 512:(ci + 1) * 512],
                            start=(ci == 0), stop=(ci == DC - 1)))(m, ci))
                    s.op("scalar", (lambda dst, t0: lambda e: e.activation(
                        t[dst][:, t0:t0 + 512], t["pa"][:, :], AF.Copy))(dst, t0))
                mv = 2 * 4 + h
                for tb in range(4):
                    for ci in range(DC):
                        s.op("tensor", (lambda mv, tb, ci: lambda e: e.matmul(
                            t["po"][:, tb * 128:(tb + 1) * 128],
                            t["xs"][:, ci * 512 + tb * 128:ci * 512 + (tb + 1) * 128],
                            t["Wq"][:, ci * 1536 + mv * 128:ci * 1536 + (mv + 1) * 128],
                            start=(ci == 0), stop=(ci == DC - 1)))(mv, tb, ci))
                s.op("scalar", (lambda t0: lambda e: e.activation(
                    t["vtm"][:, t0:t0 + 512], t["po"][:, :], AF.Copy))(t0))
            for dst, bias in (("qf", qb), ("kf", zb)):
                for sl in range(16):
                    t0 = sl * 512
                    s.op("tensor", (lambda dst, t0: lambda e: e.matmul(
                        t["pr"][:, :], t["perm"][:, :], t[dst][:, t0:t0 + 512],
                        start=True, stop=True))(dst, t0))
                    s.op("vector", (lambda dst, t0: lambda e: e.tensor_mul(
                        t["t1"][:, :], t[dst][:, t0:t0 + 512], t["cos"][:, t0:t0 + 512]))(dst, t0))
                    s.op("vector", (lambda t0: lambda e: e.tensor_mul(
                        t["t2"][:, :], t["pr"][:, :], t["sin"][:, t0:t0 + 512]))(t0))
                    s.op("vector", lambda e: e.tensor_add(t["t1"][:, :], t["t1"][:, :], t["t2"][:, :]))
                    s.op("scalar", lambda e: e.activation(t["sqr"][:, :], t["t1"][:, :], AF.Square))
                    s.op("tensor", lambda e: e.matmul(t["pa"][:, :], t["ones_bf"][:, :], t["sqr"][:, :],
                                                      start=True, stop=True))
                    s.op("scalar", (lambda: lambda e: e.activation(
                        t["rstd"][:, :], t["pa"][:, :], AF.Ln, scale=1.0 / 128, bias=eps_rms))())
                    s.op("scalar", (lambda bias: lambda e: e.activation(
                        t["rstd"][:, :], t["rstd"][:, :], AF.Exp, scale=-0.5, bias=bias))(bias))
                    s.op("vector", (lambda dst, t0: lambda e: e.tensor_mul(
                        t[dst][:, t0:t0 + 512], t["t1"][:, :], t["rstd"][:, :]))(dst, t0))
            for qt in range(16):
                q0 = qt * 512
                nkb = (q0 + 512) // 128
                for kb in range(nkb):
                    s.op("tensor", (lambda kb, q0: lambda e: e.matmul(
                        t["pe"][:, :], t["kf"][:, kb * 128:(kb + 1) * 128], t["qf"][:, q0:q0 + 512],
                        start=True, stop=True))(kb, q0))
                    s.op("scalar", (lambda: lambda e: e.activation(
                        t["pt"][:, :], t["pe"][:, :], AF.Exp, bias=exb))())
                    doff = kb * 128 - q0
                    if doff > -128:
                        mi = doff // 128
                        assert 0 <= mi < 4
                        s.op("vector", (lambda mi: lambda e: e.tensor_mul(
                            t["pt"][:, :], t["pt"][:, :], t["cmask"][:, mi * 512:(mi + 1) * 512]))(mi))
                    s.op("tensor", (lambda kb: lambda e: e.matmul(
                        t["po"][:, :], t["vtm"][:, kb * 128:(kb + 1) * 128], t["pt"][:, :],
                        start=(kb == 0), stop=(kb == nkb - 1), skip_group_check=True))(kb))
                    s.op("tensor", (lambda kb: lambda e: e.matmul(
                        t["pl"][:, :], t["ones_bf"][:, :], t["pt"][:, :],
                        start=(kb == 0), stop=(kb == nkb - 1), skip_group_check=True))(kb))
                s.op("vector", lambda e: e.reciprocal(t["linv"][:, :], t["pl"][:, :]))
                s.op("vector", lambda e: e.tensor_mul(t["hot"][:, :], t["po"][:, :], t["linv"][:, :]))
                s.dma("sync", (lambda h, q0: lambda e: e.dma_start(
                    out=o_hout[h * 128:(h + 1) * 128, q0:q0 + 512], in_=t["hot"][:, :]))(h, q0))
        sched.emit(blk, ch)
    return nc


def _consts():
    return {
        "ones_bf": np.ones((128, 128), dtype=BF),
        "ones_f32": np.ones((128, 128), dtype=np.float32),
    }


def _consts2():
    one = np.ones((128, 128), np.float32)
    return {
        **_consts(),
        "ident_bf": np.eye(128, dtype=BF),
        "ident_f32": np.eye(128, dtype=np.float32),
        "triu": np.triu(one).astype(np.float32),
        "mski": np.triu(one).astype(BF),
        "mskn": (-np.triu(one, 1)).astype(BF),
        "mskn2": (-np.tril(one, -1)).astype(BF),
    }


def _core_heads(cid):
    return cid // 4, [4 * (cid % 4) + j for j in range(4)]


def _consts4():
    masks = np.zeros((128, 4 * 512), dtype=BF)
    for mi in range(4):
        kk = np.arange(128)[:, None] + mi * 128
        qq = np.arange(512)[None, :]
        masks[:, mi * 512:(mi + 1) * 512] = (kk <= qq).astype(BF)
    perm = np.zeros((128, 128), np.float32)
    for i in range(64):
        perm[2 * i + 1, 2 * i] = -1.0
        perm[2 * i, 2 * i + 1] = 1.0
    return {
        "cmask": masks,
        "permT": perm.astype(BF),
        "iota": np.arange(S_LEN, dtype=np.float32).reshape(1, S_LEN),
        "dhalf": (np.arange(128) // 2).astype(np.float32).reshape(128, 1),
        **_consts(),
    }


def host_prep_l4(cid, xln_b, W_qkv):
    b, hs = _core_heads(cid)
    rows = ([W_qkv[h * 128:(h + 1) * 128] for h in hs]
            + [W_qkv[2048 + h * 128:2048 + (h + 1) * 128] for h in hs]
            + [W_qkv[4096 + h * 128:4096 + (h + 1) * 128] for h in hs])
    sel = np.concatenate(rows, axis=0)
    return {
        "xln": np.ascontiguousarray(xln_b),
        "wqkvT": np.ascontiguousarray(sel.T.astype(np.float32)),
        **_consts4(),
    }


def host_prep_l2(cid, xln_b, W_qkvz, W_ba, conv_w, dt_bias, A_log, norm_w):
    b, hs = _core_heads(cid)
    wmix = np.concatenate(
        [W_qkvz[:, h * 512 + off:h * 512 + off + 128] for off in (0, 128, 256) for h in hs],
        axis=1)
    wz = np.concatenate([W_qkvz[:, h * 512 + 384:h * 512 + 512] for h in hs], axis=1)
    wba = np.stack([W_ba[:, 2 * h] for h in hs] + [W_ba[:, 2 * h + 1] for h in hs], axis=1)
    rows = ([conv_w[h * 128:(h + 1) * 128] for h in hs]
            + [conv_w[2048 + h * 128:2048 + (h + 1) * 128] for h in hs]
            + [conv_w[4096 + h * 128:4096 + (h + 1) * 128] for h in hs])
    conv_sel = np.concatenate(rows, axis=0)
    convt = np.ascontiguousarray(conv_sel.reshape(12, 128, 4).transpose(1, 0, 2).reshape(128, 48))
    return {
        "xln": np.ascontiguousarray(xln_b),
        "wmix": np.ascontiguousarray(wmix.astype(np.float32)),
        "wz": np.ascontiguousarray(wz.astype(np.float32)),
        "wba": np.ascontiguousarray(wba.astype(np.float32)),
        "convw": convt.astype(np.float32),
        "dtb": dt_bias[hs].reshape(4, 1).astype(np.float32),
        "alog": A_log[hs].reshape(4, 1).astype(np.float32),
        "nw": norm_w.reshape(128, 1).astype(np.float32),
        **_consts2(),
    }


def _b_tile(b, nchunk):
    return np.ascontiguousarray(b.reshape(nchunk, 128).T.astype(np.float32))


def kernel(L0_attn_hidden_states, L0_attn_W_qkvz, L0_attn_W_ba, L0_attn_conv_weight,
           L0_attn_dt_bias, L0_attn_A_log, L0_attn_norm_weight, L0_attn_W_out,
           L0_ffn_w1, L0_ffn_b1, L0_ffn_w2, L0_ffn_b2,
           L1_attn_W_qkv, L1_attn_W_o, L1_ffn_w1, L1_ffn_b1, L1_ffn_w2, L1_ffn_b2):
    f32 = np.float32
    x0T = np.ascontiguousarray(np.asarray(L0_attn_hidden_states, f32).reshape(TOK, D).T)
    consts = _consts()

    nc1 = build_ffn_launch("absmean")
    w1T0 = np.ascontiguousarray(np.asarray(L0_ffn_w1, f32).T)
    w2T0 = np.ascontiguousarray(np.asarray(L0_ffn_w2, f32).T)
    b1_0 = _b_tile(np.asarray(L0_ffn_b1, f32), FC)
    b2_0 = _b_tile(np.asarray(L0_ffn_b2, f32), DC)
    im1 = [{"x0": np.ascontiguousarray(x0T[:, c * TPC:(c + 1) * TPC]),
            "w1t": w1T0, "w2t": w2T0, "b1": b1_0, "b2": b2_0, **consts}
           for c in range(NCORES)]
    r1 = run_bass_kernel_spmd(nc1, im1, core_ids=list(range(NCORES))).results
    x1 = [r1[c]["x1"] for c in range(NCORES)]
    xln1 = [r1[c]["xln1"] for c in range(NCORES)]
    w1q0, w2q0 = r1[0]["w1q"], r1[0]["w2q"]

    nc2 = build_launch2()
    xln1_b = {b: np.concatenate([xln1[4 * b + j] for j in range(4)], axis=1) for b in range(2)}
    im2 = [host_prep_l2(c, xln1_b[c // 4], np.asarray(L0_attn_W_qkvz, f32),
                        np.asarray(L0_attn_W_ba, f32), np.asarray(L0_attn_conv_weight, f32),
                        np.asarray(L0_attn_dt_bias, f32), np.asarray(L0_attn_A_log, f32),
                        np.asarray(L0_attn_norm_weight, f32))
           for c in range(NCORES)]
    r2 = run_bass_kernel_spmd(nc2, im2, core_ids=list(range(NCORES))).results
    hout1 = [r2[c]["hout"] for c in range(NCORES)]

    nc3 = build_launch3()
    w1T1 = np.ascontiguousarray(np.asarray(L1_ffn_w1, f32).T)
    w2T1 = np.ascontiguousarray(np.asarray(L1_ffn_w2, f32).T)
    b1_1 = _b_tile(np.asarray(L1_ffn_b1, f32), FC)
    b2_1 = _b_tile(np.asarray(L1_ffn_b2, f32), DC)
    im3 = []
    for c in range(NCORES):
        b = c // 4
        tok0 = (c % 4) * TPC
        hout_my = np.ascontiguousarray(np.concatenate(
            [hout1[4 * b + j][:, tok0:tok0 + TPC] for j in range(4)], axis=0))
        im3.append({"x1": x1[c], "hout": hout_my,
                    "wout": np.ascontiguousarray(np.asarray(L0_attn_W_out, f32)),
                    "w1q0": w1q0, "w2q0": w2q0, "b1_0": b1_0, "b2_0": b2_0,
                    "w1t_1": w1T1, "w2t_1": w2T1, "b1_1": b1_1, "b2_1": b2_1, **consts})
    r3 = run_bass_kernel_spmd(nc3, im3, core_ids=list(range(NCORES))).results
    x3 = [r3[c]["x3"] for c in range(NCORES)]
    xln3 = [r3[c]["xln3"] for c in range(NCORES)]
    w1q1, w2q1 = r3[0]["w1q1"], r3[0]["w2q1"]

    nc4 = build_launch4()
    xln3_b = {b: np.concatenate([xln3[4 * b + j] for j in range(4)], axis=1) for b in range(2)}
    im4 = [host_prep_l4(c, xln3_b[c // 4], np.asarray(L1_attn_W_qkv, f32)) for c in range(NCORES)]
    r4 = run_bass_kernel_spmd(nc4, im4, core_ids=list(range(NCORES))).results
    hout2 = [r4[c]["hout2"] for c in range(NCORES)]

    nc5 = build_launch5()
    woT = np.ascontiguousarray(np.asarray(L1_attn_W_o, f32).T)
    im5 = []
    for c in range(NCORES):
        b = c // 4
        tok0 = (c % 4) * TPC
        hout_my = np.ascontiguousarray(np.concatenate(
            [hout2[4 * b + j][:, tok0:tok0 + TPC] for j in range(4)], axis=0))
        im5.append({"x3": x3[c], "hout2": hout_my, "woT": woT,
                    "w1q1": w1q1, "w2q1": w2q1, "b1_1": b1_1, "b2_1": b2_1, **consts})
    r5 = run_bass_kernel_spmd(nc5, im5, core_ids=list(range(NCORES))).results
    out = np.concatenate([r5[c]["xout"] for c in range(NCORES)], axis=1)
    return np.ascontiguousarray(out.T).reshape(BATCH, S_LEN, D).astype(np.float32)


# version 02
# speedup vs baseline: 1.5690x; 1.5690x over previous
import numpy as np
import ml_dtypes
from contextlib import ExitStack

import concourse.bass as bass
import concourse.mybir as mybir
from concourse.bass_utils import run_bass_kernel_spmd

F32 = mybir.dt.float32
BF16 = mybir.dt.bfloat16
I32 = mybir.dt.int32
AF = mybir.ActivationFunctionType
ALU = mybir.AluOpType
AX = mybir.AxisListType

D = 2048
S_LEN = 8192
BATCH = 2
TOK = BATCH * S_LEN
NCORES = 8
TPC = TOK // NCORES
DC = D // 128
F_FF = 8192
FC = F_FF // 128
TT = 1024
NTT = TPC // TT
HPC = 4
CH = 128
NCH = S_LEN // CH

BF = ml_dtypes.bfloat16


class Sched:

    def __init__(self, nc):
        self.nc = nc
        self.steps = []
        self.track_totals = {}

    def _add(self, eng, fn, kind, after, waits, track):
        if after is not None:
            assert track is not None or kind == "raw", (
                "a chain-relaxed (after=) step must inc a track sem, not the chain")
        if track is not None:
            sem, amt = track
            self.track_totals[sem] = self.track_totals.get(sem, 0) + amt
        self.steps.append((eng, fn, kind, after, waits or [], track))
        return len(self.steps) - 1

    def dma(self, eng, fn, after=None, waits=None, track=None):
        return self._add(eng, fn, "dma", after, waits, track)

    def op(self, eng, fn, after=None, waits=None, track=None):
        return self._add(eng, fn, "compute", after, waits, track)

    def raw(self, eng, fn, after=None, waits=None):
        return self._add(eng, fn, "raw", after, waits, None)

    def join(self, eng, waits):
        def fn(e):
            return e.nop()
        return self._add(eng, fn, "compute", None, waits, None)

    def emit(self, blk, ch):
        steps = self.steps
        n = len(steps)
        solo = {"vector", "scalar", "gpsimd"}
        group_id = [0] * n
        g = 0
        for i in range(1, n):
            prev = steps[i - 1]
            cur = steps[i]
            if (cur[0] != prev[0]
                    or (cur[0] in solo and cur[2] == "compute")
                    or cur[3] is not None or cur[4]
                    or cur[5] is not None or prev[5] is not None):
                g += 1
            group_id[i] = g
        ngroups = g + 1
        inc_of = [0] * n
        last_in_group = {}
        for i in range(n):
            last_in_group[group_id[i]] = i
        for i in range(n):
            if steps[i][5] is None and steps[i][2] == "dma":
                inc_of[i] = 16
        for gid, i in last_in_group.items():
            if steps[i][5] is None and steps[i][2] == "compute":
                inc_of[i] = 1
        cum = 0
        before = [0] * n
        after_val = [0] * n
        for i in range(n):
            before[i] = cum
            cum += inc_of[i]
            after_val[i] = cum
        total = cum
        group_wait = [None] * ngroups
        seen = -1
        for i in range(n):
            if group_id[i] != seen:
                seen = group_id[i]
                if steps[i][3] is not None:
                    vals = [after_val[j] for j in steps[i][3]]
                    group_wait[seen] = max(vals) if vals else None
                else:
                    group_wait[seen] = before[i] if before[i] > 0 else None
        last_set = set(last_in_group.values())
        final_waits = list(self.track_totals.items())

        def section(name):
            drain_engines = ("vector", "scalar", "gpsimd")

            def body(e):
                prev_gid = -1
                for i in range(n):
                    eng, fn, kind, after, waits, track = steps[i]
                    if eng != name:
                        continue
                    gid = group_id[i]
                    if gid != prev_gid:
                        w = group_wait[gid]
                        if w is not None:
                            e.wait_ge(ch, w)
                        prev_gid = gid
                        for sem, val in waits:
                            if val > 0:
                                e.wait_ge(sem, val)
                    inst = fn(e)
                    if track is not None:
                        sem, amt = track
                        if (kind == "compute" and name in drain_engines):
                            e.drain().then_inc(sem, amt)
                        else:
                            inst.then_inc(sem, amt)
                    elif inc_of[i]:
                        if (kind == "compute" and name in drain_engines
                                and i in last_set):
                            e.drain().then_inc(ch, inc_of[i])
                        else:
                            inst.then_inc(ch, inc_of[i])
                if name == "sync":
                    e.wait_ge(ch, total)
                    for sem, val in final_waits:
                        e.wait_ge(sem, val)
            return body

        blk.sync(section("sync"))
        blk.tensor(section("tensor"))
        blk.vector(section("vector"))
        blk.scalar(section("scalar"))
        blk.gpsimd(section("gpsimd"))


class Ctx:
    def __init__(self, nc, sched, stack):
        self.nc = nc
        self.s = sched
        self.stack = stack
        self.t = {}
        self.cmap = {}
        self._uid = 0

    def sem(self, name):
        self._uid += 1
        return self.stack.enter_context(self.nc.semaphore(f"tk_{name}{self._uid}"))

    def cst(self, value):
        if "cpool" not in self.t:
            self.sb("cpool", [128, 32], F32)
        if value not in self.cmap:
            col = len(self.cmap)
            assert col < 32
            self.cmap[value] = col
            self.s.op("vector", (lambda col=col, value=value: (
                lambda e: e.memset(self.t["cpool"][:, col:col + 1], value)))())
        col = self.cmap[value]
        return self.t["cpool"][:, col:col + 1]

    def sb(self, name, shape, dt):
        h = self.stack.enter_context(self.nc.sbuf_tensor("s_" + name, list(shape), dt))
        self.t[name] = h
        return h

    def ps(self, name, shape, dt=F32):
        h = self.stack.enter_context(self.nc.psum_tensor("p_" + name, list(shape), dt))
        self.t[name] = h
        return h



def emit_ln(c, x_dram, xln_sbuf, tok0, ntok, x_dram_off=0, x_ready=None):
    s, t = c.s, c.t
    pa, pb = t["pa"], t["pb"]
    ones = t["ones_bf"]
    col = x_dram_off + tok0
    sLX = c.sem("lnx")
    nsl = ntok // 512
    if not hasattr(c, "xt_last_read"):
        c.xt_last_read = [None, None]
    last_read = c.xt_last_read
    nld = 0

    def xtile(ci):
        return t["xt"] if ci % 2 == 0 else t["xt2"]

    def dep(ci):
        a = [] if last_read[ci % 2] is None else [last_read[ci % 2]]
        if x_ready is not None:
            a.append(x_ready)
        return a

    for ci in range(DC):
        aft = dep(ci)
        s.dma("sync", (lambda ci: lambda e: e.dma_start(
            out=xtile(ci)[:, :ntok], in_=x_dram[ci * 128:(ci + 1) * 128, col:col + ntok]))(ci),
            after=aft, track=(sLX, 16))
        nld += 1
        s.op("scalar", (lambda ci: lambda e: e.activation(
            t["xbt"][:, :ntok], xtile(ci)[:, :ntok], AF.Copy))(ci),
            waits=[(sLX, 16 * nld)])
        last_read[ci % 2] = s.op("scalar", (lambda ci: lambda e: e.activation(
            t["sqt"][:, :ntok], xtile(ci)[:, :ntok], AF.Square))(ci))
        for sl in range(nsl):
            s.op("tensor", (lambda ci, sl: lambda e: e.matmul(
                pa[:, sl * 512:(sl + 1) * 512], ones[:, :], t["xbt"][:, sl * 512:(sl + 1) * 512],
                start=(ci == 0), stop=(ci == DC - 1), skip_group_check=True))(ci, sl))
            s.op("tensor", (lambda ci, sl: lambda e: e.matmul(
                pb[:, sl * 512:(sl + 1) * 512], ones[:, :], t["sqt"][:, sl * 512:(sl + 1) * 512],
                start=(ci == 0), stop=(ci == DC - 1), skip_group_check=True))(ci, sl))
    s.op("scalar", lambda e: e.activation(t["mb"][:, :ntok], pa[:, :ntok], AF.Copy, scale=1.0 / D))
    s.op("scalar", lambda e: e.activation(t["vb"][:, :ntok], pb[:, :ntok], AF.Copy, scale=1.0 / D))
    s.op("scalar", lambda e: e.activation(pa[:, :ntok], t["mb"][:, :ntok], AF.Square))
    s.op("vector", lambda e: e.tensor_sub(t["vb"][:, :ntok], t["vb"][:, :ntok], pa[:, :ntok]))
    eps_ap = c.cst(1e-5)
    s.op("scalar", lambda e: e.activation(t["rb"][:, :ntok], t["vb"][:, :ntok], AF.Sqrt, bias=eps_ap))
    s.op("vector", lambda e: e.reciprocal(t["rb"][:, :ntok], t["rb"][:, :ntok]))
    for ci in range(DC):
        aft = dep(ci)
        s.dma("sync", (lambda ci: lambda e: e.dma_start(
            out=xtile(ci)[:, :ntok], in_=x_dram[ci * 128:(ci + 1) * 128, col:col + ntok]))(ci),
            after=aft, track=(sLX, 16))
        nld += 1
        last_read[ci % 2] = s.op("scalar", (lambda ci: lambda e: e.activation(
            t["xbt"][:, :ntok], xtile(ci)[:, :ntok], AF.Copy))(ci),
            waits=[(sLX, 16 * nld)])
        s.op("vector", lambda e: e.tensor_sub(t["tmpf"][:, :ntok], t["xbt"][:, :ntok], t["mb"][:, :ntok]))
        s.op("vector", (lambda ci: lambda e: e.tensor_mul(
            xln_sbuf[:, ci * ntok:(ci + 1) * ntok], t["tmpf"][:, :ntok], t["rb"][:, :ntok]))(ci))


QCOL = 1024


def emit_quant(c, w_dram, rows, cols, kind, wq_internal, wq_out, ready=None):
    s, t = c.s, c.t
    nrc = rows // 128
    ncb = cols // QCOL
    nblk = nrc * ncb
    assert nblk <= 128
    sQW = c.sem("qw")
    sQS = c.sem("qs")
    nld = 0
    nst = 0
    rdy = [] if ready is None else [ready]
    last_read = [None, None]

    def wtile(j):
        return t["wrow"] if j % 2 == 0 else t["wrow2"]

    for r in range(nrc):
        for cb in range(ncb):
            j = r * ncb + cb
            aft = rdy if last_read[j % 2] is None else [last_read[j % 2]]
            s.dma("sync", (lambda r, cb, j: lambda e: e.dma_start(
                out=wtile(j)[:, :], in_=w_dram[r * 128:(r + 1) * 128, cb * QCOL:(cb + 1) * QCOL]))(r, cb, j),
                after=aft, track=(sQW, 16))
            nld += 1
            last_read[j % 2] = s.op("vector", (lambda j: lambda e: e.reduce_sum(
                t["parts"][:, 2 + j:3 + j], wtile(j)[:, :], AX.X, apply_absolute_value=True))(j),
                waits=[(sQW, 16 * nld)])
    s.op("vector", (lambda nblk: lambda e: e.reduce_sum(
        t["parts"][:, 0:1], t["parts"][:, 2:2 + nblk], AX.X))(nblk))
    s.op("tensor", lambda e: e.matmul(t["pc"][:, 0:1], t["ones_f32"][:, :], t["parts"][:, 0:1],
                                      start=True, stop=True))
    if kind == "absmean":
        s.op("scalar", lambda e: e.activation(t["thr"][:, 0:1], t["pc"][:, 0:1], AF.Copy,
                                              scale=0.7 / (rows * cols)))
    else:
        s.op("scalar", lambda e: e.activation(t["thr"][:, 0:1], t["pc"][:, 0:1], AF.Copy,
                                              scale=1.0 / (rows * cols)))
        s.op("vector", lambda e: e.tensor_scalar(t["thr"][:, 0:1], t["thr"][:, 0:1], 1e-8, 0.5,
                                                 ALU.max, ALU.mult))
    for r in range(nrc):
        for cb in range(ncb):
            j = r * ncb + cb
            cs_ = slice(cb * QCOL, (cb + 1) * QCOL)
            aft = [] if last_read[j % 2] is None else [last_read[j % 2]]
            s.dma("sync", (lambda r, cs_, j: lambda e: e.dma_start(
                out=wtile(j)[:, :], in_=w_dram[r * 128:(r + 1) * 128, cs_]))(r, cs_, j),
                after=aft, track=(sQW, 16))
            nld += 1
            s.op("scalar", (lambda j: lambda e: e.activation(
                t["sgn"][:, :], wtile(j)[:, :], AF.Sign))(j), waits=[(sQW, 16 * nld)])
            last_read[j % 2] = s.op("scalar", (lambda j: lambda e: e.activation(
                t["absw"][:, :], wtile(j)[:, :], AF.Abs))(j))
            s.op("vector", lambda e: e.tensor_scalar(
                t["absw"][:, :], t["absw"][:, :], t["thr"][:, 0:1], None, ALU.is_ge))
            qm = s.op("vector", (lambda j: lambda e: e.tensor_mul(
                (t["wqt"] if j % 2 == 0 else t["wqt2"])[:, :], t["sgn"][:, :], t["absw"][:, :]))(j),
                waits=[(sQS, max(0, nst - 1) * (32 if wq_out is not None else 16))])
            wqt = t["wqt"] if j % 2 == 0 else t["wqt2"]
            s.dma("gpsimd", (lambda r, cs_, wqt: lambda e: e.dma_start(
                out=wq_internal[r * 128:(r + 1) * 128, cs_], in_=wqt[:, :]))(r, cs_, wqt),
                after=[qm], track=(sQS, 16))
            if wq_out is not None:
                s.dma("gpsimd", (lambda r, cs_, wqt: lambda e: e.dma_start(
                    out=wq_out[r * 128:(r + 1) * 128, cs_], in_=wqt[:, :]))(r, cs_, wqt),
                    after=[qm], track=(sQS, 16))
            nst += 1
    jn = s.join("sync", [(sQS, nst * (32 if wq_out is not None else 16)), (sQW, 16 * nld)])
    c.xt_last_read = [jn, jn]
    return jn


def emit_ffn(c, x_in_dram, w1q, w2q, b1t, halfb2, x_outs, x_in_off=0,
             w_ready=None, x_ready=None):
    s, t = c.s, c.t
    XLN, H = t["XLN"], t["H"]
    pc, pd = t["pc"], t["pd"]
    w1v = w1q.rearrange("(c p) f -> p c f", p=128)
    w2v = w2q.rearrange("(c p) f -> p c f", p=128)
    sW1 = c.sem("w1")
    sW2 = c.sem("w2")
    sGel = c.sem("gel")
    sX = c.sem("ffx")
    sTS = c.sem("ffts")
    sSt = c.sem("ffst")
    nW1 = nW2 = nGel = nX = nTS = nSt = 0
    nst_per = len(x_outs) * 16
    wdep = [] if w_ready is None else [w_ready]
    join = None
    mmh_idx = [None, None]
    xwar = [0, 0]
    for ti in range(NTT):
        tok0 = ti * TT
        emit_ln(c, x_in_dram, XLN, tok0, TT, x_dram_off=x_in_off, x_ready=x_ready)
        mm_last = [None, None]
        for fi in range(FC):
            par = fi % 2
            w1t = t["w1t"] if par == 0 else t["w1t2"]
            ps_a = pc if par == 0 else pd
            aft = list(wdep) if mm_last[par] is None else [mm_last[par]]
            s.dma("sync", (lambda fi, w1t: lambda e: e.dma_start(
                out=w1t[:, :].rearrange("p (c f) -> p c f", c=DC),
                in_=w1v[:, :, fi * 128:(fi + 1) * 128]))(fi, w1t),
                after=aft, track=(sW1, 16))
            nW1 += 1
            for ci in range(DC):
                for sl in range(TT // 512):
                    w = [(sW1, 16 * nW1), (sGel, max(0, nGel - 1))] if ci == 0 and sl == 0 else None
                    idx = s.op("tensor", (lambda fi, ci, sl, w1t, ps_a: lambda e: e.matmul(
                        ps_a[:, sl * 512:(sl + 1) * 512], w1t[:, ci * 128:(ci + 1) * 128],
                        XLN[:, ci * TT + sl * 512:ci * TT + (sl + 1) * 512],
                        start=(ci == 0), stop=(ci == DC - 1)))(fi, ci, sl, w1t, ps_a),
                        waits=w)
            mm_last[par] = idx
            s.op("scalar", (lambda fi, ps_a: lambda e: e.activation(
                H[:, fi * TT:(fi + 1) * TT], ps_a[:, :TT], AF.Gelu, bias=b1t[:, fi:fi + 1]))(fi, ps_a),
                after=[idx], track=(sGel, 1))
            nGel += 1
        NQ = 4
        FQ = FC // NQ
        for di in range(DC):
            par = di % 2
            ps_b = pc if par == 0 else pd
            for q in range(NQ):
                qb = q % 2
                wt = (t["w2ta"], t["w2tb"])[qb]
                aft = list(wdep) if mmh_idx[qb] is None else [mmh_idx[qb]]
                s.dma("sync", (lambda di, q, wt: lambda e: e.dma_start(
                    out=wt[:, :].rearrange("p (c f) -> p c f", c=FQ),
                    in_=w2v[:, q * FQ:(q + 1) * FQ, di * 128:(di + 1) * 128]))(di, q, wt),
                    after=aft, track=(sW2, 16))
                nW2 += 1
                for fi2 in range(FQ):
                    fi = q * FQ + fi2
                    w = None
                    if fi2 == 0:
                        w = [(sW2, 16 * nW2)]
                        if q == 0:
                            w.append((sGel, FC * (ti + 1)))
                            w.append((sTS, max(0, nTS - 1)))
                    for sl in range(TT // 512):
                        idx = s.op("tensor", (lambda di, fi, sl, wt, fi2, ps_b: lambda e: e.matmul(
                            ps_b[:, sl * 512:(sl + 1) * 512], wt[:, fi2 * 128:(fi2 + 1) * 128],
                            H[:, fi * TT + sl * 512:fi * TT + (sl + 1) * 512],
                            start=(fi == 0), stop=(fi == FC - 1)))(di, fi, sl, wt, fi2, ps_b),
                            waits=w if sl == 0 else None)
                        w = None
                mmh_idx[qb] = idx
            mmB_end = idx
            xt2b = t["xt"] if par == 0 else t["xt2"]
            aftx = [] if x_ready is None else [x_ready]
            if c.xt_last_read[par] is not None:
                aftx = aftx + [c.xt_last_read[par]]
            s.dma("sync", (lambda di, tok0, xt2b: lambda e: e.dma_start(
                out=xt2b[:, :TT],
                in_=x_in_dram[di * 128:(di + 1) * 128, x_in_off + tok0:x_in_off + tok0 + TT]))(di, tok0, xt2b),
                after=aftx, waits=[(sTS, xwar[par])], track=(sX, 16))
            nX += 1
            c.xt_last_read[par] = None
            tmp = t["tmpf"] if par == 0 else t["tmpf2"]
            s.op("vector", (lambda di, tmp, ps_b: lambda e: e.tensor_scalar(
                tmp[:, :TT], ps_b[:, :TT], 0.5, halfb2[:, di:di + 1], ALU.mult, ALU.add))(di, tmp, ps_b),
                after=[mmB_end], waits=[(sSt, max(0, nSt - nst_per))], track=(sTS, 1))
            nTS += 1
            s.op("vector", (lambda tmp, xt2b: lambda e: e.tensor_add(
                tmp[:, :TT], tmp[:, :TT], xt2b[:, :TT]))(tmp, xt2b),
                after=[], waits=[(sTS, nTS), (sX, 16 * nX)], track=(sTS, 1))
            nTS += 1
            xwar[par] = nTS
            for xo in x_outs:
                s.dma("gpsimd", (lambda di, tok0, xo, tmp: lambda e: e.dma_start(
                    out=xo[di * 128:(di + 1) * 128, tok0:tok0 + TT],
                    in_=tmp[:, :TT]))(di, tok0, xo, tmp),
                    after=[], waits=[(sTS, nTS)], track=(sSt, 16))
                nSt += 16
        join = s.join("sync", [(sSt, nSt), (sTS, nTS), (sW1, 16 * nW1), (sW2, 16 * nW2),
                               (sX, 16 * nX), (sGel, nGel)])
        c.xt_last_read = [join, join]
        xwar = [nTS, nTS]
    return join


def emit_ln_to_dram(c, x_dram, xln_dram, x_off=0):
    s, t = c.s, c.t
    for ti in range(NTT):
        tok0 = ti * TT
        emit_ln(c, x_dram, t["XLN"], tok0, TT, x_dram_off=x_off)
        for ci in range(DC):
            s.dma("sync", (lambda ci, tok0: lambda e: e.dma_start(
                out=xln_dram[ci * 128:(ci + 1) * 128, tok0:tok0 + TT],
                in_=t["XLN"][:, ci * TT:(ci + 1) * TT]))(ci, tok0))


def alloc_common(c):
    c.sb("xt", [128, TT], F32)
    c.sb("xt2", [128, TT], F32)
    c.sb("xbt", [128, TT], BF16)
    c.sb("sqt", [128, TT], BF16)
    c.sb("tmpf", [128, TT], F32)
    c.sb("tmpf2", [128, TT], F32)
    c.sb("mb", [128, TT], F32)
    c.sb("vb", [128, TT], F32)
    c.t["rb"] = c.t["vb"]
    c.sb("ones_bf", [128, 128], BF16)
    c.sb("ones_f32", [128, 128], F32)
    c.sb("XLN", [128, DC * TT], BF16)
    c.ps("pa", [128, TT])
    c.ps("pb", [128, TT])
    c.ps("pc", [128, TT])
    c.ps("pd", [128, TT])


def alloc_ffn(c):
    c.sb("H", [128, FC * TT], BF16)
    c.sb("w1t", [128, DC * 128], BF16)
    c.sb("w1t2", [128, DC * 128], BF16)
    c.sb("w2ta", [128, (FC // 4) * 128], BF16)
    c.sb("w2tb", [128, (FC // 4) * 128], BF16)


def alloc_quant(c):
    c.sb("parts", [128, 130], F32)
    c.sb("thr", [128, 1], F32)
    t = c.t
    t["wrow"], t["wrow2"] = t["xt"], t["xt2"]
    t["sgn"], t["absw"] = t["tmpf"], t["tmpf2"]
    t["wqt"], t["wqt2"] = t["xbt"], t["sqt"]


def load_consts(c, d_ones_bf, d_ones_f32):
    s, t = c.s, c.t
    s.dma("sync", lambda e: e.dma_start(out=t["ones_bf"][:, :], in_=d_ones_bf[:, :]))
    s.dma("sync", lambda e: e.dma_start(out=t["ones_f32"][:, :], in_=d_ones_f32[:, :]))


def build_ffn_launch(quant_kind):
    nc = bass.Bass("TRN2", target_bir_lowering=False, debug=False)
    d_x = nc.dram_tensor("x0", [D, TPC], F32, kind="ExternalInput").ap()
    d_w1 = nc.dram_tensor("w1t", [D, F_FF], F32, kind="ExternalInput").ap()
    d_w2 = nc.dram_tensor("w2t", [F_FF, D], F32, kind="ExternalInput").ap()
    d_b1 = nc.dram_tensor("b1", [128, FC], F32, kind="ExternalInput").ap()
    d_b2 = nc.dram_tensor("b2", [128, DC], F32, kind="ExternalInput").ap()
    d_ones_bf = nc.dram_tensor("ones_bf", [128, 128], BF16, kind="ExternalInput").ap()
    d_ones_f32 = nc.dram_tensor("ones_f32", [128, 128], F32, kind="ExternalInput").ap()

    o_x = nc.dram_tensor("x1", [D, TPC], F32, kind="ExternalOutput").ap()
    o_xln = nc.dram_tensor("xln1", [D, TPC], BF16, kind="ExternalOutput").ap()
    o_w1q = nc.dram_tensor("w1q", [D, F_FF], BF16, kind="ExternalOutput").ap()
    o_w2q = nc.dram_tensor("w2q", [F_FF, D], BF16, kind="ExternalOutput").ap()

    i_w1q = nc.dram_tensor("i_w1q", [D, F_FF], BF16, kind="Internal").ap()
    i_w2q = nc.dram_tensor("i_w2q", [F_FF, D], BF16, kind="Internal").ap()
    i_x1 = nc.dram_tensor("i_x1", [D, TPC], F32, kind="Internal").ap()

    with ExitStack() as stack:
        sched = Sched(nc)
        c = Ctx(nc, sched, stack)
        alloc_common(c)
        alloc_ffn(c)
        alloc_quant(c)
        c.sb("b1t", [128, FC], F32)
        c.sb("halfb2", [128, DC], F32)
        ch = stack.enter_context(nc.semaphore("ch"))
        blk = stack.enter_context(nc.Block())

        s, t = c.s, c.t
        load_consts(c, d_ones_bf, d_ones_f32)
        s.dma("sync", lambda e: e.dma_start(out=t["b1t"][:, :], in_=d_b1[:, :]))
        s.dma("sync", lambda e: e.dma_start(out=t["halfb2"][:, :], in_=d_b2[:, :]))
        s.op("scalar", lambda e: e.activation(t["halfb2"][:, :], t["halfb2"][:, :], AF.Copy, scale=0.5))

        j1 = emit_quant(c, d_w1, D, F_FF, quant_kind, i_w1q, o_w1q)
        j2 = emit_quant(c, d_w2, F_FF, D, quant_kind, i_w2q, o_w2q, ready=j1)
        j3 = emit_ffn(c, d_x, i_w1q, i_w2q, t["b1t"], t["halfb2"], [o_x, i_x1], w_ready=j2)
        emit_ln_to_dram(c, i_x1, o_xln)

        sched.emit(blk, ch)
    return nc


TTA = 1024
NTA = S_LEN // TTA
BPT = TTA // CH


def build_launch2(dbg=False):
    nc = bass.Bass("TRN2", target_bir_lowering=False, debug=False)
    d_xln = nc.dram_tensor("xln", [D, S_LEN], BF16, kind="ExternalInput").ap()
    d_wmix = nc.dram_tensor("wmix", [D, 1536], F32, kind="ExternalInput").ap()
    d_wz = nc.dram_tensor("wz", [D, 512], F32, kind="ExternalInput").ap()
    d_wba = nc.dram_tensor("wba", [D, 8], F32, kind="ExternalInput").ap()
    d_convw = nc.dram_tensor("convw", [128, 48], F32, kind="ExternalInput").ap()
    d_dtb = nc.dram_tensor("dtb", [4, 1], F32, kind="ExternalInput").ap()
    d_alog = nc.dram_tensor("alog", [4, 1], F32, kind="ExternalInput").ap()
    d_nw = nc.dram_tensor("nw", [128, 1], F32, kind="ExternalInput").ap()
    d_ones_bf = nc.dram_tensor("ones_bf", [128, 128], BF16, kind="ExternalInput").ap()
    d_ones_f32 = nc.dram_tensor("ones_f32", [128, 128], F32, kind="ExternalInput").ap()
    d_ident_bf = nc.dram_tensor("ident_bf", [128, 128], BF16, kind="ExternalInput").ap()
    d_ident_f32 = nc.dram_tensor("ident_f32", [128, 128], F32, kind="ExternalInput").ap()
    d_triu = nc.dram_tensor("triu", [128, 128], F32, kind="ExternalInput").ap()
    d_mski = nc.dram_tensor("mski", [128, 128], BF16, kind="ExternalInput").ap()
    d_mskn = nc.dram_tensor("mskn", [128, 128], BF16, kind="ExternalInput").ap()
    d_mskn2 = nc.dram_tensor("mskn2", [128, 128], BF16, kind="ExternalInput").ap()
    o_hout = nc.dram_tensor("hout", [512, S_LEN], BF16, kind="ExternalOutput").ap()

    with ExitStack() as stack:
        sched = Sched(nc)
        c = Ctx(nc, sched, stack)
        s = sched
        c.sb("Wmix", [128, DC * 1536], BF16)
        c.sb("Wz", [128, DC * 512], BF16)
        c.sb("Wba", [128, DC * 8], BF16)
        c.sb("convw", [128, 48], F32)
        c.sb("nw", [128, 1], F32)
        c.sb("negEA", [4, 1], F32)
        c.sb("dtb", [4, 1], F32)
        for nm, dt in [("ones_bf", BF16), ("ones_f32", F32), ("ident_bf", BF16),
                       ("ident_f32", F32), ("triu", F32), ("mski", BF16),
                       ("mskn", BF16), ("mskn2", BF16)]:
            c.sb(nm, [128, 128], dt)
        c.sb("xs", [128, DC * 512], BF16)
        c.sb("mixed", [128, 12 * (TTA + 3)], BF16)
        c.sb("tail", [128, 12 * 3], BF16)
        c.sb("Y", [128, 12 * TTA], BF16)
        c.sb("Z", [128, 4 * TTA], BF16)
        c.sb("acc_a", [128, TTA], F32)
        c.sb("acc_b", [128, TTA], F32)
        c.sb("bgate", [4, TTA], F32)
        c.sb("arows", [4, TTA], F32)
        c.sb("grow", [4, TTA], F32)
        c.sb("betaf", [128, HPC * TTA], BF16)
        c.sb("beta_tm", [128, BPT * 4], F32)
        c.sb("g_tm", [128, BPT * 4], F32)
        c.sb("gc_all", [128, BPT * 4], F32)
        c.sb("egc_all", [128, BPT * 4], F32)
        c.sb("gl_all", [128, BPT * 4], F32)
        c.sb("ekd_all", [128, BPT * 4], F32)
        c.sb("egl_all", [128, BPT * 4], F32)
        c.sb("glast", [1, BPT * 4], F32)
        c.sb("ktm", [128, HPC * BPT * 128], BF16)
        c.sb("vtm", [128, HPC * BPT * 128], BF16)
        c.sb("sqn", [128, TTA], BF16)
        c.sb("rstdn", [128, TTA], F32)
        c.sb("gcb", [128, 128], F32)
        c.sb("dec", [128, 128], F32)
        c.sb("dec2", [128, 128], F32)
        c.sb("mn", [128, 128], F32)
        c.sb("egcf", [128, 128], F32)
        c.sb("kbfm", [128, 128], BF16)
        c.sb("ttmp", [128, 128], F32)
        c.sb("ttmp2", [128, 128], F32)
        c.sb("Atile", [128, 128], BF16)
        c.sb("Q0", [128, 128], BF16)
        c.sb("PQ_a", [128, 256], BF16)
        c.sb("PQ_b", [128, 256], BF16)
        c.sb("X_a", [128, 128], BF16)
        c.sb("X_b", [128, 128], BF16)
        c.sb("uw", [128, 256], BF16)
        c.sb("vnew", [128, 128], BF16)
        c.sb("qe", [128, 128], BF16)
        c.sb("attnT", [128, 128], BF16)
        c.sb("vb_tm", [128, 128], BF16)
        c.sb("kbe_tm", [128, 128], BF16)
        c.sb("kdec_tm", [128, 128], BF16)
        c.sb("Sf", [128, HPC * 2 * 128], F32)
        c.sb("Sb", [128, HPC * 128], BF16)
        c.sb("sqo", [128, 128], BF16)
        c.sb("rstdo", [128, 128], F32)
        c.sb("szt", [128, 128], BF16)
        c.sb("houtt", [128, 128], BF16)
        c.ps("pa", [128, 512])
        c.ps("pz", [128, 512])
        c.ps("pt", [128, 512], BF16)
        c.ps("pfa", [128, 512])
        c.ps("pfb", [128, 512])
        c.ps("po", [128, 128])
        c.ps("pg", [128, 512])

        if dbg:
            c.sb("_dbg_po", [128, 128], F32)
            _dsel = dbg if isinstance(dbg, tuple) else (0, dbg if isinstance(dbg, int) else 0)
            c.t["_dbg_sel"] = (_dsel[0], _dsel[1], 0)
            c.t["_dbg_Sdram"] = nc.dram_tensor("dbg_Straj", [128, 64 * 128], F32, kind="ExternalOutput").ap()
            c.t["_dbg_Bdram"] = nc.dram_tensor("dbg_Btraj", [128, 64 * 128], F32, kind="ExternalOutput").ap()
        ch = stack.enter_context(nc.semaphore("ch"))
        blk = stack.enter_context(nc.Block())
        t = c.t
        t["sp1"] = t["acc_a"]
        t["sp2"] = t["acc_b"]

        for nm, src in [("ones_bf", d_ones_bf), ("ones_f32", d_ones_f32),
                        ("ident_bf", d_ident_bf), ("ident_f32", d_ident_f32),
                        ("triu", d_triu), ("mski", d_mski), ("mskn", d_mskn),
                        ("mskn2", d_mskn2), ("convw", d_convw), ("nw", d_nw),
                        ("dtb", d_dtb)]:
            s.dma("sync", (lambda nm, src: lambda e: e.dma_start(out=t[nm][:, :], in_=src[:, :]))(nm, src))
        s.dma("gpsimd", lambda e: e.dma_start(
            out=t["Wmix"][:, :].rearrange("p (c f) -> p c f", c=DC),
            in_=d_wmix.rearrange("(c p) f -> p c f", p=128)))
        s.dma("gpsimd", lambda e: e.dma_start(
            out=t["Wz"][:, :].rearrange("p (c f) -> p c f", c=DC),
            in_=d_wz.rearrange("(c p) f -> p c f", p=128)))
        s.dma("gpsimd", lambda e: e.dma_start(
            out=t["Wba"][:, :].rearrange("p (c f) -> p c f", c=DC),
            in_=d_wba.rearrange("(c p) f -> p c f", p=128)))
        s.dma("sync", lambda e: e.dma_start(out=t["negEA"][:, :], in_=d_alog[:, :]))
        s.op("scalar", lambda e: e.activation(t["negEA"][:, :], t["negEA"][:, :], AF.Exp))
        s.op("vector", lambda e: e.tensor_scalar_mul(t["negEA"][:, :], t["negEA"][:, :], -1.0))
        s.op("vector", lambda e: e.memset(t["tail"][:, :], 0.0))
        s.op("vector", lambda e: e.memset(t["Sf"][:, :], 0.0))
        s.op("vector", lambda e: e.memset(t["Sb"][:, :], 0.0))

        ln128 = float(np.log(128.0))
        eps_l2 = c.cst(1e-6)
        one_c = c.cst(1.0)
        qbias = c.cst(-0.5 * ln128)
        zero_c = c.cst(0.0)
        eps_rms = c.cst(1e-6)

        for ti in range(NTA):
            tok0 = ti * TTA
            for sl in range(2):
                t0 = tok0 + sl * 512
                s.dma("sync", (lambda t0: lambda e: e.dma_start(
                    out=t["xs"][:, :].rearrange("p (c f) -> p c f", c=DC),
                    in_=d_xln.rearrange("(c p) t -> p c t", p=128)[:, :, t0:t0 + 512]))(t0))
                for m in range(12):
                    for ci in range(DC):
                        s.op("tensor", (lambda m, ci: lambda e: e.matmul(
                            t["pa"][:, :], t["Wmix"][:, ci * 1536 + m * 128:ci * 1536 + (m + 1) * 128],
                            t["xs"][:, ci * 512:(ci + 1) * 512],
                            start=(ci == 0), stop=(ci == DC - 1)))(m, ci))
                    s.op("scalar", (lambda m, sl: lambda e: e.activation(
                        t["mixed"][:, m * (TTA + 3) + 3 + sl * 512: m * (TTA + 3) + 3 + (sl + 1) * 512],
                        t["pa"][:, :], AF.Copy))(m, sl))
                for zc in range(4):
                    for ci in range(DC):
                        s.op("tensor", (lambda zc, ci: lambda e: e.matmul(
                            t["pz"][:, :], t["Wz"][:, ci * 512 + zc * 128:ci * 512 + (zc + 1) * 128],
                            t["xs"][:, ci * 512:(ci + 1) * 512],
                            start=(ci == 0), stop=(ci == DC - 1)))(zc, ci))
                    s.op("scalar", (lambda zc, sl: lambda e: e.activation(
                        t["Z"][:, zc * TTA + sl * 512: zc * TTA + (sl + 1) * 512],
                        t["pz"][:, :], AF.Copy))(zc, sl))
                for ci in range(DC):
                    s.op("tensor", (lambda ci: lambda e: e.matmul(
                        t["pfa"][0:4, 0:512], t["Wba"][:, ci * 8:ci * 8 + 4],
                        t["xs"][:, ci * 512:(ci + 1) * 512],
                        start=(ci == 0), stop=(ci == DC - 1)))(ci))
                    s.op("tensor", (lambda ci: lambda e: e.matmul(
                        t["pfb"][0:4, 0:512], t["Wba"][:, ci * 8 + 4:ci * 8 + 8],
                        t["xs"][:, ci * 512:(ci + 1) * 512],
                        start=(ci == 0), stop=(ci == DC - 1)))(ci))
                s.op("scalar", (lambda sl: lambda e: e.activation(
                    t["bgate"][:, sl * 512:(sl + 1) * 512], t["pfa"][0:4, 0:512], AF.Sigmoid))(sl))
                s.op("vector", (lambda sl: lambda e: e.tensor_scalar(
                    t["arows"][:, sl * 512:(sl + 1) * 512], t["pfb"][0:4, 0:512],
                    t["dtb"][0:4, 0:1], None, ALU.add))(sl))
            s.op("scalar", lambda e: e.activation(t["sp1"][0:4, :], t["arows"][0:4, :], AF.Abs))
            s.op("scalar", lambda e: e.activation(t["grow"][0:4, :], t["sp1"][0:4, :], AF.Exp, scale=-1.0))
            s.op("scalar", lambda e: e.activation(t["sp1"][0:4, :], t["grow"][0:4, :], AF.Ln, bias=one_c[0:4, :]))
            s.op("vector", lambda e: e.tensor_scalar_max(t["sp2"][0:4, :], t["arows"][0:4, :], 0.0))
            s.op("vector", lambda e: e.tensor_add(t["sp2"][0:4, :], t["sp2"][0:4, :], t["sp1"][0:4, :]))
            s.op("vector", lambda e: e.tensor_scalar(
                t["grow"][:, :], t["sp2"][0:4, :], t["negEA"][0:4, 0:1], None, ALU.mult))
            for b_ in range(BPT):
                s.op("tensor", (lambda b_: lambda e: e.transpose(
                    t["pg"][:, 0:4],
                    t["bgate"][0:4, b_ * 128:(b_ + 1) * 128], t["ident_f32"][0:4, 0:4]))(b_))
                s.op("scalar", (lambda b_: lambda e: e.activation(
                    t["beta_tm"][:, b_ * 4:(b_ + 1) * 4], t["pg"][:, 0:4], AF.Copy))(b_))
                s.op("tensor", (lambda b_: lambda e: e.transpose(
                    t["pg"][:, 128:132], t["grow"][0:4, b_ * 128:(b_ + 1) * 128], t["ident_f32"][0:4, 0:4]))(b_))
                s.op("scalar", (lambda b_: lambda e: e.activation(
                    t["g_tm"][:, b_ * 4:(b_ + 1) * 4], t["pg"][:, 128:132], AF.Copy))(b_))
            s.op("tensor", lambda e: e.matmul(t["pg"][:, 0:32], t["triu"][:, :], t["g_tm"][:, :],
                                              start=True, stop=True))
            s.op("scalar", lambda e: e.activation(t["gc_all"][:, :], t["pg"][:, 0:32], AF.Copy))
            s.op("scalar", lambda e: e.activation(t["egc_all"][:, :], t["pg"][:, 0:32], AF.Exp))
            s.dma("sync", lambda e: e.dma_start(out=t["glast"][0:1, :], in_=t["gc_all"][127:128, :]))
            s.op("tensor", lambda e: e.matmul(t["pg"][:, 64:96], t["ones_f32"][0:1, :], t["glast"][0:1, :],
                                              start=True, stop=True))
            s.op("scalar", lambda e: e.activation(t["gl_all"][:, :], t["pg"][:, 64:96], AF.Copy))
            s.op("scalar", lambda e: e.activation(t["egl_all"][:, :], t["pg"][:, 64:96], AF.Exp))
            s.op("vector", lambda e: e.tensor_sub(t["ekd_all"][:, :], t["gl_all"][:, :], t["gc_all"][:, :]))
            s.op("scalar", lambda e: e.activation(t["ekd_all"][:, :], t["ekd_all"][:, :], AF.Exp))
            for h in range(HPC):
                s.dma("sync", (lambda h: lambda e: e.dma_start(
                    out=t["acc_a"][0:1, :], in_=t["bgate"][h:h + 1, :]))(h))
                for sl in range(2):
                    s.op("tensor", (lambda h, sl: lambda e: e.matmul(
                        t["pg"][:, 0:512],
                        t["ones_f32"][0:1, :], t["acc_a"][0:1, sl * 512:(sl + 1) * 512],
                        start=True, stop=True))(h, sl))
                    s.op("scalar", (lambda h, sl: lambda e: e.activation(
                        t["betaf"][:, h * TTA + sl * 512: h * TTA + (sl + 1) * 512],
                        t["pg"][:, 0:512], AF.Copy))(h, sl))
            for m in range(12):
                mo = m * (TTA + 3)
                s.op("vector", (lambda m, mo: lambda e: e.tensor_copy(
                    t["mixed"][:, mo:mo + 3], t["tail"][:, m * 3:(m + 1) * 3]))(m, mo))
                s.op("vector", (lambda m, mo: lambda e: e.tensor_scalar(
                    t["acc_a"][:, :], t["mixed"][:, mo:mo + TTA], t["convw"][:, m * 4:m * 4 + 1],
                    None, ALU.mult))(m, mo))
                s.op("vector", (lambda m, mo: lambda e: e.scalar_tensor_tensor(
                    t["acc_b"][:, :], t["mixed"][:, mo + 1:mo + 1 + TTA], t["convw"][:, m * 4 + 1:m * 4 + 2],
                    t["acc_a"][:, :], ALU.mult, ALU.add))(m, mo))
                s.op("vector", (lambda m, mo: lambda e: e.scalar_tensor_tensor(
                    t["acc_a"][:, :], t["mixed"][:, mo + 2:mo + 2 + TTA], t["convw"][:, m * 4 + 2:m * 4 + 3],
                    t["acc_b"][:, :], ALU.mult, ALU.add))(m, mo))
                s.op("vector", (lambda m, mo: lambda e: e.scalar_tensor_tensor(
                    t["acc_b"][:, :], t["mixed"][:, mo + 3:mo + 3 + TTA], t["convw"][:, m * 4 + 3:m * 4 + 4],
                    t["acc_a"][:, :], ALU.mult, ALU.add))(m, mo))
                s.op("scalar", (lambda m: lambda e: e.activation(
                    t["Y"][:, m * TTA:(m + 1) * TTA], t["acc_b"][:, :], AF.Silu))(m))
                s.op("vector", (lambda m, mo: lambda e: e.tensor_copy(
                    t["tail"][:, m * 3:(m + 1) * 3], t["mixed"][:, mo + TTA:mo + TTA + 3]))(m, mo))
            for h in range(HPC):
                for which, bias in ((0, qbias), (4, zero_c)):
                    m = which + h
                    s.op("scalar", (lambda m: lambda e: e.activation(
                        t["sqn"][:, :], t["Y"][:, m * TTA:(m + 1) * TTA], AF.Square))(m))
                    for sl in range(2):
                        s.op("tensor", (lambda sl: lambda e: e.matmul(
                            t["pa"][:, :], t["ones_bf"][:, :], t["sqn"][:, sl * 512:(sl + 1) * 512],
                            start=True, stop=True))(sl))
                        s.op("scalar", (lambda sl: lambda e: e.activation(
                            t["rstdn"][:, sl * 512:(sl + 1) * 512], t["pa"][:, :], AF.Ln,
                            bias=eps_l2))(sl))
                    s.op("scalar", (lambda bias: lambda e: e.activation(
                        t["rstdn"][:, :], t["rstdn"][:, :], AF.Exp, scale=-0.5, bias=bias))(bias))
                    s.op("vector", (lambda m: lambda e: e.tensor_mul(
                        t["Y"][:, m * TTA:(m + 1) * TTA], t["Y"][:, m * TTA:(m + 1) * TTA],
                        t["rstdn"][:, :]))(m))
            for h in range(HPC):
                for b_ in range(BPT):
                    for which, dst in ((4, "ktm"), (8, "vtm")):
                        m = which + h
                        s.op("tensor", (lambda m, b_: lambda e: e.transpose(
                            t["pt"][:, 0:128], t["Y"][:, m * TTA + b_ * 128:m * TTA + (b_ + 1) * 128],
                            t["ident_bf"][:, :]))(m, b_))
                        s.op("scalar", (lambda h, b_, dst: lambda e: e.activation(
                            t[dst][:, (h * BPT + b_) * 128:(h * BPT + b_ + 1) * 128],
                            t["pt"][:, 0:128], AF.Copy))(h, b_, dst))
            if dbg and ti == (dbg[0] if isinstance(dbg, tuple) else 0):
                for nm in ["Y", "Z", "mixed", "gc_all", "beta_tm", "g_tm", "egc_all",
                           "ekd_all", "egl_all", "betaf", "ktm", "vtm", "bgate", "grow"]:
                    hnd = t[nm]
                    shp = [hnd.shape[0], hnd.shape[1]]
                    ap = nc.dram_tensor("dbg_" + nm, shp, hnd.dtype, kind="ExternalOutput").ap()
                    s.dma("sync", (lambda ap, hnd: lambda e: e.dma_start(
                        out=ap[:, :], in_=hnd[:, :]))(ap, hnd))
            for b_ in range(BPT):
                for h in range(HPC):
                    emit_delta_chunk(c, ti, b_, h, o_hout)
                    _dsel = dbg if isinstance(dbg, tuple) else (0, dbg if isinstance(dbg, int) else 0)
                    if dbg and (ti, b_, h) == (_dsel[0], _dsel[1], 0):
                        for nm in ["dec", "dec2", "egcf", "kbfm", "Atile", "Q0", "X_a",
                                   "X_b", "uw", "vnew", "qe", "attnT", "vb_tm", "kbe_tm",
                                   "kdec_tm", "houtt", "rstdo", "Sf", "_dbg_po"]:
                            hnd = t[nm]
                            shp = [hnd.shape[0], hnd.shape[1]]
                            ap = nc.dram_tensor("dbg_" + nm, shp, hnd.dtype, kind="ExternalOutput").ap()
                            s.dma("sync", (lambda ap, hnd: lambda e: e.dma_start(
                                out=ap[:, :], in_=hnd[:, :]))(ap, hnd))

        sched.emit(blk, ch)
    return nc


def emit_delta_chunk(c, ti, b_, h, o_hout):
    s, t = c.s, c.t
    col = b_ * 4 + h
    gc_col = t["gc_all"][:, col:col + 1]
    beta_col = t["beta_tm"][:, col:col + 1]
    egc_col = t["egc_all"][:, col:col + 1]
    ekd_col = t["ekd_all"][:, col:col + 1]
    egl_col = t["egl_all"][:, col:col + 1]
    qY = t["Y"][:, (0 + h) * TTA + b_ * 128:(0 + h) * TTA + (b_ + 1) * 128]
    kY = t["Y"][:, (4 + h) * TTA + b_ * 128:(4 + h) * TTA + (b_ + 1) * 128]
    ktm = t["ktm"][:, (h * BPT + b_) * 128:(h * BPT + b_ + 1) * 128]
    vtm = t["vtm"][:, (h * BPT + b_) * 128:(h * BPT + b_ + 1) * 128]
    bf_sl = t["betaf"][:, h * TTA + b_ * 128:h * TTA + (b_ + 1) * 128]
    z_sl = t["Z"][:, h * TTA + b_ * 128:h * TTA + (b_ + 1) * 128]
    par = (ti * BPT + b_) % 2
    S_old = t["Sf"][:, (h * 2 + par) * 128:(h * 2 + par + 1) * 128]
    S_new = t["Sf"][:, (h * 2 + (1 - par)) * 128:(h * 2 + (1 - par) + 1) * 128]
    S_bf = t["Sb"][:, h * 128:(h + 1) * 128]
    chunk_glob = (ti * BPT + b_)

    s.op("vector", lambda e: e.tensor_scalar(
        t["gcb"][:, :], t["ones_f32"][:, :], gc_col, None, ALU.mult))
    s.op("tensor", lambda e: e.transpose(t["pg"][:, 0:128], t["gcb"][:, :], t["ident_f32"][:, :]))
    s.op("vector", lambda e: e.tensor_scalar(
        t["ttmp"][:, :], t["pg"][:, 0:128], gc_col, None, ALU.subtract))
    s.op("vector", lambda e: e.tensor_scalar_min(t["mn"][:, :], t["ttmp"][:, :], 0.0))
    s.op("scalar", lambda e: e.activation(t["dec"][:, :], t["mn"][:, :], AF.Exp))
    s.op("vector", lambda e: e.tensor_scalar_max(t["mn"][:, :], t["ttmp"][:, :], 0.0))
    s.op("scalar", lambda e: e.activation(t["dec2"][:, :], t["mn"][:, :], AF.Exp, scale=-1.0))
    s.op("scalar", lambda e: e.activation(t["egcf"][:, :], t["pg"][:, 0:128], AF.Exp))
    s.op("vector", lambda e: e.tensor_mul(t["kbfm"][:, :], kY, bf_sl))
    s.op("tensor", lambda e: e.matmul(t["pfa"][:, 0:128], kY, t["kbfm"][:, :], start=True, stop=True))
    s.op("vector", lambda e: e.tensor_mul(t["ttmp"][:, :], t["pfa"][:, 0:128], t["dec"][:, :]))
    s.op("vector", lambda e: e.tensor_mul(t["Atile"][:, :], t["ttmp"][:, :], t["mskn"][:, :]))
    s.op("tensor", lambda e: e.matmul(t["pfa"][:, 128:256], t["kbfm"][:, :], kY, start=True, stop=True))
    s.op("vector", lambda e: e.tensor_mul(t["ttmp"][:, :], t["pfa"][:, 128:256], t["dec2"][:, :]))
    s.op("vector", lambda e: e.tensor_mul(t["Q0"][:, :], t["ttmp"][:, :], t["mskn2"][:, :]))
    s.op("vector", lambda e: e.tensor_add(t["X_a"][:, :], t["Atile"][:, :], t["ident_bf"][:, :]))
    P_prev, Q_prev, X_prev = t["Atile"], t["Q0"], t["X_a"]
    for k in range(1, 7):
        PQ = t["PQ_a"] if k % 2 == 1 else t["PQ_b"]
        Xn = t["X_b"] if k % 2 == 1 else t["X_a"]
        s.op("tensor", (lambda Q_prev, P_prev: lambda e: e.matmul(
            t["pfb"][:, 0:128], Q_prev[:, 0:128], P_prev[:, 0:128], start=True, stop=True))(Q_prev, P_prev))
        s.op("tensor", (lambda Q_prev, P_prev: lambda e: e.matmul(
            t["pfb"][:, 128:256], P_prev[:, 0:128], Q_prev[:, 0:128], start=True, stop=True))(Q_prev, P_prev))
        s.op("scalar", (lambda PQ: lambda e: e.activation(PQ[:, :], t["pfb"][:, 0:256], AF.Copy))(PQ))
        s.op("tensor", (lambda PQ, X_prev: lambda e: e.matmul(
            t["pfa"][:, 0:128], PQ[:, 128:256], X_prev[:, 0:128], start=True, stop=True))(PQ, X_prev))
        s.op("vector", (lambda Xn, X_prev: lambda e: e.tensor_add(
            Xn[:, :], t["pfa"][:, 0:128], X_prev[:, 0:128]))(Xn, X_prev))
        P_prev, Q_prev, X_prev = (lambda PQ: (PQ[:, 0:128], PQ[:, 128:256]))(PQ) + (Xn,)
    X_fin = X_prev[:, 0:128]
    s.op("vector", lambda e: e.tensor_scalar(t["vb_tm"][:, :], vtm, beta_col, None, ALU.mult))
    s.op("vector", lambda e: e.tensor_scalar(t["kbe_tm"][:, :], ktm, beta_col, egc_col, ALU.mult, ALU.mult))
    s.op("vector", lambda e: e.tensor_scalar(t["kdec_tm"][:, :], ktm, ekd_col, None, ALU.mult))
    s.op("tensor", (lambda X_fin: lambda e: e.matmul(
        t["pfb"][:, 0:128], X_fin, t["vb_tm"][:, :], start=True, stop=True))(X_fin))
    s.op("tensor", (lambda X_fin: lambda e: e.matmul(
        t["pfb"][:, 128:256], t["kbe_tm"][:, :], X_fin, start=True, stop=True))(X_fin))
    s.op("scalar", lambda e: e.activation(t["uw"][:, :], t["pfb"][:, 0:256], AF.Copy))
    s.op("tensor", lambda e: e.matmul(t["pfa"][:, 0:128], t["uw"][:, 128:256], S_bf, start=True, stop=True))
    s.op("vector", lambda e: e.tensor_sub(t["vnew"][:, :], t["uw"][:, 0:128], t["pfa"][:, 0:128]))
    s.op("vector", lambda e: e.tensor_mul(t["qe"][:, :], qY, t["egcf"][:, :]))
    s.op("tensor", lambda e: e.matmul(t["po"][:, :], S_bf, t["qe"][:, :], start=True, stop=False,
                                      skip_group_check=True))
    s.op("tensor", lambda e: e.matmul(t["pfa"][:, 128:256], kY, qY, start=True, stop=True))
    s.op("vector", lambda e: e.tensor_mul(t["ttmp2"][:, :], t["pfa"][:, 128:256], t["dec"][:, :]))
    s.op("vector", lambda e: e.tensor_mul(t["attnT"][:, :], t["ttmp2"][:, :], t["mski"][:, :]))
    s.op("tensor", lambda e: e.matmul(t["po"][:, :], t["vnew"][:, :], t["attnT"][:, :],
                                      start=False, stop=True, skip_group_check=True))
    if c.t.get("_dbg_po") is not None and (ti, b_, h) == c.t.get("_dbg_sel", (0, 1, 0)):
        s.op("vector", lambda e: e.tensor_copy(c.t["_dbg_po"][:, :], t["po"][:, :]))
    s.op("tensor", lambda e: e.matmul(t["pfb"][:, 0:128], t["kdec_tm"][:, :], t["vnew"][:, :],
                                      start=True, stop=True))
    s.op("vector", (lambda S_new, S_old: lambda e: e.scalar_tensor_tensor(
        S_new, S_old, egl_col, t["pfb"][:, 0:128], ALU.mult, ALU.add))(S_new, S_old))
    s.op("vector", (lambda S_bf, S_new: lambda e: e.tensor_copy(S_bf, S_new))(S_bf, S_new))
    if c.t.get("_dbg_Sdram") is not None and h == 0:
        cg = ti * BPT + b_
        s.dma("sync", (lambda S_new, cg: lambda e: e.dma_start(
            out=c.t["_dbg_Sdram"][:, cg * 128:(cg + 1) * 128], in_=S_new))(S_new, cg))
        s.op("vector", lambda e: e.tensor_copy(t["ttmp"][:, :], t["pfb"][:, 0:128]))
        s.dma("sync", (lambda cg: lambda e: e.dma_start(
            out=c.t["_dbg_Bdram"][:, cg * 128:(cg + 1) * 128], in_=t["ttmp"][:, :]))(cg))
    s.op("scalar", lambda e: e.activation(t["sqo"][:, :], t["po"][:, :], AF.Square))
    s.op("tensor", lambda e: e.matmul(t["pg"][:, 128:256], t["ones_bf"][:, :], t["sqo"][:, :],
                                      start=True, stop=True))
    s.op("scalar", lambda e: e.activation(t["rstdo"][:, :], t["pg"][:, 128:256], AF.Ln,
                                          scale=1.0 / 128, bias=c.cst(1e-6)))
    s.op("scalar", lambda e: e.activation(t["rstdo"][:, :], t["rstdo"][:, :], AF.Exp, scale=-0.5))
    s.op("vector", lambda e: e.tensor_mul(t["ttmp"][:, :], t["po"][:, :], t["rstdo"][:, :]))
    s.op("vector", lambda e: e.tensor_scalar(t["ttmp"][:, :], t["ttmp"][:, :], t["nw"][:, 0:1],
                                             None, ALU.mult))
    s.op("scalar", (lambda z_sl: lambda e: e.activation(t["szt"][:, :], z_sl, AF.Silu))(z_sl))
    s.op("vector", lambda e: e.tensor_mul(t["houtt"][:, :], t["ttmp"][:, :], t["szt"][:, :]))
    s.dma("sync", (lambda h, chunk_glob: lambda e: e.dma_start(
        out=o_hout[h * 128:(h + 1) * 128, chunk_glob * 128:(chunk_glob + 1) * 128],
        in_=t["houtt"][:, :]))(h, chunk_glob))


def emit_headproj_residual(c, hout_dram, w_dram_f32, x_in_dram, x_outs):
    s, t = c.s, c.t
    WB = t["H"]
    pc, pd = t["pc"], t["pd"]
    sWc = c.sem("wc")
    sHT = c.sem("ht")
    sX = c.sem("hx")
    sTS = c.sem("hts")
    sSt = c.sem("hst")
    nHT = nX = nTS = nSt = 0
    nst_per = len(x_outs) * 16
    s.dma("gpsimd", lambda e: e.dma_start(
        out=WB[:, 0:DC * D].rearrange("p (c f) -> p c f", c=DC),
        in_=w_dram_f32.rearrange("(c p) f -> p c f", p=128)),
        after=[], track=(sWc, 16))
    half = DC * 512
    mm_last_sl = [None, None]
    xwar = [0, 0]
    pswar = [0, 0]
    for sl in range(TPC // 512):
        t0 = sl * 512
        par_s = sl % 2
        HT = t["XLN"][:, par_s * half:(par_s + 1) * half]
        aft = [] if mm_last_sl[par_s] is None else [mm_last_sl[par_s]]
        s.dma("sync", (lambda t0, HT: lambda e: e.dma_start(
            out=HT.rearrange("p (c f) -> p c f", c=DC),
            in_=hout_dram.rearrange("(c p) t -> p c t", p=128)[:, :, t0:t0 + 512]))(t0, HT),
            after=aft, track=(sHT, 16))
        nHT += 1
        for di in range(DC):
            par = di % 2
            ps = pc if par == 0 else pd
            for cc in range(DC):
                w = None
                if cc == 0:
                    w = [(sTS, pswar[par])]
                    if di == 0:
                        w += [(sHT, 16 * nHT), (sWc, 16)]
                idx = s.op("tensor", (lambda di, cc, HT, ps: lambda e: e.matmul(
                    ps[:, 0:512], WB[:, cc * D + di * 128:cc * D + (di + 1) * 128],
                    HT[:, cc * 512:(cc + 1) * 512],
                    start=(cc == 0), stop=(cc == DC - 1)))(di, cc, HT, ps),
                    waits=w)
            mm_last_sl[par_s] = idx
            xt2b = t["xt"] if par == 0 else t["xt2"]
            s.dma("sync", (lambda di, t0, xt2b: lambda e: e.dma_start(
                out=xt2b[:, 0:512],
                in_=x_in_dram[di * 128:(di + 1) * 128, t0:t0 + 512]))(di, t0, xt2b),
                after=[], waits=[(sTS, xwar[par])], track=(sX, 16))
            nX += 1
            tmp = t["tmpf"] if par == 0 else t["tmpf2"]
            s.op("vector", (lambda tmp, xt2b, ps: lambda e: e.tensor_add(
                tmp[:, 0:512], ps[:, 0:512], xt2b[:, 0:512]))(tmp, xt2b, ps),
                after=[idx], waits=[(sX, 16 * nX), (sSt, max(0, nSt - nst_per))],
                track=(sTS, 1))
            nTS += 1
            pswar[par] = nTS
            xwar[par] = nTS
            for xo in x_outs:
                s.dma("gpsimd", (lambda di, t0, xo, tmp: lambda e: e.dma_start(
                    out=xo[di * 128:(di + 1) * 128, t0:t0 + 512], in_=tmp[:, 0:512]))(di, t0, xo, tmp),
                    after=[], waits=[(sTS, nTS)], track=(sSt, 16))
                nSt += 16
    jn = s.join("sync", [(sSt, nSt), (sTS, nTS), (sHT, 16 * nHT), (sX, 16 * nX), (sWc, 16)])
    c.xt_last_read = [jn, jn]
    return jn


def build_launch3():
    nc = bass.Bass("TRN2", target_bir_lowering=False, debug=False)
    d_x = nc.dram_tensor("x1", [D, TPC], F32, kind="ExternalInput").ap()
    d_hout = nc.dram_tensor("hout", [D, TPC], BF16, kind="ExternalInput").ap()
    d_wout = nc.dram_tensor("wout", [D, D], F32, kind="ExternalInput").ap()
    d_w1q0 = nc.dram_tensor("w1q0", [D, F_FF], BF16, kind="ExternalInput").ap()
    d_w2q0 = nc.dram_tensor("w2q0", [F_FF, D], BF16, kind="ExternalInput").ap()
    d_b1_0 = nc.dram_tensor("b1_0", [128, FC], F32, kind="ExternalInput").ap()
    d_b2_0 = nc.dram_tensor("b2_0", [128, DC], F32, kind="ExternalInput").ap()
    d_w1_1 = nc.dram_tensor("w1t_1", [D, F_FF], F32, kind="ExternalInput").ap()
    d_w2_1 = nc.dram_tensor("w2t_1", [F_FF, D], F32, kind="ExternalInput").ap()
    d_b1_1 = nc.dram_tensor("b1_1", [128, FC], F32, kind="ExternalInput").ap()
    d_b2_1 = nc.dram_tensor("b2_1", [128, DC], F32, kind="ExternalInput").ap()
    d_ones_bf = nc.dram_tensor("ones_bf", [128, 128], BF16, kind="ExternalInput").ap()
    d_ones_f32 = nc.dram_tensor("ones_f32", [128, 128], F32, kind="ExternalInput").ap()

    o_x = nc.dram_tensor("x3", [D, TPC], F32, kind="ExternalOutput").ap()
    o_xln = nc.dram_tensor("xln3", [D, TPC], BF16, kind="ExternalOutput").ap()
    o_w1q = nc.dram_tensor("w1q1", [D, F_FF], BF16, kind="ExternalOutput").ap()
    o_w2q = nc.dram_tensor("w2q1", [F_FF, D], BF16, kind="ExternalOutput").ap()

    i_x2 = nc.dram_tensor("i_x2", [D, TPC], F32, kind="Internal").ap()
    i_x3a = nc.dram_tensor("i_x3a", [D, TPC], F32, kind="Internal").ap()
    i_x3 = nc.dram_tensor("i_x3", [D, TPC], F32, kind="Internal").ap()
    i_w1q1 = nc.dram_tensor("i_w1q1", [D, F_FF], BF16, kind="Internal").ap()
    i_w2q1 = nc.dram_tensor("i_w2q1", [F_FF, D], BF16, kind="Internal").ap()

    with ExitStack() as stack:
        sched = Sched(nc)
        c = Ctx(nc, sched, stack)
        alloc_common(c)
        alloc_ffn(c)
        alloc_quant(c)
        c.sb("b1t", [128, FC], F32)
        c.sb("halfb2", [128, DC], F32)
        c.sb("b1t1", [128, FC], F32)
        c.sb("halfb21", [128, DC], F32)
        ch = stack.enter_context(nc.semaphore("ch"))
        blk = stack.enter_context(nc.Block())
        s, t = c.s, c.t
        load_consts(c, d_ones_bf, d_ones_f32)
        for dst, src in [("b1t", d_b1_0), ("halfb2", d_b2_0), ("b1t1", d_b1_1), ("halfb21", d_b2_1)]:
            s.dma("sync", (lambda dst, src: lambda e: e.dma_start(out=t[dst][:, :], in_=src[:, :]))(dst, src))
        s.op("scalar", lambda e: e.activation(t["halfb2"][:, :], t["halfb2"][:, :], AF.Copy, scale=0.5))
        s.op("scalar", lambda e: e.activation(t["halfb21"][:, :], t["halfb21"][:, :], AF.Copy, scale=0.5))

        j1 = emit_headproj_residual(c, d_hout, d_wout, d_x, [i_x2])
        j2 = emit_ffn(c, i_x2, d_w1q0, d_w2q0, t["b1t"], t["halfb2"], [i_x3a], x_ready=j1)
        j3 = emit_quant(c, d_w1_1, D, F_FF, "bitquant", i_w1q1, o_w1q, ready=j2)
        j4 = emit_quant(c, d_w2_1, F_FF, D, "bitquant", i_w2q1, o_w2q, ready=j3)
        j5 = emit_ffn(c, i_x3a, i_w1q1, i_w2q1, t["b1t1"], t["halfb21"], [o_x, i_x3],
                      w_ready=j4, x_ready=j2)
        emit_ln_to_dram(c, i_x3, o_xln)
        sched.emit(blk, ch)
    return nc


def build_launch5():
    nc = bass.Bass("TRN2", target_bir_lowering=False, debug=False)
    d_x = nc.dram_tensor("x3", [D, TPC], F32, kind="ExternalInput").ap()
    d_hout = nc.dram_tensor("hout2", [D, TPC], BF16, kind="ExternalInput").ap()
    d_wo = nc.dram_tensor("woT", [D, D], F32, kind="ExternalInput").ap()
    d_w1q1 = nc.dram_tensor("w1q1", [D, F_FF], BF16, kind="ExternalInput").ap()
    d_w2q1 = nc.dram_tensor("w2q1", [F_FF, D], BF16, kind="ExternalInput").ap()
    d_b1_1 = nc.dram_tensor("b1_1", [128, FC], F32, kind="ExternalInput").ap()
    d_b2_1 = nc.dram_tensor("b2_1", [128, DC], F32, kind="ExternalInput").ap()
    d_ones_bf = nc.dram_tensor("ones_bf", [128, 128], BF16, kind="ExternalInput").ap()
    d_ones_f32 = nc.dram_tensor("ones_f32", [128, 128], F32, kind="ExternalInput").ap()
    o_x = nc.dram_tensor("xout", [D, TPC], F32, kind="ExternalOutput").ap()
    i_x4 = nc.dram_tensor("i_x4", [D, TPC], F32, kind="Internal").ap()

    with ExitStack() as stack:
        sched = Sched(nc)
        c = Ctx(nc, sched, stack)
        alloc_common(c)
        alloc_ffn(c)
        c.sb("b1t1", [128, FC], F32)
        c.sb("halfb21", [128, DC], F32)
        ch = stack.enter_context(nc.semaphore("ch"))
        blk = stack.enter_context(nc.Block())
        s, t = c.s, c.t
        load_consts(c, d_ones_bf, d_ones_f32)
        s.dma("sync", lambda e: e.dma_start(out=t["b1t1"][:, :], in_=d_b1_1[:, :]))
        s.dma("sync", lambda e: e.dma_start(out=t["halfb21"][:, :], in_=d_b2_1[:, :]))
        s.op("scalar", lambda e: e.activation(t["halfb21"][:, :], t["halfb21"][:, :], AF.Copy, scale=0.5))
        j1 = emit_headproj_residual(c, d_hout, d_wo, d_x, [i_x4])
        emit_ffn(c, i_x4, d_w1q1, d_w2q1, t["b1t1"], t["halfb21"], [o_x], x_ready=j1)
        sched.emit(blk, ch)
    return nc


TWO_PI = 6.283185307179586
_hi64 = np.float64(TWO_PI)
_2PI_HI = np.float32(np.ldexp(np.round(np.ldexp(_hi64, 10)), -10))
_2PI_MID = np.float32(np.ldexp(np.round(np.ldexp(_hi64 - _2PI_HI, 22)), -22))
_2PI_LO = np.float32(_hi64 - np.float64(_2PI_HI) - np.float64(_2PI_MID))
RMS_EPS = 1.1920929e-07


def build_launch4():
    nc = bass.Bass("TRN2", target_bir_lowering=False, debug=False)
    d_xln = nc.dram_tensor("xln", [D, S_LEN], BF16, kind="ExternalInput").ap()
    d_wqkvT = nc.dram_tensor("wqkvT", [D, 1536], F32, kind="ExternalInput").ap()
    d_iota = nc.dram_tensor("iota", [1, S_LEN], F32, kind="ExternalInput").ap()
    d_dhalf = nc.dram_tensor("dhalf", [128, 1], F32, kind="ExternalInput").ap()
    d_perm = nc.dram_tensor("permT", [128, 128], BF16, kind="ExternalInput").ap()
    d_ones_bf = nc.dram_tensor("ones_bf", [128, 128], BF16, kind="ExternalInput").ap()
    d_ones_f32 = nc.dram_tensor("ones_f32", [128, 128], F32, kind="ExternalInput").ap()
    d_masks = nc.dram_tensor("cmask", [128, 4 * 512], BF16, kind="ExternalInput").ap()
    o_hout = nc.dram_tensor("hout2", [512, S_LEN], BF16, kind="ExternalOutput").ap()

    with ExitStack() as stack:
        sched = Sched(nc)
        c = Ctx(nc, sched, stack)
        s = sched
        c.sb("Wq", [128, DC * 1536], BF16)
        c.sb("sin", [128, S_LEN], BF16)
        c.sb("cos", [128, S_LEN], BF16)
        c.sb("iota", [1, S_LEN], F32)
        c.sb("dhalf", [128, 1], F32)
        c.sb("invf", [128, 1], F32)
        c.sb("perm", [128, 128], BF16)
        c.sb("ones_bf", [128, 128], BF16)
        c.sb("ones_f32", [128, 128], F32)
        c.sb("cmask", [128, 4 * 512], BF16)
        c.sb("xs", [128, DC * 512], BF16)
        c.sb("qf", [128, S_LEN], BF16)
        c.sb("kf", [128, S_LEN], BF16)
        c.sb("vtm", [128, S_LEN], BF16)
        c.sb("ang", [128, 512], F32)
        c.sb("tha", [128, 512], F32)
        c.sb("thb", [128, 512], F32)
        c.sb("ki", [128, 512], I32)
        c.sb("kfl", [128, 512], F32)
        c.sb("t1", [128, 512], F32)
        c.sb("t2", [128, 512], F32)
        c.sb("sqr", [128, 512], BF16)
        c.sb("rstd", [128, 512], F32)
        c.sb("pt", [128, 512], BF16)
        c.sb("linv", [128, 512], F32)
        c.sb("hot", [128, 512], BF16)
        c.ps("pa", [128, 512])
        c.ps("pe", [128, 512])
        c.ps("po", [128, 512])
        c.ps("pl", [128, 512])
        c.ps("pr", [128, 512])
        ch = stack.enter_context(nc.semaphore("ch"))
        blk = stack.enter_context(nc.Block())
        t = c.t

        for nm, src in [("iota", d_iota), ("dhalf", d_dhalf), ("perm", d_perm),
                        ("ones_bf", d_ones_bf), ("ones_f32", d_ones_f32), ("cmask", d_masks)]:
            s.dma("sync", (lambda nm, src: lambda e: e.dma_start(out=t[nm][:, :], in_=src[:, :]))(nm, src))
        s.dma("gpsimd", lambda e: e.dma_start(
            out=t["Wq"][:, :].rearrange("p (c f) -> p c f", c=DC),
            in_=d_wqkvT.rearrange("(c p) f -> p c f", p=128)))
        s.op("scalar", lambda e: e.activation(t["invf"][:, :], t["dhalf"][:, :], AF.Exp,
                                              scale=-float(np.log(10000.0)) / 64.0))
        for sl in range(16):
            t0 = sl * 512
            s.op("tensor", (lambda t0: lambda e: e.matmul(
                t["pa"][:, :], t["ones_f32"][0:1, :], t["iota"][0:1, t0:t0 + 512],
                start=True, stop=True))(t0))
            s.op("vector", lambda e: e.tensor_scalar(t["ang"][:, :], t["pa"][:, :], t["invf"][:, 0:1],
                                                     None, ALU.mult))
            for which, dst in ((0.0, "sin"), (np.pi / 2.0, "cos")):
                if which == 0.0:
                    s.op("vector", lambda e: e.tensor_copy(t["tha"][:, :], t["ang"][:, :]))
                else:
                    s.op("vector", (lambda which: lambda e: e.tensor_scalar(
                        t["tha"][:, :], t["ang"][:, :], which, None, ALU.add))(which))
                s.op("vector", lambda e: e.tensor_scalar(t["thb"][:, :], t["tha"][:, :],
                                                         1.0 / TWO_PI, None, ALU.mult))
                s.op("vector", lambda e: e.tensor_copy(t["ki"][:, :], t["thb"][:, :]))
                s.op("vector", lambda e: e.tensor_copy(t["kfl"][:, :], t["ki"][:, :]))
                s.op("vector", lambda e: e.scalar_tensor_tensor(
                    t["thb"][:, :], t["kfl"][:, :], -float(_2PI_HI), t["tha"][:, :], ALU.mult, ALU.add))
                s.op("vector", lambda e: e.scalar_tensor_tensor(
                    t["tha"][:, :], t["kfl"][:, :], -float(_2PI_MID), t["thb"][:, :], ALU.mult, ALU.add))
                s.op("vector", lambda e: e.scalar_tensor_tensor(
                    t["thb"][:, :], t["kfl"][:, :], -float(_2PI_LO), t["tha"][:, :], ALU.mult, ALU.add))
                s.op("scalar", (lambda dst, t0: lambda e: e.activation(
                    t[dst][:, t0:t0 + 512], t["thb"][:, :], AF.Sin))(dst, t0))

        eps_rms = c.cst(RMS_EPS)
        qb = c.cst(-0.5 * float(np.log(128.0)))
        zb = c.cst(0.0)
        exb = c.cst(-12.0)
        for h in range(HPC):
            for sl in range(16):
                t0 = sl * 512
                s.dma("sync", (lambda t0: lambda e: e.dma_start(
                    out=t["xs"][:, :].rearrange("p (c f) -> p c f", c=DC),
                    in_=d_xln.rearrange("(c p) t -> p c t", p=128)[:, :, t0:t0 + 512]))(t0))
                for which, m in ((0, 0 * 4 + h), (1, 1 * 4 + h)):
                    dst = "qf" if which == 0 else "kf"
                    for ci in range(DC):
                        s.op("tensor", (lambda m, ci: lambda e: e.matmul(
                            t["pa"][:, :], t["Wq"][:, ci * 1536 + m * 128:ci * 1536 + (m + 1) * 128],
                            t["xs"][:, ci * 512:(ci + 1) * 512],
                            start=(ci == 0), stop=(ci == DC - 1)))(m, ci))
                    s.op("scalar", (lambda dst, t0: lambda e: e.activation(
                        t[dst][:, t0:t0 + 512], t["pa"][:, :], AF.Copy))(dst, t0))
                mv = 2 * 4 + h
                for tb in range(4):
                    for ci in range(DC):
                        s.op("tensor", (lambda mv, tb, ci: lambda e: e.matmul(
                            t["po"][:, tb * 128:(tb + 1) * 128],
                            t["xs"][:, ci * 512 + tb * 128:ci * 512 + (tb + 1) * 128],
                            t["Wq"][:, ci * 1536 + mv * 128:ci * 1536 + (mv + 1) * 128],
                            start=(ci == 0), stop=(ci == DC - 1)))(mv, tb, ci))
                s.op("scalar", (lambda t0: lambda e: e.activation(
                    t["vtm"][:, t0:t0 + 512], t["po"][:, :], AF.Copy))(t0))
            for dst, bias in (("qf", qb), ("kf", zb)):
                for sl in range(16):
                    t0 = sl * 512
                    s.op("tensor", (lambda dst, t0: lambda e: e.matmul(
                        t["pr"][:, :], t["perm"][:, :], t[dst][:, t0:t0 + 512],
                        start=True, stop=True))(dst, t0))
                    s.op("vector", (lambda dst, t0: lambda e: e.tensor_mul(
                        t["t1"][:, :], t[dst][:, t0:t0 + 512], t["cos"][:, t0:t0 + 512]))(dst, t0))
                    s.op("vector", (lambda t0: lambda e: e.tensor_mul(
                        t["t2"][:, :], t["pr"][:, :], t["sin"][:, t0:t0 + 512]))(t0))
                    s.op("vector", lambda e: e.tensor_add(t["t1"][:, :], t["t1"][:, :], t["t2"][:, :]))
                    s.op("scalar", lambda e: e.activation(t["sqr"][:, :], t["t1"][:, :], AF.Square))
                    s.op("tensor", lambda e: e.matmul(t["pa"][:, :], t["ones_bf"][:, :], t["sqr"][:, :],
                                                      start=True, stop=True))
                    s.op("scalar", (lambda: lambda e: e.activation(
                        t["rstd"][:, :], t["pa"][:, :], AF.Ln, scale=1.0 / 128, bias=eps_rms))())
                    s.op("scalar", (lambda bias: lambda e: e.activation(
                        t["rstd"][:, :], t["rstd"][:, :], AF.Exp, scale=-0.5, bias=bias))(bias))
                    s.op("vector", (lambda dst, t0: lambda e: e.tensor_mul(
                        t[dst][:, t0:t0 + 512], t["t1"][:, :], t["rstd"][:, :]))(dst, t0))
            for qt in range(16):
                q0 = qt * 512
                nkb = (q0 + 512) // 128
                for kb in range(nkb):
                    s.op("tensor", (lambda kb, q0: lambda e: e.matmul(
                        t["pe"][:, :], t["kf"][:, kb * 128:(kb + 1) * 128], t["qf"][:, q0:q0 + 512],
                        start=True, stop=True))(kb, q0))
                    s.op("scalar", (lambda: lambda e: e.activation(
                        t["pt"][:, :], t["pe"][:, :], AF.Exp, bias=exb))())
                    doff = kb * 128 - q0
                    if doff > -128:
                        mi = doff // 128
                        assert 0 <= mi < 4
                        s.op("vector", (lambda mi: lambda e: e.tensor_mul(
                            t["pt"][:, :], t["pt"][:, :], t["cmask"][:, mi * 512:(mi + 1) * 512]))(mi))
                    s.op("tensor", (lambda kb: lambda e: e.matmul(
                        t["po"][:, :], t["vtm"][:, kb * 128:(kb + 1) * 128], t["pt"][:, :],
                        start=(kb == 0), stop=(kb == nkb - 1), skip_group_check=True))(kb))
                    s.op("tensor", (lambda kb: lambda e: e.matmul(
                        t["pl"][:, :], t["ones_bf"][:, :], t["pt"][:, :],
                        start=(kb == 0), stop=(kb == nkb - 1), skip_group_check=True))(kb))
                s.op("vector", lambda e: e.reciprocal(t["linv"][:, :], t["pl"][:, :]))
                s.op("vector", lambda e: e.tensor_mul(t["hot"][:, :], t["po"][:, :], t["linv"][:, :]))
                s.dma("sync", (lambda h, q0: lambda e: e.dma_start(
                    out=o_hout[h * 128:(h + 1) * 128, q0:q0 + 512], in_=t["hot"][:, :]))(h, q0))
        sched.emit(blk, ch)
    return nc


def _consts():
    return {
        "ones_bf": np.ones((128, 128), dtype=BF),
        "ones_f32": np.ones((128, 128), dtype=np.float32),
    }


def _consts2():
    one = np.ones((128, 128), np.float32)
    return {
        **_consts(),
        "ident_bf": np.eye(128, dtype=BF),
        "ident_f32": np.eye(128, dtype=np.float32),
        "triu": np.triu(one).astype(np.float32),
        "mski": np.triu(one).astype(BF),
        "mskn": (-np.triu(one, 1)).astype(BF),
        "mskn2": (-np.tril(one, -1)).astype(BF),
    }


def _core_heads(cid):
    return cid // 4, [4 * (cid % 4) + j for j in range(4)]


def _consts4():
    masks = np.zeros((128, 4 * 512), dtype=BF)
    for mi in range(4):
        kk = np.arange(128)[:, None] + mi * 128
        qq = np.arange(512)[None, :]
        masks[:, mi * 512:(mi + 1) * 512] = (kk <= qq).astype(BF)
    perm = np.zeros((128, 128), np.float32)
    for i in range(64):
        perm[2 * i + 1, 2 * i] = -1.0
        perm[2 * i, 2 * i + 1] = 1.0
    return {
        "cmask": masks,
        "permT": perm.astype(BF),
        "iota": np.arange(S_LEN, dtype=np.float32).reshape(1, S_LEN),
        "dhalf": (np.arange(128) // 2).astype(np.float32).reshape(128, 1),
        **_consts(),
    }


def host_prep_l4(cid, xln_b, W_qkv):
    b, hs = _core_heads(cid)
    rows = ([W_qkv[h * 128:(h + 1) * 128] for h in hs]
            + [W_qkv[2048 + h * 128:2048 + (h + 1) * 128] for h in hs]
            + [W_qkv[4096 + h * 128:4096 + (h + 1) * 128] for h in hs])
    sel = np.concatenate(rows, axis=0)
    return {
        "xln": np.ascontiguousarray(xln_b),
        "wqkvT": np.ascontiguousarray(sel.T.astype(np.float32)),
        **_consts4(),
    }


def host_prep_l2(cid, xln_b, W_qkvz, W_ba, conv_w, dt_bias, A_log, norm_w):
    b, hs = _core_heads(cid)
    wmix = np.concatenate(
        [W_qkvz[:, h * 512 + off:h * 512 + off + 128] for off in (0, 128, 256) for h in hs],
        axis=1)
    wz = np.concatenate([W_qkvz[:, h * 512 + 384:h * 512 + 512] for h in hs], axis=1)
    wba = np.stack([W_ba[:, 2 * h] for h in hs] + [W_ba[:, 2 * h + 1] for h in hs], axis=1)
    rows = ([conv_w[h * 128:(h + 1) * 128] for h in hs]
            + [conv_w[2048 + h * 128:2048 + (h + 1) * 128] for h in hs]
            + [conv_w[4096 + h * 128:4096 + (h + 1) * 128] for h in hs])
    conv_sel = np.concatenate(rows, axis=0)
    convt = np.ascontiguousarray(conv_sel.reshape(12, 128, 4).transpose(1, 0, 2).reshape(128, 48))
    return {
        "xln": np.ascontiguousarray(xln_b),
        "wmix": np.ascontiguousarray(wmix.astype(np.float32)),
        "wz": np.ascontiguousarray(wz.astype(np.float32)),
        "wba": np.ascontiguousarray(wba.astype(np.float32)),
        "convw": convt.astype(np.float32),
        "dtb": dt_bias[hs].reshape(4, 1).astype(np.float32),
        "alog": A_log[hs].reshape(4, 1).astype(np.float32),
        "nw": norm_w.reshape(128, 1).astype(np.float32),
        **_consts2(),
    }


def _b_tile(b, nchunk):
    return np.ascontiguousarray(b.reshape(nchunk, 128).T.astype(np.float32))


def kernel(L0_attn_hidden_states, L0_attn_W_qkvz, L0_attn_W_ba, L0_attn_conv_weight,
           L0_attn_dt_bias, L0_attn_A_log, L0_attn_norm_weight, L0_attn_W_out,
           L0_ffn_w1, L0_ffn_b1, L0_ffn_w2, L0_ffn_b2,
           L1_attn_W_qkv, L1_attn_W_o, L1_ffn_w1, L1_ffn_b1, L1_ffn_w2, L1_ffn_b2):
    f32 = np.float32
    x0T = np.ascontiguousarray(np.asarray(L0_attn_hidden_states, f32).reshape(TOK, D).T)
    consts = _consts()

    nc1 = build_ffn_launch("absmean")
    w1T0 = np.ascontiguousarray(np.asarray(L0_ffn_w1, f32).T)
    w2T0 = np.ascontiguousarray(np.asarray(L0_ffn_w2, f32).T)
    b1_0 = _b_tile(np.asarray(L0_ffn_b1, f32), FC)
    b2_0 = _b_tile(np.asarray(L0_ffn_b2, f32), DC)
    im1 = [{"x0": np.ascontiguousarray(x0T[:, c * TPC:(c + 1) * TPC]),
            "w1t": w1T0, "w2t": w2T0, "b1": b1_0, "b2": b2_0, **consts}
           for c in range(NCORES)]
    r1 = run_bass_kernel_spmd(nc1, im1, core_ids=list(range(NCORES))).results
    x1 = [r1[c]["x1"] for c in range(NCORES)]
    xln1 = [r1[c]["xln1"] for c in range(NCORES)]
    w1q0, w2q0 = r1[0]["w1q"], r1[0]["w2q"]

    nc2 = build_launch2()
    xln1_b = {b: np.concatenate([xln1[4 * b + j] for j in range(4)], axis=1) for b in range(2)}
    im2 = [host_prep_l2(c, xln1_b[c // 4], np.asarray(L0_attn_W_qkvz, f32),
                        np.asarray(L0_attn_W_ba, f32), np.asarray(L0_attn_conv_weight, f32),
                        np.asarray(L0_attn_dt_bias, f32), np.asarray(L0_attn_A_log, f32),
                        np.asarray(L0_attn_norm_weight, f32))
           for c in range(NCORES)]
    r2 = run_bass_kernel_spmd(nc2, im2, core_ids=list(range(NCORES))).results
    hout1 = [r2[c]["hout"] for c in range(NCORES)]

    nc3 = build_launch3()
    w1T1 = np.ascontiguousarray(np.asarray(L1_ffn_w1, f32).T)
    w2T1 = np.ascontiguousarray(np.asarray(L1_ffn_w2, f32).T)
    b1_1 = _b_tile(np.asarray(L1_ffn_b1, f32), FC)
    b2_1 = _b_tile(np.asarray(L1_ffn_b2, f32), DC)
    im3 = []
    for c in range(NCORES):
        b = c // 4
        tok0 = (c % 4) * TPC
        hout_my = np.ascontiguousarray(np.concatenate(
            [hout1[4 * b + j][:, tok0:tok0 + TPC] for j in range(4)], axis=0))
        im3.append({"x1": x1[c], "hout": hout_my,
                    "wout": np.ascontiguousarray(np.asarray(L0_attn_W_out, f32)),
                    "w1q0": w1q0, "w2q0": w2q0, "b1_0": b1_0, "b2_0": b2_0,
                    "w1t_1": w1T1, "w2t_1": w2T1, "b1_1": b1_1, "b2_1": b2_1, **consts})
    r3 = run_bass_kernel_spmd(nc3, im3, core_ids=list(range(NCORES))).results
    x3 = [r3[c]["x3"] for c in range(NCORES)]
    xln3 = [r3[c]["xln3"] for c in range(NCORES)]
    w1q1, w2q1 = r3[0]["w1q1"], r3[0]["w2q1"]

    nc4 = build_launch4()
    xln3_b = {b: np.concatenate([xln3[4 * b + j] for j in range(4)], axis=1) for b in range(2)}
    im4 = [host_prep_l4(c, xln3_b[c // 4], np.asarray(L1_attn_W_qkv, f32)) for c in range(NCORES)]
    r4 = run_bass_kernel_spmd(nc4, im4, core_ids=list(range(NCORES))).results
    hout2 = [r4[c]["hout2"] for c in range(NCORES)]

    nc5 = build_launch5()
    woT = np.ascontiguousarray(np.asarray(L1_attn_W_o, f32).T)
    im5 = []
    for c in range(NCORES):
        b = c // 4
        tok0 = (c % 4) * TPC
        hout_my = np.ascontiguousarray(np.concatenate(
            [hout2[4 * b + j][:, tok0:tok0 + TPC] for j in range(4)], axis=0))
        im5.append({"x3": x3[c], "hout2": hout_my, "woT": woT,
                    "w1q1": w1q1, "w2q1": w2q1, "b1_1": b1_1, "b2_1": b2_1, **consts})
    r5 = run_bass_kernel_spmd(nc5, im5, core_ids=list(range(NCORES))).results
    out = np.concatenate([r5[c]["xout"] for c in range(NCORES)], axis=1)
    return np.ascontiguousarray(out.T).reshape(BATCH, S_LEN, D).astype(np.float32)


# version 05 (final)
# speedup vs baseline: 1.7157x; 1.0935x over previous
import numpy as np
import ml_dtypes
from contextlib import ExitStack

import concourse.bass as bass
import concourse.mybir as mybir
from concourse.bass_utils import run_bass_kernel_spmd

F32 = mybir.dt.float32
BF16 = mybir.dt.bfloat16
I32 = mybir.dt.int32
AF = mybir.ActivationFunctionType
ALU = mybir.AluOpType
AX = mybir.AxisListType

D = 2048
S_LEN = 8192
BATCH = 2
TOK = BATCH * S_LEN
NCORES = 8
TPC = TOK // NCORES
DC = D // 128
F_FF = 8192
FC = F_FF // 128
TT = 1024
NTT = TPC // TT
HPC = 4
CH = 128
NCH = S_LEN // CH

BF = ml_dtypes.bfloat16


class Sched:

    def __init__(self, nc):
        self.nc = nc
        self.steps = []
        self.track_totals = {}

    def _add(self, eng, fn, kind, after, waits, track):
        if after is not None:
            assert track is not None or kind == "raw", (
                "a chain-relaxed (after=) step must inc a track sem, not the chain")
        if track is not None:
            sem, amt = track
            self.track_totals[sem] = self.track_totals.get(sem, 0) + amt
        self.steps.append((eng, fn, kind, after, waits or [], track))
        return len(self.steps) - 1

    def dma(self, eng, fn, after=None, waits=None, track=None):
        return self._add(eng, fn, "dma", after, waits, track)

    def op(self, eng, fn, after=None, waits=None, track=None):
        return self._add(eng, fn, "compute", after, waits, track)

    def raw(self, eng, fn, after=None, waits=None):
        return self._add(eng, fn, "raw", after, waits, None)

    def join(self, eng, waits):
        def fn(e):
            return e.nop()
        return self._add(eng, fn, "compute", None, waits, None)

    def emit(self, blk, ch):
        steps = self.steps
        n = len(steps)
        solo = {"vector", "scalar", "gpsimd"}
        group_id = [0] * n
        g = 0
        for i in range(1, n):
            prev = steps[i - 1]
            cur = steps[i]
            if (cur[0] != prev[0]
                    or (cur[0] in solo and cur[2] == "compute")
                    or cur[3] is not None or cur[4]
                    or cur[5] is not None or prev[5] is not None):
                g += 1
            group_id[i] = g
        ngroups = g + 1
        inc_of = [0] * n
        last_in_group = {}
        for i in range(n):
            last_in_group[group_id[i]] = i
        for i in range(n):
            if steps[i][5] is None and steps[i][2] == "dma":
                inc_of[i] = 16
        for gid, i in last_in_group.items():
            if steps[i][5] is None and steps[i][2] == "compute":
                inc_of[i] = 1
        cum = 0
        before = [0] * n
        after_val = [0] * n
        for i in range(n):
            before[i] = cum
            cum += inc_of[i]
            after_val[i] = cum
        total = cum
        group_wait = [None] * ngroups
        seen = -1
        for i in range(n):
            if group_id[i] != seen:
                seen = group_id[i]
                if steps[i][3] is not None:
                    vals = [after_val[j] for j in steps[i][3]]
                    group_wait[seen] = max(vals) if vals else None
                else:
                    group_wait[seen] = before[i] if before[i] > 0 else None
        last_set = set(last_in_group.values())
        final_waits = list(self.track_totals.items())

        def section(name):
            drain_engines = ("vector", "scalar", "gpsimd")

            def body(e):
                prev_gid = -1
                for i in range(n):
                    eng, fn, kind, after, waits, track = steps[i]
                    if eng != name:
                        continue
                    gid = group_id[i]
                    if gid != prev_gid:
                        w = group_wait[gid]
                        if w is not None:
                            e.wait_ge(ch, w)
                        prev_gid = gid
                        for sem, val in waits:
                            if val > 0:
                                e.wait_ge(sem, val)
                    inst = fn(e)
                    if track is not None:
                        sem, amt = track
                        if (kind == "compute" and name in drain_engines):
                            e.drain().then_inc(sem, amt)
                        else:
                            inst.then_inc(sem, amt)
                    elif inc_of[i]:
                        if (kind == "compute" and name in drain_engines
                                and i in last_set):
                            e.drain().then_inc(ch, inc_of[i])
                        else:
                            inst.then_inc(ch, inc_of[i])
                if name == "sync":
                    e.wait_ge(ch, total)
                    for sem, val in final_waits:
                        e.wait_ge(sem, val)
            return body

        blk.sync(section("sync"))
        blk.tensor(section("tensor"))
        blk.vector(section("vector"))
        blk.scalar(section("scalar"))
        blk.gpsimd(section("gpsimd"))


class Ctx:
    def __init__(self, nc, sched, stack):
        self.nc = nc
        self.s = sched
        self.stack = stack
        self.t = {}
        self.cmap = {}
        self._uid = 0

    def sem(self, name):
        self._uid += 1
        return self.stack.enter_context(self.nc.semaphore(f"tk_{name}{self._uid}"))

    def cst(self, value):
        if "cpool" not in self.t:
            self.sb("cpool", [128, 32], F32)
        if value not in self.cmap:
            col = len(self.cmap)
            assert col < 32
            self.cmap[value] = col
            self.s.op("vector", (lambda col=col, value=value: (
                lambda e: e.memset(self.t["cpool"][:, col:col + 1], value)))())
        col = self.cmap[value]
        return self.t["cpool"][:, col:col + 1]

    def sb(self, name, shape, dt):
        h = self.stack.enter_context(self.nc.sbuf_tensor("s_" + name, list(shape), dt))
        self.t[name] = h
        return h

    def ps(self, name, shape, dt=F32):
        h = self.stack.enter_context(self.nc.psum_tensor("p_" + name, list(shape), dt))
        self.t[name] = h
        return h



def emit_ln(c, x_dram, xln_sbuf, tok0, ntok, x_dram_off=0, x_ready=None):
    s, t = c.s, c.t
    pa, pb = t["pa"], t["pb"]
    ones = t["ones_bf"]
    col = x_dram_off + tok0
    sLX = c.sem("lnx")
    nsl = ntok // 512
    if not hasattr(c, "xt_last_read"):
        c.xt_last_read = [None, None]
    last_read = c.xt_last_read
    nld = 0

    def xtile(ci):
        return t["xt"] if ci % 2 == 0 else t["xt2"]

    def dep(ci):
        a = [] if last_read[ci % 2] is None else [last_read[ci % 2]]
        if x_ready is not None:
            a.append(x_ready)
        return a

    for ci in range(DC):
        aft = dep(ci)
        s.dma("sync", (lambda ci: lambda e: e.dma_start(
            out=xtile(ci)[:, :ntok], in_=x_dram[ci * 128:(ci + 1) * 128, col:col + ntok]))(ci),
            after=aft, track=(sLX, 16))
        nld += 1
        s.op("scalar", (lambda ci: lambda e: e.activation(
            t["xbt"][:, :ntok], xtile(ci)[:, :ntok], AF.Copy))(ci),
            waits=[(sLX, 16 * nld)])
        last_read[ci % 2] = s.op("scalar", (lambda ci: lambda e: e.activation(
            t["sqt"][:, :ntok], xtile(ci)[:, :ntok], AF.Square))(ci))
        for sl in range(nsl):
            s.op("tensor", (lambda ci, sl: lambda e: e.matmul(
                pa[:, sl * 512:(sl + 1) * 512], ones[:, :], t["xbt"][:, sl * 512:(sl + 1) * 512],
                start=(ci == 0), stop=(ci == DC - 1), skip_group_check=True))(ci, sl))
            s.op("tensor", (lambda ci, sl: lambda e: e.matmul(
                pb[:, sl * 512:(sl + 1) * 512], ones[:, :], t["sqt"][:, sl * 512:(sl + 1) * 512],
                start=(ci == 0), stop=(ci == DC - 1), skip_group_check=True))(ci, sl))
    s.op("scalar", lambda e: e.activation(t["mb"][:, :ntok], pa[:, :ntok], AF.Copy, scale=1.0 / D))
    s.op("scalar", lambda e: e.activation(t["vb"][:, :ntok], pb[:, :ntok], AF.Copy, scale=1.0 / D))
    s.op("scalar", lambda e: e.activation(pa[:, :ntok], t["mb"][:, :ntok], AF.Square))
    s.op("vector", lambda e: e.tensor_sub(t["vb"][:, :ntok], t["vb"][:, :ntok], pa[:, :ntok]))
    eps_ap = c.cst(1e-5)
    s.op("scalar", lambda e: e.activation(t["rb"][:, :ntok], t["vb"][:, :ntok], AF.Sqrt, bias=eps_ap))
    s.op("vector", lambda e: e.reciprocal(t["rb"][:, :ntok], t["rb"][:, :ntok]))
    for ci in range(DC):
        aft = dep(ci)
        s.dma("sync", (lambda ci: lambda e: e.dma_start(
            out=xtile(ci)[:, :ntok], in_=x_dram[ci * 128:(ci + 1) * 128, col:col + ntok]))(ci),
            after=aft, track=(sLX, 16))
        nld += 1
        last_read[ci % 2] = s.op("scalar", (lambda ci: lambda e: e.activation(
            t["xbt"][:, :ntok], xtile(ci)[:, :ntok], AF.Copy))(ci),
            waits=[(sLX, 16 * nld)])
        s.op("vector", lambda e: e.tensor_sub(t["tmpf"][:, :ntok], t["xbt"][:, :ntok], t["mb"][:, :ntok]))
        s.op("vector", (lambda ci: lambda e: e.tensor_mul(
            xln_sbuf[:, ci * ntok:(ci + 1) * ntok], t["tmpf"][:, :ntok], t["rb"][:, :ntok]))(ci))


QCOL = 1024


def emit_quant(c, w_dram, rows, cols, kind, wq_internal, wq_out, ready=None):
    s, t = c.s, c.t
    nrc = rows // 128
    ncb = cols // QCOL
    nblk = nrc * ncb
    assert nblk <= 128
    sQW = c.sem("qw")
    sQS = c.sem("qs")
    sRd = c.sem("qrd")
    sAS = c.sem("qas")
    sGe = c.sem("qge")
    sMu = c.sem("qmu")
    st_per = 32 if wq_out is not None else 16
    nld = 0
    nst = 0
    rdy = [] if ready is None else [ready]

    def wtile(j):
        return t["wrow"] if j % 2 == 0 else t["wrow2"]

    def sgn(j):
        return t["sgn"] if j % 2 == 0 else t["sgn2"]

    def absw(j):
        return t["absw"] if j % 2 == 0 else t["absw2"]

    def wqt(j):
        return t["wqt"] if j % 2 == 0 else t["wqt2"]

    for r in range(nrc):
        for cb in range(ncb):
            j = r * ncb + cb
            war = [(sRd, j - 1)] if j >= 2 else None
            s.dma("sync", (lambda r, cb, j: lambda e: e.dma_start(
                out=wtile(j)[:, :], in_=w_dram[r * 128:(r + 1) * 128, cb * QCOL:(cb + 1) * QCOL]))(r, cb, j),
                after=rdy, waits=war, track=(sQW, 16))
            nld += 1
            s.op("vector", (lambda j: lambda e: e.reduce_sum(
                t["parts"][:, 2 + j:3 + j], wtile(j)[:, :], AX.X, apply_absolute_value=True))(j),
                after=rdy, waits=[(sQW, 16 * nld)], track=(sRd, 1))
    s.join("sync", [(sRd, nblk)])
    s.op("vector", (lambda nblk: lambda e: e.reduce_sum(
        t["parts"][:, 0:1], t["parts"][:, 2:2 + nblk], AX.X))(nblk))
    s.op("tensor", lambda e: e.matmul(t["pc"][:, 0:1], t["ones_f32"][:, :], t["parts"][:, 0:1],
                                      start=True, stop=True))
    if kind == "absmean":
        jthr = s.op("scalar", lambda e: e.activation(t["thr"][:, 0:1], t["pc"][:, 0:1], AF.Copy,
                                                     scale=0.7 / (rows * cols)))
    else:
        s.op("scalar", lambda e: e.activation(t["thr"][:, 0:1], t["pc"][:, 0:1], AF.Copy,
                                              scale=1.0 / (rows * cols)))
        jthr = s.op("vector", lambda e: e.tensor_scalar(t["thr"][:, 0:1], t["thr"][:, 0:1], 1e-8, 0.5,
                                                        ALU.max, ALU.mult))
    for r in range(nrc):
        for cb in range(ncb):
            j = r * ncb + cb
            cs_ = slice(cb * QCOL, (cb + 1) * QCOL)
            war = [(sRd, nblk)] if j < 2 else [(sAS, 2 * (j - 1))]
            s.dma("sync", (lambda r, cs_, j: lambda e: e.dma_start(
                out=wtile(j)[:, :], in_=w_dram[r * 128:(r + 1) * 128, cs_]))(r, cs_, j),
                after=rdy, waits=war, track=(sQW, 16))
            nld += 1
            s.op("scalar", (lambda j: lambda e: e.activation(
                absw(j)[:, :], wtile(j)[:, :], AF.Abs))(j),
                after=[jthr], waits=[(sQW, 16 * nld), (sMu, max(0, j - 1))], track=(sAS, 1))
            s.op("scalar", (lambda j: lambda e: e.activation(
                sgn(j)[:, :], wtile(j)[:, :], AF.Sign))(j),
                after=[jthr], track=(sAS, 1))
            s.op("vector", (lambda j: lambda e: e.tensor_scalar(
                absw(j)[:, :], absw(j)[:, :], t["thr"][:, 0:1], None, ALU.is_ge))(j),
                after=[jthr], waits=[(sAS, 2 * j + 1)], track=(sGe, 1))
            s.op("vector", (lambda j: lambda e: e.tensor_mul(
                wqt(j)[:, :], sgn(j)[:, :], absw(j)[:, :]))(j),
                after=[jthr],
                waits=[(sAS, 2 * j + 2), (sGe, j + 1), (sQS, max(0, st_per * (j - 1)))],
                track=(sMu, 1))
            s.dma("gpsimd", (lambda r, cs_, j: lambda e: e.dma_start(
                out=wq_internal[r * 128:(r + 1) * 128, cs_], in_=wqt(j)[:, :]))(r, cs_, j),
                after=[], waits=[(sMu, j + 1)], track=(sQS, 16))
            if wq_out is not None:
                s.dma("gpsimd", (lambda r, cs_, j: lambda e: e.dma_start(
                    out=wq_out[r * 128:(r + 1) * 128, cs_], in_=wqt(j)[:, :]))(r, cs_, j),
                    after=[], track=(sQS, 16))
            nst += 1
    jn = s.join("sync", [(sQS, nst * st_per), (sQW, 16 * nld), (sRd, nblk),
                         (sAS, 2 * nblk), (sGe, nblk), (sMu, nblk)])
    c.xt_last_read = [jn, jn]
    return jn


def emit_ffn(c, x_in_dram, w1q, w2q, b1t, halfb2, x_outs, x_in_off=0,
             w_ready=None, x_ready=None):
    s, t = c.s, c.t
    XLN, H = t["XLN"], t["H"]
    pc, pd = t["pc"], t["pd"]
    w1v = w1q.rearrange("(c p) f -> p c f", p=128)
    w2v = w2q.rearrange("(c p) f -> p c f", p=128)
    sW1 = c.sem("w1")
    sW2 = c.sem("w2")
    sGel = c.sem("gel")
    sX = c.sem("ffx")
    sTS = c.sem("ffts")
    sSt = c.sem("ffst")
    nW1 = nW2 = nGel = nX = nTS = nSt = 0
    nst_per = len(x_outs) * 16
    wdep = [] if w_ready is None else [w_ready]
    join = None
    mmh_idx = [None, None]
    xwar = [0, 0]
    for ti in range(NTT):
        tok0 = ti * TT
        emit_ln(c, x_in_dram, XLN, tok0, TT, x_dram_off=x_in_off, x_ready=x_ready)
        mm_last = [None, None]
        for fi in range(FC):
            par = fi % 2
            w1t = t["w1t"] if par == 0 else t["w1t2"]
            ps_a = pc if par == 0 else pd
            aft = list(wdep) if mm_last[par] is None else [mm_last[par]]
            s.dma("sync", (lambda fi, w1t: lambda e: e.dma_start(
                out=w1t[:, :].rearrange("p (c f) -> p c f", c=DC),
                in_=w1v[:, :, fi * 128:(fi + 1) * 128]))(fi, w1t),
                after=aft, track=(sW1, 16))
            nW1 += 1
            for ci in range(DC):
                for sl in range(TT // 512):
                    w = [(sW1, 16 * nW1), (sGel, max(0, nGel - 1))] if ci == 0 and sl == 0 else None
                    idx = s.op("tensor", (lambda fi, ci, sl, w1t, ps_a: lambda e: e.matmul(
                        ps_a[:, sl * 512:(sl + 1) * 512], w1t[:, ci * 128:(ci + 1) * 128],
                        XLN[:, ci * TT + sl * 512:ci * TT + (sl + 1) * 512],
                        start=(ci == 0), stop=(ci == DC - 1)))(fi, ci, sl, w1t, ps_a),
                        waits=w)
            mm_last[par] = idx
            s.op("scalar", (lambda fi, ps_a: lambda e: e.activation(
                H[:, fi * TT:(fi + 1) * TT], ps_a[:, :TT], AF.Gelu, bias=b1t[:, fi:fi + 1]))(fi, ps_a),
                after=[idx], track=(sGel, 1))
            nGel += 1
        NQ = 4
        FQ = FC // NQ
        for di in range(DC):
            par = di % 2
            ps_b = pc if par == 0 else pd
            for q in range(NQ):
                qb = q % 2
                wt = (t["w2ta"], t["w2tb"])[qb]
                aft = list(wdep) if mmh_idx[qb] is None else [mmh_idx[qb]]
                s.dma("sync", (lambda di, q, wt: lambda e: e.dma_start(
                    out=wt[:, :].rearrange("p (c f) -> p c f", c=FQ),
                    in_=w2v[:, q * FQ:(q + 1) * FQ, di * 128:(di + 1) * 128]))(di, q, wt),
                    after=aft, track=(sW2, 16))
                nW2 += 1
                for fi2 in range(FQ):
                    fi = q * FQ + fi2
                    w = None
                    if fi2 == 0:
                        w = [(sW2, 16 * nW2)]
                        if q == 0:
                            w.append((sGel, FC * (ti + 1)))
                            w.append((sTS, max(0, nTS - 1)))
                    for sl in range(TT // 512):
                        idx = s.op("tensor", (lambda di, fi, sl, wt, fi2, ps_b: lambda e: e.matmul(
                            ps_b[:, sl * 512:(sl + 1) * 512], wt[:, fi2 * 128:(fi2 + 1) * 128],
                            H[:, fi * TT + sl * 512:fi * TT + (sl + 1) * 512],
                            start=(fi == 0), stop=(fi == FC - 1)))(di, fi, sl, wt, fi2, ps_b),
                            waits=w if sl == 0 else None)
                        w = None
                mmh_idx[qb] = idx
            mmB_end = idx
            xt2b = t["xt"] if par == 0 else t["xt2"]
            aftx = [] if x_ready is None else [x_ready]
            if c.xt_last_read[par] is not None:
                aftx = aftx + [c.xt_last_read[par]]
            s.dma("sync", (lambda di, tok0, xt2b: lambda e: e.dma_start(
                out=xt2b[:, :TT],
                in_=x_in_dram[di * 128:(di + 1) * 128, x_in_off + tok0:x_in_off + tok0 + TT]))(di, tok0, xt2b),
                after=aftx, waits=[(sTS, xwar[par])], track=(sX, 16))
            nX += 1
            c.xt_last_read[par] = None
            tmp = t["tmpf"] if par == 0 else t["tmpf2"]
            s.op("vector", (lambda di, tmp, ps_b: lambda e: e.tensor_scalar(
                tmp[:, :TT], ps_b[:, :TT], 0.5, halfb2[:, di:di + 1], ALU.mult, ALU.add))(di, tmp, ps_b),
                after=[mmB_end], waits=[(sSt, max(0, nSt - nst_per))], track=(sTS, 1))
            nTS += 1
            s.op("vector", (lambda tmp, xt2b: lambda e: e.tensor_add(
                tmp[:, :TT], tmp[:, :TT], xt2b[:, :TT]))(tmp, xt2b),
                after=[], waits=[(sTS, nTS), (sX, 16 * nX)], track=(sTS, 1))
            nTS += 1
            xwar[par] = nTS
            for xo in x_outs:
                s.dma("gpsimd", (lambda di, tok0, xo, tmp: lambda e: e.dma_start(
                    out=xo[di * 128:(di + 1) * 128, tok0:tok0 + TT],
                    in_=tmp[:, :TT]))(di, tok0, xo, tmp),
                    after=[], waits=[(sTS, nTS)], track=(sSt, 16))
                nSt += 16
        join = s.join("sync", [(sSt, nSt), (sTS, nTS), (sW1, 16 * nW1), (sW2, 16 * nW2),
                               (sX, 16 * nX), (sGel, nGel)])
        c.xt_last_read = [join, join]
        xwar = [nTS, nTS]
    return join


def emit_ln_to_dram(c, x_dram, xln_dram, x_off=0):
    s, t = c.s, c.t
    for ti in range(NTT):
        tok0 = ti * TT
        emit_ln(c, x_dram, t["XLN"], tok0, TT, x_dram_off=x_off)
        for ci in range(DC):
            s.dma("sync", (lambda ci, tok0: lambda e: e.dma_start(
                out=xln_dram[ci * 128:(ci + 1) * 128, tok0:tok0 + TT],
                in_=t["XLN"][:, ci * TT:(ci + 1) * TT]))(ci, tok0))


def alloc_common(c):
    c.sb("xt", [128, TT], F32)
    c.sb("xt2", [128, TT], F32)
    c.sb("xbt", [128, TT], BF16)
    c.sb("sqt", [128, TT], BF16)
    c.sb("tmpf", [128, TT], F32)
    c.sb("tmpf2", [128, TT], F32)
    c.sb("mb", [128, TT], F32)
    c.sb("vb", [128, TT], F32)
    c.t["rb"] = c.t["vb"]
    c.sb("ones_bf", [128, 128], BF16)
    c.sb("ones_f32", [128, 128], F32)
    c.sb("XLN", [128, DC * TT], BF16)
    c.ps("pa", [128, TT])
    c.ps("pb", [128, TT])
    c.ps("pc", [128, TT])
    c.ps("pd", [128, TT])


def alloc_ffn(c):
    c.sb("H", [128, FC * TT], BF16)
    c.sb("w1t", [128, DC * 128], BF16)
    c.sb("w1t2", [128, DC * 128], BF16)
    c.sb("w2ta", [128, (FC // 4) * 128], BF16)
    c.sb("w2tb", [128, (FC // 4) * 128], BF16)


def alloc_quant(c):
    c.sb("parts", [128, 130], F32)
    c.sb("thr", [128, 1], F32)
    t = c.t
    t["wrow"], t["wrow2"] = t["xt"], t["xt2"]
    t["sgn"], t["absw"] = t["tmpf"], t["tmpf2"]
    t["sgn2"], t["absw2"] = t["mb"], t["vb"]
    t["wqt"], t["wqt2"] = t["xbt"], t["sqt"]


def load_consts(c, d_ones_bf, d_ones_f32):
    s, t = c.s, c.t
    s.dma("sync", lambda e: e.dma_start(out=t["ones_bf"][:, :], in_=d_ones_bf[:, :]))
    s.dma("sync", lambda e: e.dma_start(out=t["ones_f32"][:, :], in_=d_ones_f32[:, :]))


def build_ffn_launch(quant_kind):
    nc = bass.Bass("TRN2", target_bir_lowering=False, debug=False)
    d_x = nc.dram_tensor("x0", [D, TPC], F32, kind="ExternalInput").ap()
    d_w1 = nc.dram_tensor("w1t", [D, F_FF], F32, kind="ExternalInput").ap()
    d_w2 = nc.dram_tensor("w2t", [F_FF, D], F32, kind="ExternalInput").ap()
    d_b1 = nc.dram_tensor("b1", [128, FC], F32, kind="ExternalInput").ap()
    d_b2 = nc.dram_tensor("b2", [128, DC], F32, kind="ExternalInput").ap()
    d_ones_bf = nc.dram_tensor("ones_bf", [128, 128], BF16, kind="ExternalInput").ap()
    d_ones_f32 = nc.dram_tensor("ones_f32", [128, 128], F32, kind="ExternalInput").ap()

    o_x = nc.dram_tensor("x1", [D, TPC], F32, kind="ExternalOutput").ap()
    o_xln = nc.dram_tensor("xln1", [D, TPC], BF16, kind="ExternalOutput").ap()
    o_w1q = nc.dram_tensor("w1q", [D, F_FF], BF16, kind="ExternalOutput").ap()
    o_w2q = nc.dram_tensor("w2q", [F_FF, D], BF16, kind="ExternalOutput").ap()

    i_w1q = nc.dram_tensor("i_w1q", [D, F_FF], BF16, kind="Internal").ap()
    i_w2q = nc.dram_tensor("i_w2q", [F_FF, D], BF16, kind="Internal").ap()
    i_x1 = nc.dram_tensor("i_x1", [D, TPC], F32, kind="Internal").ap()

    with ExitStack() as stack:
        sched = Sched(nc)
        c = Ctx(nc, sched, stack)
        alloc_common(c)
        alloc_ffn(c)
        alloc_quant(c)
        c.sb("b1t", [128, FC], F32)
        c.sb("halfb2", [128, DC], F32)
        ch = stack.enter_context(nc.semaphore("ch"))
        blk = stack.enter_context(nc.Block())

        s, t = c.s, c.t
        load_consts(c, d_ones_bf, d_ones_f32)
        s.dma("sync", lambda e: e.dma_start(out=t["b1t"][:, :], in_=d_b1[:, :]))
        s.dma("sync", lambda e: e.dma_start(out=t["halfb2"][:, :], in_=d_b2[:, :]))
        s.op("scalar", lambda e: e.activation(t["halfb2"][:, :], t["halfb2"][:, :], AF.Copy, scale=0.5))

        j1 = emit_quant(c, d_w1, D, F_FF, quant_kind, i_w1q, o_w1q)
        j2 = emit_quant(c, d_w2, F_FF, D, quant_kind, i_w2q, o_w2q, ready=j1)
        j3 = emit_ffn(c, d_x, i_w1q, i_w2q, t["b1t"], t["halfb2"], [o_x, i_x1], w_ready=j2)
        emit_ln_to_dram(c, i_x1, o_xln)

        sched.emit(blk, ch)
    return nc


TTA = 1024
NTA = S_LEN // TTA
BPT = TTA // CH


def build_launch2(dbg=False):
    nc = bass.Bass("TRN2", target_bir_lowering=False, debug=False)
    d_xln = nc.dram_tensor("xln", [D, S_LEN], BF16, kind="ExternalInput").ap()
    d_wmix = nc.dram_tensor("wmix", [D, 1536], F32, kind="ExternalInput").ap()
    d_wz = nc.dram_tensor("wz", [D, 512], F32, kind="ExternalInput").ap()
    d_wba = nc.dram_tensor("wba", [D, 8], F32, kind="ExternalInput").ap()
    d_convw = nc.dram_tensor("convw", [128, 48], F32, kind="ExternalInput").ap()
    d_dtb = nc.dram_tensor("dtb", [4, 1], F32, kind="ExternalInput").ap()
    d_alog = nc.dram_tensor("alog", [4, 1], F32, kind="ExternalInput").ap()
    d_nw = nc.dram_tensor("nw", [128, 1], F32, kind="ExternalInput").ap()
    d_ones_bf = nc.dram_tensor("ones_bf", [128, 128], BF16, kind="ExternalInput").ap()
    d_ones_f32 = nc.dram_tensor("ones_f32", [128, 128], F32, kind="ExternalInput").ap()
    d_ident_bf = nc.dram_tensor("ident_bf", [128, 128], BF16, kind="ExternalInput").ap()
    d_ident_f32 = nc.dram_tensor("ident_f32", [128, 128], F32, kind="ExternalInput").ap()
    d_triu = nc.dram_tensor("triu", [128, 128], F32, kind="ExternalInput").ap()
    d_mski = nc.dram_tensor("mski", [128, 128], BF16, kind="ExternalInput").ap()
    d_mskn = nc.dram_tensor("mskn", [128, 128], BF16, kind="ExternalInput").ap()
    d_mskn2 = nc.dram_tensor("mskn2", [128, 128], BF16, kind="ExternalInput").ap()
    o_hout = nc.dram_tensor("hout", [512, S_LEN], BF16, kind="ExternalOutput").ap()

    with ExitStack() as stack:
        sched = Sched(nc)
        c = Ctx(nc, sched, stack)
        s = sched
        c.sb("Wmix", [128, DC * 1536], BF16)
        c.sb("Wz", [128, DC * 512], BF16)
        c.sb("Wba", [128, DC * 8], BF16)
        c.sb("convw", [128, 48], F32)
        c.sb("nw", [128, 1], F32)
        c.sb("negEA", [4, 1], F32)
        c.sb("dtb", [4, 1], F32)
        for nm, dt in [("ones_bf", BF16), ("ones_f32", F32), ("ident_bf", BF16),
                       ("ident_f32", F32), ("triu", F32), ("mski", BF16),
                       ("mskn", BF16), ("mskn2", BF16)]:
            c.sb(nm, [128, 128], dt)
        c.sb("xs", [128, DC * 512], BF16)
        c.sb("xs2", [128, DC * 512], BF16)
        c.sb("mixed", [128, 12 * (TTA + 3)], BF16)
        c.sb("tail", [128, 12 * 3], BF16)
        c.sb("Z", [128, 4 * TTA], BF16)
        c.sb("acc_a", [128, TTA], F32)
        c.sb("acc_b", [128, TTA], F32)
        c.sb("bgate", [4, TTA], F32)
        c.sb("arows", [4, TTA], F32)
        c.sb("grow", [4, TTA], F32)
        c.sb("betaf", [128, HPC * TTA], BF16)
        c.sb("beta_tm", [128, BPT * 4], F32)
        c.sb("g_tm", [128, BPT * 4], F32)
        c.sb("gc_all", [128, BPT * 4], F32)
        c.sb("egc_all", [128, BPT * 4], F32)
        c.sb("gl_all", [128, BPT * 4], F32)
        c.sb("ekd_all", [128, BPT * 4], F32)
        c.sb("egl_all", [128, BPT * 4], F32)
        c.sb("glast", [1, BPT * 4], F32)
        c.sb("ktm", [128, HPC * BPT * 128], BF16)
        c.sb("vtm", [128, HPC * BPT * 128], BF16)
        c.sb("sqn", [128, TTA], BF16)
        c.sb("rstdn", [128, TTA], F32)
        c.sb("gcb", [128, 128], F32)
        c.sb("dec", [128, 128], F32)
        c.sb("dec2", [128, 128], F32)
        c.sb("mn", [128, 128], F32)
        c.sb("egcf", [128, 128], F32)
        c.sb("kbfm", [128, 128], BF16)
        c.sb("ttmp", [128, 128], F32)
        c.sb("ttmp2", [128, 128], F32)
        c.sb("Atile", [128, 128], BF16)
        c.sb("Q0", [128, 128], BF16)
        c.sb("PQ_a", [128, 256], BF16)
        c.sb("PQ_b", [128, 256], BF16)
        c.sb("X_a", [128, 128], BF16)
        c.sb("X_b", [128, 128], BF16)
        c.sb("uw", [128, 256], BF16)
        c.sb("vnew", [128, 128], BF16)
        c.sb("qe", [128, 128], BF16)
        c.sb("attnT", [128, 128], BF16)
        c.sb("vb_tm", [128, 128], BF16)
        c.sb("kbe_tm", [128, 128], BF16)
        c.sb("kdec_tm", [128, 128], BF16)
        c.sb("Sf", [128, HPC * 2 * 128], F32)
        c.sb("Sb", [128, HPC * 128], BF16)
        c.sb("sqo", [128, 128], BF16)
        c.sb("rstdo", [128, 128], F32)
        c.sb("szt", [128, 128], BF16)
        c.sb("houtt", [128, 128], BF16)
        c.ps("pa", [128, 512])
        c.ps("pz", [128, 512])
        c.ps("pz2", [128, 512])
        c.ps("pt", [128, 512], BF16)
        c.ps("pfa", [128, 512])
        c.ps("pfb", [128, 512])
        c.ps("po", [128, 128])
        c.ps("pg", [128, 512])

        if dbg:
            c.sb("_dbg_po", [128, 128], F32)
            _dsel = dbg if isinstance(dbg, tuple) else (0, dbg if isinstance(dbg, int) else 0)
            c.t["_dbg_sel"] = (_dsel[0], _dsel[1], 0)
            c.t["_dbg_Sdram"] = nc.dram_tensor("dbg_Straj", [128, 64 * 128], F32, kind="ExternalOutput").ap()
            c.t["_dbg_Bdram"] = nc.dram_tensor("dbg_Btraj", [128, 64 * 128], F32, kind="ExternalOutput").ap()
        ch = stack.enter_context(nc.semaphore("ch"))
        blk = stack.enter_context(nc.Block())
        t = c.t
        t["sp1"] = t["acc_a"]
        t["sp2"] = t["acc_b"]
        c.Yv = lambda m, a, b: t["mixed"][:, m * (TTA + 3) + 3 + a:m * (TTA + 3) + 3 + b]

        for nm, src in [("ones_bf", d_ones_bf), ("ones_f32", d_ones_f32),
                        ("ident_bf", d_ident_bf), ("ident_f32", d_ident_f32),
                        ("triu", d_triu), ("mski", d_mski), ("mskn", d_mskn),
                        ("mskn2", d_mskn2), ("convw", d_convw), ("nw", d_nw),
                        ("dtb", d_dtb)]:
            s.dma("sync", (lambda nm, src: lambda e: e.dma_start(out=t[nm][:, :], in_=src[:, :]))(nm, src))
        s.dma("gpsimd", lambda e: e.dma_start(
            out=t["Wmix"][:, :].rearrange("p (c f) -> p c f", c=DC),
            in_=d_wmix.rearrange("(c p) f -> p c f", p=128)))
        s.dma("gpsimd", lambda e: e.dma_start(
            out=t["Wz"][:, :].rearrange("p (c f) -> p c f", c=DC),
            in_=d_wz.rearrange("(c p) f -> p c f", p=128)))
        s.dma("gpsimd", lambda e: e.dma_start(
            out=t["Wba"][:, :].rearrange("p (c f) -> p c f", c=DC),
            in_=d_wba.rearrange("(c p) f -> p c f", p=128)))
        s.dma("sync", lambda e: e.dma_start(out=t["negEA"][:, :], in_=d_alog[:, :]))
        s.op("scalar", lambda e: e.activation(t["negEA"][:, :], t["negEA"][:, :], AF.Exp))
        s.op("vector", lambda e: e.tensor_scalar_mul(t["negEA"][:, :], t["negEA"][:, :], -1.0))
        s.op("vector", lambda e: e.memset(t["tail"][:, :], 0.0))
        s.op("vector", lambda e: e.memset(t["Sf"][:, :], 0.0))
        s.op("vector", lambda e: e.memset(t["Sb"][:, :], 0.0))

        ln128 = float(np.log(128.0))
        eps_l2 = c.cst(1e-6)
        one_c = c.cst(1.0)
        qbias = c.cst(-0.5 * ln128)
        zero_c = c.cst(0.0)
        eps_rms = c.cst(1e-6)

        sXS = c.sem("xs")
        sEv = c.sem("ev")
        nXS = nEv = 0
        xs_last = [None, None]
        for ti in range(NTA):
            tok0 = ti * TTA
            for sl in range(2):
                t0 = tok0 + sl * 512
                gsl = 2 * ti + sl
                xs = t["xs"] if gsl % 2 == 0 else t["xs2"]
                aft = [] if xs_last[gsl % 2] is None else [xs_last[gsl % 2]]
                s.dma("sync", (lambda t0, xs: lambda e: e.dma_start(
                    out=xs[:, :].rearrange("p (c f) -> p c f", c=DC),
                    in_=d_xln.rearrange("(c p) t -> p c t", p=128)[:, :, t0:t0 + 512]))(t0, xs),
                    after=aft, track=(sXS, 16))
                nXS += 1
                for m in range(12):
                    ps = t["pa"] if m % 2 == 0 else t["pz2"]
                    for ci in range(DC):
                        w = None
                        if ci == 0:
                            w = [(sEv, max(0, nEv - 1))]
                            if m == 0:
                                w.append((sXS, 16 * nXS))
                        idx = s.op("tensor", (lambda m, ci, xs, ps: lambda e: e.matmul(
                            ps[:, :], t["Wmix"][:, ci * 1536 + m * 128:ci * 1536 + (m + 1) * 128],
                            xs[:, ci * 512:(ci + 1) * 512],
                            start=(ci == 0), stop=(ci == DC - 1)))(m, ci, xs, ps),
                            waits=w)
                    s.op("scalar", (lambda m, sl, ps: lambda e: e.activation(
                        t["mixed"][:, m * (TTA + 3) + 3 + sl * 512: m * (TTA + 3) + 3 + (sl + 1) * 512],
                        ps[:, :], AF.Copy))(m, sl, ps),
                        after=[idx], track=(sEv, 1))
                    nEv += 1
                for zc in range(4):
                    psz = t["pa"] if zc % 2 == 0 else t["pz2"]
                    for ci in range(DC):
                        w = [(sEv, max(0, nEv - 1))] if ci == 0 else None
                        idx = s.op("tensor", (lambda zc, ci, xs, psz: lambda e: e.matmul(
                            psz[:, :], t["Wz"][:, ci * 512 + zc * 128:ci * 512 + (zc + 1) * 128],
                            xs[:, ci * 512:(ci + 1) * 512],
                            start=(ci == 0), stop=(ci == DC - 1)))(zc, ci, xs, psz),
                            waits=w)
                    s.op("scalar", (lambda zc, sl, psz: lambda e: e.activation(
                        t["Z"][:, zc * TTA + sl * 512: zc * TTA + (sl + 1) * 512],
                        psz[:, :], AF.Silu))(zc, sl, psz),
                        after=[idx], track=(sEv, 1))
                    nEv += 1
                for ci in range(DC):
                    s.op("tensor", (lambda ci, xs: lambda e: e.matmul(
                        t["pfa"][0:4, 0:512], t["Wba"][:, ci * 8:ci * 8 + 4],
                        xs[:, ci * 512:(ci + 1) * 512],
                        start=(ci == 0), stop=(ci == DC - 1)))(ci, xs))
                    idx = s.op("tensor", (lambda ci, xs: lambda e: e.matmul(
                        t["pfb"][0:4, 0:512], t["Wba"][:, ci * 8 + 4:ci * 8 + 8],
                        xs[:, ci * 512:(ci + 1) * 512],
                        start=(ci == 0), stop=(ci == DC - 1)))(ci, xs))
                xs_last[gsl % 2] = idx
                s.op("scalar", (lambda sl: lambda e: e.activation(
                    t["bgate"][:, sl * 512:(sl + 1) * 512], t["pfa"][0:4, 0:512], AF.Sigmoid))(sl))
                s.op("vector", (lambda sl: lambda e: e.tensor_scalar(
                    t["arows"][:, sl * 512:(sl + 1) * 512], t["pfb"][0:4, 0:512],
                    t["dtb"][0:4, 0:1], None, ALU.add))(sl))
            s.join("sync", [(sEv, nEv), (sXS, 16 * nXS)])
            s.op("scalar", lambda e: e.activation(t["sp1"][0:4, :], t["arows"][0:4, :], AF.Abs))
            s.op("scalar", lambda e: e.activation(t["grow"][0:4, :], t["sp1"][0:4, :], AF.Exp, scale=-1.0))
            s.op("scalar", lambda e: e.activation(t["sp1"][0:4, :], t["grow"][0:4, :], AF.Ln, bias=one_c[0:4, :]))
            s.op("vector", lambda e: e.tensor_scalar_max(t["sp2"][0:4, :], t["arows"][0:4, :], 0.0))
            s.op("vector", lambda e: e.tensor_add(t["sp2"][0:4, :], t["sp2"][0:4, :], t["sp1"][0:4, :]))
            s.op("vector", lambda e: e.tensor_scalar(
                t["grow"][:, :], t["sp2"][0:4, :], t["negEA"][0:4, 0:1], None, ALU.mult))
            for b_ in range(BPT):
                s.op("tensor", (lambda b_: lambda e: e.transpose(
                    t["pg"][:, 0:4],
                    t["bgate"][0:4, b_ * 128:(b_ + 1) * 128], t["ident_f32"][0:4, 0:4]))(b_))
                s.op("scalar", (lambda b_: lambda e: e.activation(
                    t["beta_tm"][:, b_ * 4:(b_ + 1) * 4], t["pg"][:, 0:4], AF.Copy))(b_))
                s.op("tensor", (lambda b_: lambda e: e.transpose(
                    t["pg"][:, 128:132], t["grow"][0:4, b_ * 128:(b_ + 1) * 128], t["ident_f32"][0:4, 0:4]))(b_))
                s.op("scalar", (lambda b_: lambda e: e.activation(
                    t["g_tm"][:, b_ * 4:(b_ + 1) * 4], t["pg"][:, 128:132], AF.Copy))(b_))
            s.op("tensor", lambda e: e.matmul(t["pg"][:, 0:32], t["triu"][:, :], t["g_tm"][:, :],
                                              start=True, stop=True))
            s.op("scalar", lambda e: e.activation(t["gc_all"][:, :], t["pg"][:, 0:32], AF.Copy))
            s.op("scalar", lambda e: e.activation(t["egc_all"][:, :], t["pg"][:, 0:32], AF.Exp))
            s.dma("sync", lambda e: e.dma_start(out=t["glast"][0:1, :], in_=t["gc_all"][127:128, :]))
            s.op("tensor", lambda e: e.matmul(t["pg"][:, 64:96], t["ones_f32"][0:1, :], t["glast"][0:1, :],
                                              start=True, stop=True))
            s.op("scalar", lambda e: e.activation(t["gl_all"][:, :], t["pg"][:, 64:96], AF.Copy))
            s.op("scalar", lambda e: e.activation(t["egl_all"][:, :], t["pg"][:, 64:96], AF.Exp))
            s.op("vector", lambda e: e.tensor_sub(t["ekd_all"][:, :], t["gl_all"][:, :], t["gc_all"][:, :]))
            s.op("scalar", lambda e: e.activation(t["ekd_all"][:, :], t["ekd_all"][:, :], AF.Exp))
            for h in range(HPC):
                s.dma("sync", (lambda h: lambda e: e.dma_start(
                    out=t["acc_a"][0:1, :], in_=t["bgate"][h:h + 1, :]))(h))
                for sl in range(2):
                    s.op("tensor", (lambda h, sl: lambda e: e.matmul(
                        t["pg"][:, 0:512],
                        t["ones_f32"][0:1, :], t["acc_a"][0:1, sl * 512:(sl + 1) * 512],
                        start=True, stop=True))(h, sl))
                    s.op("scalar", (lambda h, sl: lambda e: e.activation(
                        t["betaf"][:, h * TTA + sl * 512: h * TTA + (sl + 1) * 512],
                        t["pg"][:, 0:512], AF.Copy))(h, sl))
            for m in range(12):
                mo = m * (TTA + 3)
                s.op("vector", (lambda m, mo: lambda e: e.tensor_copy(
                    t["mixed"][:, mo:mo + 3], t["tail"][:, m * 3:(m + 1) * 3]))(m, mo))
                s.op("vector", (lambda m, mo: lambda e: e.tensor_scalar(
                    t["acc_a"][:, :], t["mixed"][:, mo:mo + TTA], t["convw"][:, m * 4:m * 4 + 1],
                    None, ALU.mult))(m, mo))
                s.op("vector", (lambda m, mo: lambda e: e.scalar_tensor_tensor(
                    t["acc_b"][:, :], t["mixed"][:, mo + 1:mo + 1 + TTA], t["convw"][:, m * 4 + 1:m * 4 + 2],
                    t["acc_a"][:, :], ALU.mult, ALU.add))(m, mo))
                s.op("vector", (lambda m, mo: lambda e: e.scalar_tensor_tensor(
                    t["acc_a"][:, :], t["mixed"][:, mo + 2:mo + 2 + TTA], t["convw"][:, m * 4 + 2:m * 4 + 3],
                    t["acc_b"][:, :], ALU.mult, ALU.add))(m, mo))
                s.op("vector", (lambda m, mo: lambda e: e.scalar_tensor_tensor(
                    t["acc_b"][:, :], t["mixed"][:, mo + 3:mo + 3 + TTA], t["convw"][:, m * 4 + 3:m * 4 + 4],
                    t["acc_a"][:, :], ALU.mult, ALU.add))(m, mo))
                s.op("vector", (lambda m, mo: lambda e: e.tensor_copy(
                    t["tail"][:, m * 3:(m + 1) * 3], t["mixed"][:, mo + TTA:mo + TTA + 3]))(m, mo))
                s.op("scalar", (lambda m: lambda e: e.activation(
                    c.Yv(m, 0, TTA), t["acc_b"][:, :], AF.Silu))(m))
            for h in range(HPC):
                for which, bias in ((0, qbias), (4, zero_c)):
                    m = which + h
                    s.op("scalar", (lambda m: lambda e: e.activation(
                        t["sqn"][:, :], c.Yv(m, 0, TTA), AF.Square))(m))
                    for sl in range(2):
                        s.op("tensor", (lambda sl: lambda e: e.matmul(
                            t["pa"][:, :], t["ones_bf"][:, :], t["sqn"][:, sl * 512:(sl + 1) * 512],
                            start=True, stop=True))(sl))
                        s.op("scalar", (lambda sl: lambda e: e.activation(
                            t["rstdn"][:, sl * 512:(sl + 1) * 512], t["pa"][:, :], AF.Ln,
                            bias=eps_l2))(sl))
                    s.op("scalar", (lambda bias: lambda e: e.activation(
                        t["rstdn"][:, :], t["rstdn"][:, :], AF.Exp, scale=-0.5, bias=bias))(bias))
                    s.op("vector", (lambda m: lambda e: e.tensor_mul(
                        c.Yv(m, 0, TTA), c.Yv(m, 0, TTA),
                        t["rstdn"][:, :]))(m))
            for h in range(HPC):
                for b_ in range(BPT):
                    for which, dst in ((4, "ktm"), (8, "vtm")):
                        m = which + h
                        s.op("tensor", (lambda m, b_: lambda e: e.transpose(
                            t["pt"][:, 0:128], c.Yv(m, b_ * 128, (b_ + 1) * 128),
                            t["ident_bf"][:, :]))(m, b_))
                        s.op("scalar", (lambda h, b_, dst: lambda e: e.activation(
                            t[dst][:, (h * BPT + b_) * 128:(h * BPT + b_ + 1) * 128],
                            t["pt"][:, 0:128], AF.Copy))(h, b_, dst))
            if dbg and ti == (dbg[0] if isinstance(dbg, tuple) else 0):
                for nm in ["Z", "mixed", "gc_all", "beta_tm", "g_tm", "egc_all",
                           "ekd_all", "egl_all", "betaf", "ktm", "vtm", "bgate", "grow"]:
                    hnd = t[nm]
                    shp = [hnd.shape[0], hnd.shape[1]]
                    ap = nc.dram_tensor("dbg_" + nm, shp, hnd.dtype, kind="ExternalOutput").ap()
                    s.dma("sync", (lambda ap, hnd: lambda e: e.dma_start(
                        out=ap[:, :], in_=hnd[:, :]))(ap, hnd))
            for b_ in range(BPT):
                for h in range(HPC):
                    emit_delta_chunk(c, ti, b_, h, o_hout)
                    _dsel = dbg if isinstance(dbg, tuple) else (0, dbg if isinstance(dbg, int) else 0)
                    if dbg and (ti, b_, h) == (_dsel[0], _dsel[1], 0):
                        for nm in ["dec", "dec2", "egcf", "kbfm", "Atile", "Q0", "X_a",
                                   "X_b", "uw", "vnew", "qe", "attnT", "vb_tm", "kbe_tm",
                                   "kdec_tm", "houtt", "rstdo", "Sf", "_dbg_po"]:
                            hnd = t[nm]
                            shp = [hnd.shape[0], hnd.shape[1]]
                            ap = nc.dram_tensor("dbg_" + nm, shp, hnd.dtype, kind="ExternalOutput").ap()
                            s.dma("sync", (lambda ap, hnd: lambda e: e.dma_start(
                                out=ap[:, :], in_=hnd[:, :]))(ap, hnd))

        sched.emit(blk, ch)
    return nc


def emit_delta_chunk(c, ti, b_, h, o_hout):
    s, t = c.s, c.t
    col = b_ * 4 + h
    gc_col = t["gc_all"][:, col:col + 1]
    beta_col = t["beta_tm"][:, col:col + 1]
    egc_col = t["egc_all"][:, col:col + 1]
    ekd_col = t["ekd_all"][:, col:col + 1]
    egl_col = t["egl_all"][:, col:col + 1]
    qY = c.Yv(0 + h, b_ * 128, (b_ + 1) * 128)
    kY = c.Yv(4 + h, b_ * 128, (b_ + 1) * 128)
    ktm = t["ktm"][:, (h * BPT + b_) * 128:(h * BPT + b_ + 1) * 128]
    vtm = t["vtm"][:, (h * BPT + b_) * 128:(h * BPT + b_ + 1) * 128]
    bf_sl = t["betaf"][:, h * TTA + b_ * 128:h * TTA + (b_ + 1) * 128]
    z_sl = t["Z"][:, h * TTA + b_ * 128:h * TTA + (b_ + 1) * 128]
    par = (ti * BPT + b_) % 2
    S_old = t["Sf"][:, (h * 2 + par) * 128:(h * 2 + par + 1) * 128]
    S_new = t["Sf"][:, (h * 2 + (1 - par)) * 128:(h * 2 + (1 - par) + 1) * 128]
    S_bf = t["Sb"][:, h * 128:(h + 1) * 128]
    chunk_glob = (ti * BPT + b_)

    s.op("vector", lambda e: e.tensor_scalar(
        t["gcb"][:, :], t["ones_f32"][:, :], gc_col, None, ALU.mult))
    s.op("tensor", lambda e: e.transpose(t["pg"][:, 0:128], t["gcb"][:, :], t["ident_f32"][:, :]))
    s.op("vector", lambda e: e.tensor_scalar(
        t["ttmp"][:, :], t["pg"][:, 0:128], gc_col, None, ALU.subtract))
    s.op("vector", lambda e: e.tensor_scalar_min(t["mn"][:, :], t["ttmp"][:, :], 0.0))
    s.op("scalar", lambda e: e.activation(t["dec"][:, :], t["mn"][:, :], AF.Exp))
    s.op("vector", lambda e: e.tensor_scalar_max(t["mn"][:, :], t["ttmp"][:, :], 0.0))
    s.op("scalar", lambda e: e.activation(t["dec2"][:, :], t["mn"][:, :], AF.Exp, scale=-1.0))
    s.op("scalar", lambda e: e.activation(t["egcf"][:, :], t["pg"][:, 0:128], AF.Exp))
    s.op("vector", lambda e: e.tensor_mul(t["kbfm"][:, :], kY, bf_sl))
    s.op("tensor", lambda e: e.matmul(t["pfa"][:, 0:128], kY, t["kbfm"][:, :], start=True, stop=True))
    s.op("vector", lambda e: e.tensor_mul(t["ttmp"][:, :], t["pfa"][:, 0:128], t["dec"][:, :]))
    s.op("vector", lambda e: e.tensor_mul(t["Atile"][:, :], t["ttmp"][:, :], t["mskn"][:, :]))
    s.op("tensor", lambda e: e.matmul(t["pfa"][:, 128:256], t["kbfm"][:, :], kY, start=True, stop=True))
    s.op("vector", lambda e: e.tensor_mul(t["ttmp"][:, :], t["pfa"][:, 128:256], t["dec2"][:, :]))
    s.op("vector", lambda e: e.tensor_mul(t["Q0"][:, :], t["ttmp"][:, :], t["mskn2"][:, :]))
    s.op("vector", lambda e: e.tensor_add(t["X_a"][:, :], t["Atile"][:, :], t["ident_bf"][:, :]))
    P_prev, Q_prev, X_prev = t["Atile"], t["Q0"], t["X_a"]
    for k in range(1, 7):
        PQ = t["PQ_a"] if k % 2 == 1 else t["PQ_b"]
        Xn = t["X_b"] if k % 2 == 1 else t["X_a"]
        s.op("tensor", (lambda Q_prev, P_prev: lambda e: e.matmul(
            t["pfb"][:, 0:128], Q_prev[:, 0:128], P_prev[:, 0:128], start=True, stop=True))(Q_prev, P_prev))
        s.op("tensor", (lambda Q_prev, P_prev: lambda e: e.matmul(
            t["pfb"][:, 128:256], P_prev[:, 0:128], Q_prev[:, 0:128], start=True, stop=True))(Q_prev, P_prev))
        s.op("scalar", (lambda PQ: lambda e: e.activation(PQ[:, :], t["pfb"][:, 0:256], AF.Copy))(PQ))
        s.op("tensor", (lambda PQ, X_prev: lambda e: e.matmul(
            t["pfa"][:, 0:128], PQ[:, 128:256], X_prev[:, 0:128], start=True, stop=True))(PQ, X_prev))
        s.op("vector", (lambda Xn, X_prev: lambda e: e.tensor_add(
            Xn[:, :], t["pfa"][:, 0:128], X_prev[:, 0:128]))(Xn, X_prev))
        P_prev, Q_prev, X_prev = (lambda PQ: (PQ[:, 0:128], PQ[:, 128:256]))(PQ) + (Xn,)
    X_fin = X_prev[:, 0:128]
    s.op("vector", lambda e: e.tensor_scalar(t["vb_tm"][:, :], vtm, beta_col, None, ALU.mult))
    s.op("vector", lambda e: e.tensor_scalar(t["kbe_tm"][:, :], ktm, beta_col, egc_col, ALU.mult, ALU.mult))
    s.op("vector", lambda e: e.tensor_scalar(t["kdec_tm"][:, :], ktm, ekd_col, None, ALU.mult))
    s.op("tensor", (lambda X_fin: lambda e: e.matmul(
        t["pfb"][:, 0:128], X_fin, t["vb_tm"][:, :], start=True, stop=True))(X_fin))
    s.op("tensor", (lambda X_fin: lambda e: e.matmul(
        t["pfb"][:, 128:256], t["kbe_tm"][:, :], X_fin, start=True, stop=True))(X_fin))
    s.op("scalar", lambda e: e.activation(t["uw"][:, :], t["pfb"][:, 0:256], AF.Copy))
    s.op("tensor", lambda e: e.matmul(t["pfa"][:, 0:128], t["uw"][:, 128:256], S_bf, start=True, stop=True))
    s.op("vector", lambda e: e.tensor_sub(t["vnew"][:, :], t["uw"][:, 0:128], t["pfa"][:, 0:128]))
    s.op("vector", lambda e: e.tensor_mul(t["qe"][:, :], qY, t["egcf"][:, :]))
    s.op("tensor", lambda e: e.matmul(t["po"][:, :], S_bf, t["qe"][:, :], start=True, stop=False,
                                      skip_group_check=True))
    s.op("tensor", lambda e: e.matmul(t["pfa"][:, 128:256], kY, qY, start=True, stop=True))
    s.op("vector", lambda e: e.tensor_mul(t["ttmp2"][:, :], t["pfa"][:, 128:256], t["dec"][:, :]))
    s.op("vector", lambda e: e.tensor_mul(t["attnT"][:, :], t["ttmp2"][:, :], t["mski"][:, :]))
    s.op("tensor", lambda e: e.matmul(t["po"][:, :], t["vnew"][:, :], t["attnT"][:, :],
                                      start=False, stop=True, skip_group_check=True))
    if c.t.get("_dbg_po") is not None and (ti, b_, h) == c.t.get("_dbg_sel", (0, 1, 0)):
        s.op("vector", lambda e: e.tensor_copy(c.t["_dbg_po"][:, :], t["po"][:, :]))
    s.op("tensor", lambda e: e.matmul(t["pfb"][:, 0:128], t["kdec_tm"][:, :], t["vnew"][:, :],
                                      start=True, stop=True))
    s.op("vector", (lambda S_new, S_old: lambda e: e.scalar_tensor_tensor(
        S_new, S_old, egl_col, t["pfb"][:, 0:128], ALU.mult, ALU.add))(S_new, S_old))
    s.op("vector", (lambda S_bf, S_new: lambda e: e.tensor_copy(S_bf, S_new))(S_bf, S_new))
    if c.t.get("_dbg_Sdram") is not None and h == 0:
        cg = ti * BPT + b_
        s.dma("sync", (lambda S_new, cg: lambda e: e.dma_start(
            out=c.t["_dbg_Sdram"][:, cg * 128:(cg + 1) * 128], in_=S_new))(S_new, cg))
        s.op("vector", lambda e: e.tensor_copy(t["ttmp"][:, :], t["pfb"][:, 0:128]))
        s.dma("sync", (lambda cg: lambda e: e.dma_start(
            out=c.t["_dbg_Bdram"][:, cg * 128:(cg + 1) * 128], in_=t["ttmp"][:, :]))(cg))
    s.op("scalar", lambda e: e.activation(t["sqo"][:, :], t["po"][:, :], AF.Square))
    s.op("tensor", lambda e: e.matmul(t["pg"][:, 128:256], t["ones_bf"][:, :], t["sqo"][:, :],
                                      start=True, stop=True))
    s.op("scalar", lambda e: e.activation(t["rstdo"][:, :], t["pg"][:, 128:256], AF.Ln,
                                          scale=1.0 / 128, bias=c.cst(1e-6)))
    s.op("scalar", lambda e: e.activation(t["rstdo"][:, :], t["rstdo"][:, :], AF.Exp, scale=-0.5))
    s.op("vector", lambda e: e.tensor_mul(t["ttmp"][:, :], t["po"][:, :], t["rstdo"][:, :]))
    s.op("vector", lambda e: e.tensor_scalar(t["ttmp"][:, :], t["ttmp"][:, :], t["nw"][:, 0:1],
                                             None, ALU.mult))
    s.op("vector", (lambda z_sl: lambda e: e.tensor_mul(
        t["houtt"][:, :], t["ttmp"][:, :], z_sl))(z_sl))
    s.dma("sync", (lambda h, chunk_glob: lambda e: e.dma_start(
        out=o_hout[h * 128:(h + 1) * 128, chunk_glob * 128:(chunk_glob + 1) * 128],
        in_=t["houtt"][:, :]))(h, chunk_glob))


def emit_headproj_residual(c, hout_dram, w_dram_f32, x_in_dram, x_outs):
    s, t = c.s, c.t
    WB = t["H"]
    pc, pd = t["pc"], t["pd"]
    sWc = c.sem("wc")
    sHT = c.sem("ht")
    sX = c.sem("hx")
    sTS = c.sem("hts")
    sSt = c.sem("hst")
    nHT = nX = nTS = nSt = 0
    nst_per = len(x_outs) * 16
    s.dma("gpsimd", lambda e: e.dma_start(
        out=WB[:, 0:DC * D].rearrange("p (c f) -> p c f", c=DC),
        in_=w_dram_f32.rearrange("(c p) f -> p c f", p=128)),
        after=[], track=(sWc, 16))
    half = DC * 512
    mm_last_sl = [None, None]
    xwar = [0, 0]
    pswar = [0, 0]
    for sl in range(TPC // 512):
        t0 = sl * 512
        par_s = sl % 2
        HT = t["XLN"][:, par_s * half:(par_s + 1) * half]
        aft = [] if mm_last_sl[par_s] is None else [mm_last_sl[par_s]]
        s.dma("sync", (lambda t0, HT: lambda e: e.dma_start(
            out=HT.rearrange("p (c f) -> p c f", c=DC),
            in_=hout_dram.rearrange("(c p) t -> p c t", p=128)[:, :, t0:t0 + 512]))(t0, HT),
            after=aft, track=(sHT, 16))
        nHT += 1
        for di in range(DC):
            par = di % 2
            ps = pc if par == 0 else pd
            for cc in range(DC):
                w = None
                if cc == 0:
                    w = [(sTS, pswar[par])]
                    if di == 0:
                        w += [(sHT, 16 * nHT), (sWc, 16)]
                idx = s.op("tensor", (lambda di, cc, HT, ps: lambda e: e.matmul(
                    ps[:, 0:512], WB[:, cc * D + di * 128:cc * D + (di + 1) * 128],
                    HT[:, cc * 512:(cc + 1) * 512],
                    start=(cc == 0), stop=(cc == DC - 1)))(di, cc, HT, ps),
                    waits=w)
            mm_last_sl[par_s] = idx
            xt2b = t["xt"] if par == 0 else t["xt2"]
            s.dma("sync", (lambda di, t0, xt2b: lambda e: e.dma_start(
                out=xt2b[:, 0:512],
                in_=x_in_dram[di * 128:(di + 1) * 128, t0:t0 + 512]))(di, t0, xt2b),
                after=[], waits=[(sTS, xwar[par])], track=(sX, 16))
            nX += 1
            tmp = t["tmpf"] if par == 0 else t["tmpf2"]
            s.op("vector", (lambda tmp, xt2b, ps: lambda e: e.tensor_add(
                tmp[:, 0:512], ps[:, 0:512], xt2b[:, 0:512]))(tmp, xt2b, ps),
                after=[idx], waits=[(sX, 16 * nX), (sSt, max(0, nSt - nst_per))],
                track=(sTS, 1))
            nTS += 1
            pswar[par] = nTS
            xwar[par] = nTS
            for xo in x_outs:
                s.dma("gpsimd", (lambda di, t0, xo, tmp: lambda e: e.dma_start(
                    out=xo[di * 128:(di + 1) * 128, t0:t0 + 512], in_=tmp[:, 0:512]))(di, t0, xo, tmp),
                    after=[], waits=[(sTS, nTS)], track=(sSt, 16))
                nSt += 16
    jn = s.join("sync", [(sSt, nSt), (sTS, nTS), (sHT, 16 * nHT), (sX, 16 * nX), (sWc, 16)])
    c.xt_last_read = [jn, jn]
    return jn


def build_launch3():
    nc = bass.Bass("TRN2", target_bir_lowering=False, debug=False)
    d_x = nc.dram_tensor("x1", [D, TPC], F32, kind="ExternalInput").ap()
    d_hout = nc.dram_tensor("hout", [D, TPC], BF16, kind="ExternalInput").ap()
    d_wout = nc.dram_tensor("wout", [D, D], F32, kind="ExternalInput").ap()
    d_w1q0 = nc.dram_tensor("w1q0", [D, F_FF], BF16, kind="ExternalInput").ap()
    d_w2q0 = nc.dram_tensor("w2q0", [F_FF, D], BF16, kind="ExternalInput").ap()
    d_b1_0 = nc.dram_tensor("b1_0", [128, FC], F32, kind="ExternalInput").ap()
    d_b2_0 = nc.dram_tensor("b2_0", [128, DC], F32, kind="ExternalInput").ap()
    d_w1_1 = nc.dram_tensor("w1t_1", [D, F_FF], F32, kind="ExternalInput").ap()
    d_w2_1 = nc.dram_tensor("w2t_1", [F_FF, D], F32, kind="ExternalInput").ap()
    d_b1_1 = nc.dram_tensor("b1_1", [128, FC], F32, kind="ExternalInput").ap()
    d_b2_1 = nc.dram_tensor("b2_1", [128, DC], F32, kind="ExternalInput").ap()
    d_ones_bf = nc.dram_tensor("ones_bf", [128, 128], BF16, kind="ExternalInput").ap()
    d_ones_f32 = nc.dram_tensor("ones_f32", [128, 128], F32, kind="ExternalInput").ap()

    o_x = nc.dram_tensor("x3", [D, TPC], F32, kind="ExternalOutput").ap()
    o_xln = nc.dram_tensor("xln3", [D, TPC], BF16, kind="ExternalOutput").ap()
    o_w1q = nc.dram_tensor("w1q1", [D, F_FF], BF16, kind="ExternalOutput").ap()
    o_w2q = nc.dram_tensor("w2q1", [F_FF, D], BF16, kind="ExternalOutput").ap()

    i_x2 = nc.dram_tensor("i_x2", [D, TPC], F32, kind="Internal").ap()
    i_x3a = nc.dram_tensor("i_x3a", [D, TPC], F32, kind="Internal").ap()
    i_x3 = nc.dram_tensor("i_x3", [D, TPC], F32, kind="Internal").ap()
    i_w1q1 = nc.dram_tensor("i_w1q1", [D, F_FF], BF16, kind="Internal").ap()
    i_w2q1 = nc.dram_tensor("i_w2q1", [F_FF, D], BF16, kind="Internal").ap()

    with ExitStack() as stack:
        sched = Sched(nc)
        c = Ctx(nc, sched, stack)
        alloc_common(c)
        alloc_ffn(c)
        alloc_quant(c)
        c.sb("b1t", [128, FC], F32)
        c.sb("halfb2", [128, DC], F32)
        c.sb("b1t1", [128, FC], F32)
        c.sb("halfb21", [128, DC], F32)
        ch = stack.enter_context(nc.semaphore("ch"))
        blk = stack.enter_context(nc.Block())
        s, t = c.s, c.t
        load_consts(c, d_ones_bf, d_ones_f32)
        for dst, src in [("b1t", d_b1_0), ("halfb2", d_b2_0), ("b1t1", d_b1_1), ("halfb21", d_b2_1)]:
            s.dma("sync", (lambda dst, src: lambda e: e.dma_start(out=t[dst][:, :], in_=src[:, :]))(dst, src))
        s.op("scalar", lambda e: e.activation(t["halfb2"][:, :], t["halfb2"][:, :], AF.Copy, scale=0.5))
        s.op("scalar", lambda e: e.activation(t["halfb21"][:, :], t["halfb21"][:, :], AF.Copy, scale=0.5))

        j1 = emit_headproj_residual(c, d_hout, d_wout, d_x, [i_x2])
        j2 = emit_ffn(c, i_x2, d_w1q0, d_w2q0, t["b1t"], t["halfb2"], [i_x3a], x_ready=j1)
        j3 = emit_quant(c, d_w1_1, D, F_FF, "bitquant", i_w1q1, o_w1q, ready=j2)
        j4 = emit_quant(c, d_w2_1, F_FF, D, "bitquant", i_w2q1, o_w2q, ready=j3)
        j5 = emit_ffn(c, i_x3a, i_w1q1, i_w2q1, t["b1t1"], t["halfb21"], [o_x, i_x3],
                      w_ready=j4, x_ready=j2)
        emit_ln_to_dram(c, i_x3, o_xln)
        sched.emit(blk, ch)
    return nc


def build_launch5():
    nc = bass.Bass("TRN2", target_bir_lowering=False, debug=False)
    d_x = nc.dram_tensor("x3", [D, TPC], F32, kind="ExternalInput").ap()
    d_hout = nc.dram_tensor("hout2", [D, TPC], BF16, kind="ExternalInput").ap()
    d_wo = nc.dram_tensor("woT", [D, D], F32, kind="ExternalInput").ap()
    d_w1q1 = nc.dram_tensor("w1q1", [D, F_FF], BF16, kind="ExternalInput").ap()
    d_w2q1 = nc.dram_tensor("w2q1", [F_FF, D], BF16, kind="ExternalInput").ap()
    d_b1_1 = nc.dram_tensor("b1_1", [128, FC], F32, kind="ExternalInput").ap()
    d_b2_1 = nc.dram_tensor("b2_1", [128, DC], F32, kind="ExternalInput").ap()
    d_ones_bf = nc.dram_tensor("ones_bf", [128, 128], BF16, kind="ExternalInput").ap()
    d_ones_f32 = nc.dram_tensor("ones_f32", [128, 128], F32, kind="ExternalInput").ap()
    o_x = nc.dram_tensor("xout", [D, TPC], F32, kind="ExternalOutput").ap()
    i_x4 = nc.dram_tensor("i_x4", [D, TPC], F32, kind="Internal").ap()

    with ExitStack() as stack:
        sched = Sched(nc)
        c = Ctx(nc, sched, stack)
        alloc_common(c)
        alloc_ffn(c)
        c.sb("b1t1", [128, FC], F32)
        c.sb("halfb21", [128, DC], F32)
        ch = stack.enter_context(nc.semaphore("ch"))
        blk = stack.enter_context(nc.Block())
        s, t = c.s, c.t
        load_consts(c, d_ones_bf, d_ones_f32)
        s.dma("sync", lambda e: e.dma_start(out=t["b1t1"][:, :], in_=d_b1_1[:, :]))
        s.dma("sync", lambda e: e.dma_start(out=t["halfb21"][:, :], in_=d_b2_1[:, :]))
        s.op("scalar", lambda e: e.activation(t["halfb21"][:, :], t["halfb21"][:, :], AF.Copy, scale=0.5))
        j1 = emit_headproj_residual(c, d_hout, d_wo, d_x, [i_x4])
        emit_ffn(c, i_x4, d_w1q1, d_w2q1, t["b1t1"], t["halfb21"], [o_x], x_ready=j1)
        sched.emit(blk, ch)
    return nc


TWO_PI = 6.283185307179586
_hi64 = np.float64(TWO_PI)
_2PI_HI = np.float32(np.ldexp(np.round(np.ldexp(_hi64, 10)), -10))
_2PI_MID = np.float32(np.ldexp(np.round(np.ldexp(_hi64 - _2PI_HI, 22)), -22))
_2PI_LO = np.float32(_hi64 - np.float64(_2PI_HI) - np.float64(_2PI_MID))
RMS_EPS = 1.1920929e-07


def build_launch4():
    nc = bass.Bass("TRN2", target_bir_lowering=False, debug=False)
    d_xln = nc.dram_tensor("xln", [D, S_LEN], BF16, kind="ExternalInput").ap()
    d_wqkvT = nc.dram_tensor("wqkvT", [D, 1536], F32, kind="ExternalInput").ap()
    d_iota = nc.dram_tensor("iota", [1, 512], F32, kind="ExternalInput").ap()
    d_dhalf = nc.dram_tensor("dhalf", [128, 1], F32, kind="ExternalInput").ap()
    d_perm = nc.dram_tensor("permT", [128, 128], BF16, kind="ExternalInput").ap()
    d_ones_bf = nc.dram_tensor("ones_bf", [128, 128], BF16, kind="ExternalInput").ap()
    d_ones_f32 = nc.dram_tensor("ones_f32", [128, 128], F32, kind="ExternalInput").ap()
    d_masks = nc.dram_tensor("cmask", [128, 4 * 512], BF16, kind="ExternalInput").ap()
    o_hout = nc.dram_tensor("hout2", [512, S_LEN], BF16, kind="ExternalOutput").ap()

    with ExitStack() as stack:
        sched = Sched(nc)
        c = Ctx(nc, sched, stack)
        s = sched
        c.sb("Wq", [128, DC * 1536], BF16)
        c.sb("sin", [128, S_LEN], BF16)
        c.sb("cos", [128, S_LEN], BF16)
        c.sb("iota", [1, 512], F32)
        c.sb("tof", [128, 1], F32)
        c.sb("dhalf", [128, 1], F32)
        c.sb("invf", [128, 1], F32)
        c.sb("perm", [128, 128], BF16)
        c.sb("ones_bf", [128, 128], BF16)
        c.sb("ones_f32", [128, 128], F32)
        c.sb("cmask", [128, 4 * 512], BF16)
        c.sb("xs", [128, DC * 512], BF16)
        c.sb("xs2", [128, DC * 512], BF16)
        c.sb("qf", [128, S_LEN], BF16)
        c.sb("kf", [128, S_LEN], BF16)
        c.sb("vtm", [128, S_LEN], BF16)
        c.sb("ki", [128, 512], I32)
        c.sb("t1", [128, 512], F32)
        c.sb("t2", [128, 512], F32)
        c.sb("sqr", [128, 512], BF16)
        c.sb("rstd", [128, 512], F32)
        c.sb("pt", [128, 512], BF16)
        c.sb("pt2", [128, 512], BF16)
        c.sb("linv", [128, 512], F32)
        c.sb("hot", [128, 512], BF16)
        c.sb("hot2", [128, 512], BF16)
        c.t["ang"] = c.t["t1"]
        c.t["tha"] = c.t["t2"]
        c.t["thb"] = c.t["rstd"]
        c.t["kfl"] = c.t["linv"]
        c.ps("pa", [128, 512])
        c.ps("pe", [128, 512])
        c.ps("po", [128, 512])
        c.ps("pl", [128, 512])
        c.ps("pr", [128, 512])
        ch = stack.enter_context(nc.semaphore("ch"))
        blk = stack.enter_context(nc.Block())
        t = c.t

        for nm, src in [("iota", d_iota), ("dhalf", d_dhalf), ("perm", d_perm),
                        ("ones_bf", d_ones_bf), ("ones_f32", d_ones_f32), ("cmask", d_masks)]:
            s.dma("sync", (lambda nm, src: lambda e: e.dma_start(out=t[nm][:, :], in_=src[:, :]))(nm, src))
        s.dma("gpsimd", lambda e: e.dma_start(
            out=t["Wq"][:, :].rearrange("p (c f) -> p c f", c=DC),
            in_=d_wqkvT.rearrange("(c p) f -> p c f", p=128)))
        s.op("scalar", lambda e: e.activation(t["invf"][:, :], t["dhalf"][:, :], AF.Exp,
                                              scale=-float(np.log(10000.0)) / 64.0))
        for sl in range(16):
            t0 = sl * 512
            s.op("tensor", (lambda t0: lambda e: e.matmul(
                t["pa"][:, :], t["ones_f32"][0:1, :], t["iota"][0:1, 0:512],
                start=True, stop=True))(t0))
            s.op("vector", (lambda t0: lambda e: e.tensor_scalar(
                t["tof"][:, :], t["invf"][:, 0:1], float(t0), None, ALU.mult))(t0))
            s.op("vector", lambda e: e.tensor_scalar(t["ang"][:, :], t["pa"][:, :], t["invf"][:, 0:1],
                                                     t["tof"][:, 0:1], ALU.mult, ALU.add))
            for which, dst in ((0.0, "sin"), (np.pi / 2.0, "cos")):
                if which == 0.0:
                    s.op("vector", lambda e: e.tensor_copy(t["tha"][:, :], t["ang"][:, :]))
                else:
                    s.op("vector", (lambda which: lambda e: e.tensor_scalar(
                        t["tha"][:, :], t["ang"][:, :], which, None, ALU.add))(which))
                s.op("vector", lambda e: e.tensor_scalar(t["thb"][:, :], t["tha"][:, :],
                                                         1.0 / TWO_PI, None, ALU.mult))
                s.op("vector", lambda e: e.tensor_copy(t["ki"][:, :], t["thb"][:, :]))
                s.op("vector", lambda e: e.tensor_copy(t["kfl"][:, :], t["ki"][:, :]))
                s.op("vector", lambda e: e.scalar_tensor_tensor(
                    t["thb"][:, :], t["kfl"][:, :], -float(_2PI_HI), t["tha"][:, :], ALU.mult, ALU.add))
                s.op("vector", lambda e: e.scalar_tensor_tensor(
                    t["tha"][:, :], t["kfl"][:, :], -float(_2PI_MID), t["thb"][:, :], ALU.mult, ALU.add))
                s.op("vector", lambda e: e.scalar_tensor_tensor(
                    t["thb"][:, :], t["kfl"][:, :], -float(_2PI_LO), t["tha"][:, :], ALU.mult, ALU.add))
                s.op("scalar", (lambda dst, t0: lambda e: e.activation(
                    t[dst][:, t0:t0 + 512], t["thb"][:, :], AF.Sin))(dst, t0))

        eps_rms = c.cst(RMS_EPS)
        qb = c.cst(-0.5 * float(np.log(128.0)))
        zb = c.cst(0.0)
        exb = c.cst(-12.0)
        sXS = c.sem("xs")
        sEv = c.sem("ev")
        sExp = c.sem("xp")
        sHot = c.sem("hot")
        sSt = c.sem("st")
        nXS = nEv = nExp = nHot = nSt = 0
        xs_last = [None, None]
        for h in range(HPC):
            ev_base = nEv
            for sl in range(16):
                t0 = sl * 512
                xs = t["xs"] if sl % 2 == 0 else t["xs2"]
                aft = [] if xs_last[sl % 2] is None else [xs_last[sl % 2]]
                s.dma("sync", (lambda t0, xs: lambda e: e.dma_start(
                    out=xs[:, :].rearrange("p (c f) -> p c f", c=DC),
                    in_=d_xln.rearrange("(c p) t -> p c t", p=128)[:, :, t0:t0 + 512]))(t0, xs),
                    after=aft, track=(sXS, 16))
                nXS += 1
                for which, m in ((0, 0 * 4 + h), (1, 1 * 4 + h)):
                    dst = "qf" if which == 0 else "kf"
                    ps = t["pa"] if which == 0 else t["pe"]
                    for ci in range(DC):
                        w = [(sXS, 16 * nXS), (sEv, max(0, nEv - 2))] if ci == 0 else None
                        idx = s.op("tensor", (lambda m, ci, xs, ps: lambda e: e.matmul(
                            ps[:, :], t["Wq"][:, ci * 1536 + m * 128:ci * 1536 + (m + 1) * 128],
                            xs[:, ci * 512:(ci + 1) * 512],
                            start=(ci == 0), stop=(ci == DC - 1)))(m, ci, xs, ps),
                            waits=w)
                    s.op("scalar", (lambda dst, t0, ps: lambda e: e.activation(
                        t[dst][:, t0:t0 + 512], ps[:, :], AF.Copy))(dst, t0, ps),
                        after=[idx], track=(sEv, 1))
                    nEv += 1
                mv = 2 * 4 + h
                for tb in range(4):
                    for ci in range(DC):
                        w = [(sEv, max(0, nEv - 2))] if tb == 0 and ci == 0 else None
                        idx = s.op("tensor", (lambda mv, tb, ci, xs: lambda e: e.matmul(
                            t["po"][:, tb * 128:(tb + 1) * 128],
                            xs[:, ci * 512 + tb * 128:ci * 512 + (tb + 1) * 128],
                            t["Wq"][:, ci * 1536 + mv * 128:ci * 1536 + (mv + 1) * 128],
                            start=(ci == 0), stop=(ci == DC - 1)))(mv, tb, ci, xs),
                            waits=w)
                xs_last[sl % 2] = idx
                s.op("scalar", (lambda t0: lambda e: e.activation(
                    t["vtm"][:, t0:t0 + 512], t["po"][:, :], AF.Copy))(t0),
                    after=[idx], track=(sEv, 1))
                nEv += 1
            s.join("sync", [(sEv, nEv), (sXS, 16 * nXS)])
            for dst, bias in (("qf", qb), ("kf", zb)):
                for sl in range(16):
                    t0 = sl * 512
                    s.op("tensor", (lambda dst, t0: lambda e: e.matmul(
                        t["pr"][:, :], t["perm"][:, :], t[dst][:, t0:t0 + 512],
                        start=True, stop=True))(dst, t0))
                    s.op("vector", (lambda dst, t0: lambda e: e.tensor_mul(
                        t["t1"][:, :], t[dst][:, t0:t0 + 512], t["cos"][:, t0:t0 + 512]))(dst, t0))
                    s.op("vector", (lambda t0: lambda e: e.tensor_mul(
                        t["t2"][:, :], t["pr"][:, :], t["sin"][:, t0:t0 + 512]))(t0))
                    s.op("vector", lambda e: e.tensor_add(t["t1"][:, :], t["t1"][:, :], t["t2"][:, :]))
                    s.op("scalar", lambda e: e.activation(t["sqr"][:, :], t["t1"][:, :], AF.Square))
                    s.op("tensor", lambda e: e.matmul(t["pa"][:, :], t["ones_bf"][:, :], t["sqr"][:, :],
                                                      start=True, stop=True))
                    s.op("scalar", (lambda: lambda e: e.activation(
                        t["rstd"][:, :], t["pa"][:, :], AF.Ln, scale=1.0 / 128, bias=eps_rms))())
                    s.op("scalar", (lambda bias: lambda e: e.activation(
                        t["rstd"][:, :], t["rstd"][:, :], AF.Exp, scale=-0.5, bias=bias))(bias))
                    s.op("vector", (lambda dst, t0: lambda e: e.tensor_mul(
                        t[dst][:, t0:t0 + 512], t["t1"][:, :], t["rstd"][:, :]))(dst, t0))
            for qt in range(16):
                q0 = qt * 512
                nkb = (q0 + 512) // 128
                pv_idx = {}
                s_idx = {}
                for kb in range(nkb + 1):
                    if kb < nkb:
                        ps = t["pe"] if kb % 2 == 0 else t["pr"]
                        w = [(sExp, max(0, nExp - 3))]
                        if kb == 0:
                            w.append((sHot, nHot))
                        s_idx[kb] = s.op("tensor", (lambda kb, q0, ps: lambda e: e.matmul(
                            ps[:, :], t["kf"][:, kb * 128:(kb + 1) * 128], t["qf"][:, q0:q0 + 512],
                            start=True, stop=True, skip_group_check=True))(kb, q0, ps),
                            waits=w)
                        pt = t["pt"] if kb % 2 == 0 else t["pt2"]
                        aft = [s_idx[kb]]
                        if kb >= 2 and (kb - 2) in pv_idx:
                            aft.append(pv_idx[kb - 2])
                        doff = kb * 128 - q0
                        if doff > -128:
                            mi = doff // 128
                            s.op("scalar", (lambda kb, ps, pt: lambda e: e.activation(
                                pt[:, :], ps[:, :], AF.Exp, bias=exb))(kb, ps, pt),
                                after=aft, track=(sExp, 1))
                            nExp += 1
                            s.op("vector", (lambda mi, pt: lambda e: e.tensor_mul(
                                pt[:, :], pt[:, :], t["cmask"][:, mi * 512:(mi + 1) * 512]))(mi, pt),
                                after=[], waits=[(sExp, nExp)], track=(sExp, 1))
                            nExp += 1
                        else:
                            s.op("scalar", (lambda kb, ps, pt: lambda e: e.activation(
                                pt[:, :], ps[:, :], AF.Exp, bias=exb))(kb, ps, pt),
                                after=aft, track=(sExp, 2))
                            nExp += 2
                    if kb >= 1:
                        k2 = kb - 1
                        pt = t["pt"] if k2 % 2 == 0 else t["pt2"]
                        w = [(sExp, nExp if k2 == nkb - 1 else max(0, nExp - 2))]
                        pv_idx[k2] = s.op("tensor", (lambda k2, pt: lambda e: e.matmul(
                            t["po"][:, :], t["vtm"][:, k2 * 128:(k2 + 1) * 128], pt[:, :],
                            start=(k2 == 0), stop=(k2 == nkb - 1), skip_group_check=True))(k2, pt),
                            waits=w)
                        s.op("tensor", (lambda k2, pt: lambda e: e.matmul(
                            t["pl"][:, :], t["ones_bf"][:, :], pt[:, :],
                            start=(k2 == 0), stop=(k2 == nkb - 1), skip_group_check=True))(k2, pt))
                rc = s.op("vector", lambda e: e.reciprocal(t["linv"][:, :], t["pl"][:, :]))
                hot = t["hot"] if qt % 2 == 0 else t["hot2"]
                s.op("vector", (lambda hot: lambda e: e.tensor_mul(
                    hot[:, :], t["po"][:, :], t["linv"][:, :]))(hot),
                    after=[rc], waits=[(sSt, max(0, nSt - 16))], track=(sHot, 1))
                nHot += 1
                s.dma("gpsimd", (lambda h, q0, hot: lambda e: e.dma_start(
                    out=o_hout[h * 128:(h + 1) * 128, q0:q0 + 512], in_=hot[:, :]))(h, q0, hot),
                    after=[], waits=[(sHot, nHot)], track=(sSt, 16))
                nSt += 16
            s.join("sync", [(sHot, nHot), (sExp, nExp)])
        s.join("sync", [(sSt, nSt)])
        sched.emit(blk, ch)
    return nc


def _consts():
    return {
        "ones_bf": np.ones((128, 128), dtype=BF),
        "ones_f32": np.ones((128, 128), dtype=np.float32),
    }


def _consts2():
    one = np.ones((128, 128), np.float32)
    return {
        **_consts(),
        "ident_bf": np.eye(128, dtype=BF),
        "ident_f32": np.eye(128, dtype=np.float32),
        "triu": np.triu(one).astype(np.float32),
        "mski": np.triu(one).astype(BF),
        "mskn": (-np.triu(one, 1)).astype(BF),
        "mskn2": (-np.tril(one, -1)).astype(BF),
    }


def _core_heads(cid):
    return cid // 4, [4 * (cid % 4) + j for j in range(4)]


def _consts4():
    masks = np.zeros((128, 4 * 512), dtype=BF)
    for mi in range(4):
        kk = np.arange(128)[:, None] + mi * 128
        qq = np.arange(512)[None, :]
        masks[:, mi * 512:(mi + 1) * 512] = (kk <= qq).astype(BF)
    perm = np.zeros((128, 128), np.float32)
    for i in range(64):
        perm[2 * i + 1, 2 * i] = -1.0
        perm[2 * i, 2 * i + 1] = 1.0
    return {
        "cmask": masks,
        "permT": perm.astype(BF),
        "iota": np.arange(512, dtype=np.float32).reshape(1, 512),
        "dhalf": (np.arange(128) // 2).astype(np.float32).reshape(128, 1),
        **_consts(),
    }


def host_prep_l4(cid, xln_b, W_qkv):
    b, hs = _core_heads(cid)
    rows = ([W_qkv[h * 128:(h + 1) * 128] for h in hs]
            + [W_qkv[2048 + h * 128:2048 + (h + 1) * 128] for h in hs]
            + [W_qkv[4096 + h * 128:4096 + (h + 1) * 128] for h in hs])
    sel = np.concatenate(rows, axis=0)
    return {
        "xln": np.ascontiguousarray(xln_b),
        "wqkvT": np.ascontiguousarray(sel.T.astype(np.float32)),
        **_consts4(),
    }


def host_prep_l2(cid, xln_b, W_qkvz, W_ba, conv_w, dt_bias, A_log, norm_w):
    b, hs = _core_heads(cid)
    wmix = np.concatenate(
        [W_qkvz[:, h * 512 + off:h * 512 + off + 128] for off in (0, 128, 256) for h in hs],
        axis=1)
    wz = np.concatenate([W_qkvz[:, h * 512 + 384:h * 512 + 512] for h in hs], axis=1)
    wba = np.stack([W_ba[:, 2 * h] for h in hs] + [W_ba[:, 2 * h + 1] for h in hs], axis=1)
    rows = ([conv_w[h * 128:(h + 1) * 128] for h in hs]
            + [conv_w[2048 + h * 128:2048 + (h + 1) * 128] for h in hs]
            + [conv_w[4096 + h * 128:4096 + (h + 1) * 128] for h in hs])
    conv_sel = np.concatenate(rows, axis=0)
    convt = np.ascontiguousarray(conv_sel.reshape(12, 128, 4).transpose(1, 0, 2).reshape(128, 48))
    return {
        "xln": np.ascontiguousarray(xln_b),
        "wmix": np.ascontiguousarray(wmix.astype(np.float32)),
        "wz": np.ascontiguousarray(wz.astype(np.float32)),
        "wba": np.ascontiguousarray(wba.astype(np.float32)),
        "convw": convt.astype(np.float32),
        "dtb": dt_bias[hs].reshape(4, 1).astype(np.float32),
        "alog": A_log[hs].reshape(4, 1).astype(np.float32),
        "nw": norm_w.reshape(128, 1).astype(np.float32),
        **_consts2(),
    }


def _b_tile(b, nchunk):
    return np.ascontiguousarray(b.reshape(nchunk, 128).T.astype(np.float32))


def kernel(L0_attn_hidden_states, L0_attn_W_qkvz, L0_attn_W_ba, L0_attn_conv_weight,
           L0_attn_dt_bias, L0_attn_A_log, L0_attn_norm_weight, L0_attn_W_out,
           L0_ffn_w1, L0_ffn_b1, L0_ffn_w2, L0_ffn_b2,
           L1_attn_W_qkv, L1_attn_W_o, L1_ffn_w1, L1_ffn_b1, L1_ffn_w2, L1_ffn_b2):
    f32 = np.float32
    x0T = np.ascontiguousarray(np.asarray(L0_attn_hidden_states, f32).reshape(TOK, D).T)
    consts = _consts()

    nc1 = build_ffn_launch("absmean")
    w1T0 = np.ascontiguousarray(np.asarray(L0_ffn_w1, f32).T)
    w2T0 = np.ascontiguousarray(np.asarray(L0_ffn_w2, f32).T)
    b1_0 = _b_tile(np.asarray(L0_ffn_b1, f32), FC)
    b2_0 = _b_tile(np.asarray(L0_ffn_b2, f32), DC)
    im1 = [{"x0": np.ascontiguousarray(x0T[:, c * TPC:(c + 1) * TPC]),
            "w1t": w1T0, "w2t": w2T0, "b1": b1_0, "b2": b2_0, **consts}
           for c in range(NCORES)]
    r1 = run_bass_kernel_spmd(nc1, im1, core_ids=list(range(NCORES))).results
    x1 = [r1[c]["x1"] for c in range(NCORES)]
    xln1 = [r1[c]["xln1"] for c in range(NCORES)]
    w1q0, w2q0 = r1[0]["w1q"], r1[0]["w2q"]

    nc2 = build_launch2()
    xln1_b = {b: np.concatenate([xln1[4 * b + j] for j in range(4)], axis=1) for b in range(2)}
    im2 = [host_prep_l2(c, xln1_b[c // 4], np.asarray(L0_attn_W_qkvz, f32),
                        np.asarray(L0_attn_W_ba, f32), np.asarray(L0_attn_conv_weight, f32),
                        np.asarray(L0_attn_dt_bias, f32), np.asarray(L0_attn_A_log, f32),
                        np.asarray(L0_attn_norm_weight, f32))
           for c in range(NCORES)]
    r2 = run_bass_kernel_spmd(nc2, im2, core_ids=list(range(NCORES))).results
    hout1 = [r2[c]["hout"] for c in range(NCORES)]

    nc3 = build_launch3()
    w1T1 = np.ascontiguousarray(np.asarray(L1_ffn_w1, f32).T)
    w2T1 = np.ascontiguousarray(np.asarray(L1_ffn_w2, f32).T)
    b1_1 = _b_tile(np.asarray(L1_ffn_b1, f32), FC)
    b2_1 = _b_tile(np.asarray(L1_ffn_b2, f32), DC)
    im3 = []
    for c in range(NCORES):
        b = c // 4
        tok0 = (c % 4) * TPC
        hout_my = np.ascontiguousarray(np.concatenate(
            [hout1[4 * b + j][:, tok0:tok0 + TPC] for j in range(4)], axis=0))
        im3.append({"x1": x1[c], "hout": hout_my,
                    "wout": np.ascontiguousarray(np.asarray(L0_attn_W_out, f32)),
                    "w1q0": w1q0, "w2q0": w2q0, "b1_0": b1_0, "b2_0": b2_0,
                    "w1t_1": w1T1, "w2t_1": w2T1, "b1_1": b1_1, "b2_1": b2_1, **consts})
    r3 = run_bass_kernel_spmd(nc3, im3, core_ids=list(range(NCORES))).results
    x3 = [r3[c]["x3"] for c in range(NCORES)]
    xln3 = [r3[c]["xln3"] for c in range(NCORES)]
    w1q1, w2q1 = r3[0]["w1q1"], r3[0]["w2q1"]

    nc4 = build_launch4()
    xln3_b = {b: np.concatenate([xln3[4 * b + j] for j in range(4)], axis=1) for b in range(2)}
    im4 = [host_prep_l4(c, xln3_b[c // 4], np.asarray(L1_attn_W_qkv, f32)) for c in range(NCORES)]
    r4 = run_bass_kernel_spmd(nc4, im4, core_ids=list(range(NCORES))).results
    hout2 = [r4[c]["hout2"] for c in range(NCORES)]

    nc5 = build_launch5()
    woT = np.ascontiguousarray(np.asarray(L1_attn_W_o, f32).T)
    im5 = []
    for c in range(NCORES):
        b = c // 4
        tok0 = (c % 4) * TPC
        hout_my = np.ascontiguousarray(np.concatenate(
            [hout2[4 * b + j][:, tok0:tok0 + TPC] for j in range(4)], axis=0))
        im5.append({"x3": x3[c], "hout2": hout_my, "woT": woT,
                    "w1q1": w1q1, "w2q1": w2q1, "b1_1": b1_1, "b2_1": b2_1, **consts})
    r5 = run_bass_kernel_spmd(nc5, im5, core_ids=list(range(NCORES))).results
    out = np.concatenate([r5[c]["xout"] for c in range(NCORES)], axis=1)
    return np.ascontiguousarray(out.T).reshape(BATCH, S_LEN, D).astype(np.float32)
